# version 02 (final)
# baseline (speedup 1.0000x reference)
import numpy as np
import concourse.bass as bass
import concourse.mybir as mybir
from concourse.bass_utils import run_bass_kernel_spmd

F32 = mybir.dt.float32
BF16 = mybir.dt.bfloat16
I32 = mybir.dt.int32
AF = mybir.ActivationFunctionType
ALU = mybir.AluOpType
AX = mybir.AxisListType

D = 1024
T = 2048
NK = D // 128
TS = 512
NSUB = T // TS
N_SUB_BLOCKS = 12
DBG = False
L1CUT = 0

LN_EPS = 1e-5


class Prog:

    ENGS = ("sync", "scalar", "vector", "tensor", "gpsimd")

    def __init__(self):
        self.entries = []
        self.phval = 0
        self.phgval = 0
        self.bar = (0, 0)
        self.engwait = {e: (0, 0) for e in self.ENGS}
        self.dirty = {e: None for e in self.ENGS}

    def _wait_for(self, eng):
        w = self.engwait[eng]
        wa = self.bar[0] if w[0] < self.bar[0] else None
        wb = self.bar[1] if w[1] < self.bar[1] else None
        if wa is not None or wb is not None:
            self.engwait[eng] = (max(w[0], self.bar[0]),
                                 max(w[1], self.bar[1]))
        return (wa, wb)

    def op(self, eng, fn):
        e = {"eng": eng, "fn": fn, "wait": self._wait_for(eng), "inc": 0,
             "sem": "ph"}
        self.entries.append(e)
        self.dirty[eng] = e
        return e

    def dma(self, eng, out, in_):
        sem = "phg" if eng == "gpsimd" else "ph"
        e = {"eng": eng, "fn": (lambda q, o=out, i=in_: q.dma_start(out=o, in_=i)),
             "wait": self._wait_for(eng), "inc": 16, "sem": sem}
        self.entries.append(e)
        if sem == "ph":
            self.phval += 16
        else:
            self.phgval += 16
        return e

    def barrier(self):
        for eng in self.ENGS:
            d = self.dirty[eng]
            if d is not None:
                d["inc"] += 1
                self.phval += 1
                self.dirty[eng] = None
        self.bar = (self.phval, self.phgval)

    def replay(self, block, ph, phg):
        self.barrier()
        per = {e: [en for en in self.entries if en["eng"] == e]
               for e in self.ENGS}
        fa, fb = self.phval, self.phgval

        def run(q, eng):
            for en in per[eng]:
                wa, wb = en["wait"]
                if wa is not None:
                    q.wait_ge(ph, wa)
                if wb is not None:
                    q.wait_ge(phg, wb)
                inst = en["fn"](q)
                if en["inc"]:
                    inst.then_inc(ph if en["sem"] == "ph" else phg, en["inc"])
            if eng == "sync":
                q.wait_ge(ph, fa)
                if fb:
                    q.wait_ge(phg, fb)

        block.sync(lambda q: run(q, "sync"))
        block.scalar(lambda q: run(q, "scalar"))
        block.vector(lambda q: run(q, "vector"))
        block.tensor(lambda q: run(q, "tensor"))
        block.gpsimd(lambda q: run(q, "gpsimd"))


class Builder:
    def __init__(self, blocks=None):
        self.blocks = list(range(N_SUB_BLOCKS)) if blocks is None else list(blocks)
        nc = bass.Bass("TRN2", target_bir_lowering=False,
                       detect_race_conditions=False)
        self.nc = nc
        self.p = Prog()
        self._ctx = []
        self.inp = {}
        self.input_names = []
        self._declare()
        try:
            self._emit()
            sem_cm = nc.semaphore("ph")
            ph = sem_cm.__enter__(); self._ctx.append(sem_cm)
            semg_cm = nc.semaphore("phg")
            phg = semg_cm.__enter__(); self._ctx.append(semg_cm)
            blk_cm = nc.Block()
            blk = blk_cm.__enter__(); self._ctx.append(blk_cm)
            self.p.replay(blk, ph, phg)
        finally:
            for cm in reversed(self._ctx):
                cm.__exit__(None, None, None)

    def din(self, name, shape, dtype=F32):
        ap = self.nc.dram_tensor(name, shape, dtype, kind="ExternalInput").ap()
        self.inp[name] = ap
        self.input_names.append(name)
        return ap

    def internal(self, name, shape, dtype=F32, shared=False):
        kw = {"kind": "Internal"}
        if shared:
            kw["addr_space"] = "Shared"
        return self.nc.dram_tensor(name, shape, dtype, **kw).ap()

    def sb(self, name, shape, dtype):
        cm = self.nc.sbuf_tensor(name, shape, dtype)
        t = cm.__enter__(); self._ctx.append(cm)
        return t

    def psum_t(self, name, shape, dtype=F32):
        cm = self.nc.psum_tensor(name, shape, dtype)
        t = cm.__enter__(); self._ctx.append(cm)
        return t

    def _declare(self):
        d = self.din
        self.x_in = d("x_fm", [D, T])
        self.out = self.nc.dram_tensor("out_fm", [D, T], F32,
                                       kind="ExternalOutput").ap()
        self.dbg = self.nc.dram_tensor("dbg_fm", [128, 8192], F32,
                                       kind="ExternalOutput").ap()
        self.dbg_off = 0
        d("ones_bf", [128, 128], BF16)
        d("ones_row_bf", [1, 512], BF16)
        d("ident_bf", [128, 128], BF16)
        d("ident_f32", [128, 128], F32)
        d("rope_cos", [64, T])
        d("rope_sin", [64, T])
        d("triu_incl_bf", [128, 128], BF16)
        d("sel64_bf", [64, 2048], BF16)
        d("l0_kernel", [D, D]); d("l0_bias_row", [1, D])
        d("l0_a", [D, 16]); d("l0_b", [16, D])
        d("l0_wq", [D, D]); d("l0_wk", [D, D]); d("l0_wv", [D, D])
        d("l0_wo", [D, D]); d("l0_bo_row", [1, D])
        d("l0_wba", [D, 16]); d("l0_bba", [16, 1])
        d("l1_gate", [D, 8]); d("l1_gateb_row", [1, 8])
        d("l1_W1", [D, 2048]); d("l1_b1cols", [128, 16])
        d("l1_W2", [2048, D]); d("l1_b2", [8, D])
        d("comb2_all", [8, 1]); d("comb2_prec", [8, 1]); d("comb2_thrbase", [1, 1])
        d("l2_W1", [D, 4096]); d("l2_b1row", [1, 4096])
        d("l2_theta", [128, 32]); d("l2_wr", [128, 32]); d("l2_wi", [128, 32])
        d("l2_W2", [4096, D]); d("l2_b2row", [1, D])
        d("l3_wkt", [D, D]); d("l3_wvt", [D, D]); d("l3_wrt", [D, D])
        d("l3_wkc", [D, 4096]); d("l3_wvc", [4096, D]); d("l3_wrc", [D, D])
        d("l3_ln1g", [128, 8]); d("l3_ln1b", [128, 8])
        d("l3_ln2g", [128, 8]); d("l3_ln2b", [128, 8])
        d("l3_mixk", [128, 8]); d("l3_mixv", [128, 8]); d("l3_mixr", [128, 8])
        d("l3_cmixk", [128, 8]); d("l3_cmixr", [128, 8])
        d("l3_tf", [128, 8])
        d("l3f_wr", [D, D]); d("l3f_wk", [D, D]); d("l3f_wv", [D, D])
        d("l3f_mur", [128, 8]); d("l3f_muk", [128, 8])
        d("comb3_prev", [8, 1])
        d("l2a_lng", [128, 8]); d("l2a_lnb", [128, 8])
        d("l2a_Win", [D, 4096]); d("l2a_convw", [128, 64]); d("l2a_convb", [128, 16])
        d("l2a_Wx", [2048, 33]); d("l2a_wdt", [128, 16]); d("l2a_bdt", [128, 16])
        d("l2a_alog", [128, 256]); d("l2a_dpar", [128, 16])
        d("l2a_Wout", [2048, D])
        d("ones_f32", [1, 128])
        self.i_scalar = self.internal("i_scalar", [1, 16], F32)
        self.i_wq = self.internal("i_wq", [128, NK * D], BF16)
        self.i_k = self.internal("i_k", [128, 8 * T], BF16)
        self.i_v = self.internal("i_v", [128, 16 * 1024], BF16)
        self.i_o = self.internal("i_o", [128, 32768], BF16)
        self.i_small = self.internal("i_small", [8, 32], F32)
        self.i_lcs = self.internal("i_lcs", [8, 16], F32)
        self.i_z = self.internal("i_z", [128, 32768], BF16)
        self.i_bcB = self.internal("i_bcB", [16, 2048], BF16)
        self.i_bcC = self.internal("i_bcC", [16, 2048], BF16)
        self.i_bcD = self.internal("i_bcD", [1, 2048], F32)
        self.ag4_in = self.internal("ag4_in", [128, 256], F32)
        self.ag4_out = self.internal("ag4_out", [1024, 256], F32, shared=True)
        self.i_xc = self.internal("i_xc", [128, 32768], BF16)
        self.ag3_in = self.internal("ag3_in", [128, 8], F32)
        self.ag3_out = self.internal("ag3_out", [1024, 8], F32, shared=True)
        self.ag2_in = self.internal("ag2_in", [1, 16], F32)
        self.ag2_out = self.internal("ag2_out", [8, 16], F32, shared=True)
        self.i_xf32 = self.internal("i_xf32", [128, 16384], F32)
        self.i_qf32 = self.internal("i_qf32", [128, 16384], F32)
        self.i_kf32 = self.internal("i_kf32", [128, 16384], F32)
        self.i_bk = self.internal("i_bk", [128, 16384], F32)
        self.i_bf32 = self.internal("i_bf32", [16, 2048], F32)
        self.ag_in = self.internal("ag_in", [128, 1032], F32)
        self.ag_out = self.internal("ag_out", [8 * 128, 1032], F32, shared=True)
        d("l1a_wq", [D, D]); d("l1a_wk", [D, D]); d("l1a_wv", [D, D])
        d("l1a_wo", [D, D]); d("l1a_wb", [D, 16]); d("l1a_bbcol", [16, 1])
        d("l1a_cqw", [128, 32]); d("l1a_ckw", [128, 32])
        d("l1a_cqb", [128, 8]); d("l1a_ckb", [128, 8])
        d("l1a_lnw", [128, 8]); d("l1a_lnb", [128, 8])
        d("l1a_cons", [128, 514], BF16)
        d("l1a_consf", [128, 130], F32)
        d("comb_alpha", [8, 1])
        d("comb_gamma", [8, 1])
        d("comb_a", [8, 1])

        self.xs = [self.sb(f"xs{k}", [128, T], F32) for k in range(NK)]
        self.xb = [self.sb(f"xb{k}", [128, T], BF16) for k in range(NK)]
        self.gf = [self.sb(f"gf{k}", [128, T], F32) for k in range(3)]
        self.gb = [self.sb(f"gb{k}", [128, T], BF16) for k in range(6)]
        self.wst = self.sb("wst", [128, 512], F32)
        self.wbf = self.sb("wbf", [128, 16384], BF16)
        self.wbf2 = self.sb("wbf2", [128, 4096], BF16)
        self.resw = self.sb("resw", [128, 2560], BF16)
        self.resw2 = self.sb("resw2", [128, 1024], BF16)
        self.cons = self.sb("cons", [128, 896], BF16)
        self.consf = self.sb("consf", [128, 128], F32)
        self.onesf_row = self.sb("onesf_row", [1, 128], F32)
        self.sel64 = self.sb("sel64", [64, 2048], BF16)
        self.colf = self.sb("colf", [128, 460], F32)
        self.l1c = self.sb("l1c", [128, 514], BF16)
        self.l1cf = self.sb("l1cf", [128, 130], F32)
        self.bcolf = self.sb("bcolf", [128, 104], F32)
        self.iscr = self.sb("iscr", [128, 64], I32)
        self.ropecs = self.sb("ropecs", [128, 512], F32)
        self.ropecs2 = self.sb("ropecs2", [128, 512], F32)
        self.ps = [self.psum_t(f"ps{i}", [128, 512], F32) for i in range(6)]
        self.psb = self.psum_t("psb", [128, 1024], BF16)

    @property
    def ones_bf(self): return self.cons[:, 0:128]
    @property
    def ident_bf(self): return self.cons[:, 128:256]
    @property
    def triu_incl(self): return self.cons[:, 256:384]
    @property
    def onesrow(self): return self.cons[0:1, 384:896]

    def tt(self, eng, out, a, b, op):
        self.p.op(eng, lambda q: q.tensor_tensor(out, a, b, op))

    def ts2(self, eng, out, a, s1, s2, op0, op1):
        self.p.op(eng, lambda q: q.tensor_scalar(out, a, s1, s2, op0, op1))

    def tss(self, eng, out, a, s, op):
        self.p.op(eng, lambda q: q.tensor_single_scalar(out, a, s, op))

    def stt(self, eng, out, in0, s, in1, op0, op1):
        self.p.op(eng, lambda q: q.scalar_tensor_tensor(out, in0, s, in1,
                                                        op0, op1))

    def act(self, out, in_, func, bias=0.0, scale=1.0):
        self.p.op("scalar", lambda q: q.activation(out, in_, func,
                                                   bias=bias, scale=scale))

    def cp(self, eng, out, in_):
        if eng == "scalar":
            self.p.op(eng, lambda q: q.activation(out, in_, AF.Copy))
        else:
            self.p.op(eng, lambda q: q.tensor_copy(out, in_))

    def mm(self, out, lhsT, rhs, start, stop):
        self.p.op("tensor", lambda q: q.matmul(out, lhsT, rhs,
                                               start=start, stop=stop))

    def transpose(self, out_ps, in_, ident):
        self.p.op("tensor", lambda q: q.transpose(out_ps, in_, ident))

    def scan(self, eng, out, d0, d1, init, op0, op1):
        self.p.op(eng, lambda q: q.tensor_tensor_scan(out, d0, d1, init,
                                                      op0, op1))

    def recip(self, out, in_):
        self.p.op("vector", lambda q: q.reciprocal(out, in_))

    def reduce(self, eng, out, in_, op, absval=False):
        self.p.op(eng, lambda q: q.tensor_reduce(out, in_, AX.X, op,
                                                 apply_absolute_value=absval))

    def memset(self, eng, ap, v):
        self.p.op(eng, lambda q: q.memset(ap, v))

    def dbg_dump(self, ap, cols, cast=False, name=None):
        if cast:
            self.cp("vector", self.gf[2][0:ap.shape[0], 0:cols], ap)
            self.p.barrier()
            ap = self.gf[2][0:ap.shape[0], 0:cols]
        rows = ap.shape[0]
        dst = self.dbg[0:rows, self.dbg_off:self.dbg_off + cols]
        e = {"eng": "sync", "fn": (lambda q, o=dst, i=ap: q.dma_start(
            out=o, in_=i, allow_slow_non_contiguous=True)),
             "wait": self.p._wait_for("sync"), "inc": 16, "sem": "ph"}
        self.p.entries.append(e)
        self.p.phval += 16
        self.p.barrier()
        if not hasattr(self, "dbg_log"):
            self.dbg_log = []
        self.dbg_log.append((name, self.dbg_off, cols))
        self.dbg_off += cols

    def load_consts(self):
        p, i = self.p, self.inp
        p.dma("sync", self.cons[:, 0:128], i["ones_bf"])
        p.dma("sync", self.cons[:, 128:256], i["ident_bf"])
        p.dma("sync", self.cons[:, 256:384], i["triu_incl_bf"])
        p.dma("sync", self.cons[0:1, 384:896], i["ones_row_bf"])
        p.dma("sync", self.consf[:, 0:128], i["ident_f32"])
        p.dma("sync", self.onesf_row[0:1, :], i["ones_f32"])
        p.dma("sync", self.sel64[:], i["sel64_bf"])
        for k in range(NK):
            p.dma("sync", self.xs[k][:], self.x_in[k * 128:(k + 1) * 128, :])
        p.barrier()
        self.memset("vector", self.gf[0][:, 0:1032], 0.0)
        p.barrier()
        p.dma("sync", self.ag_in[:, :], self.gf[0][:, 0:1032])
        p.dma("sync", self.ag4_in[:, :], self.gf[0][:, 0:256])
        p.dma("sync", self.ag3_in[:, :], self.gf[0][:, 0:8])
        p.dma("sync", self.ag2_in[0:1, :], self.gf[0][0:1, 0:16])
        p.barrier()

    def load_col(self, dram_ap, dest_col):
        self.p.dma("sync", dest_col, dram_ap)

    def load_weight(self, name, din, dout, dest, dest_off=0):
        p = self.p
        w = self.inp[name] if isinstance(name, str) else name
        nchunk = din // 128
        for c in range(nchunk):
            npiece = (dout + 511) // 512
            for j in range(npiece):
                wdt = min(512, dout - j * 512)
                p.dma("sync", self.wst[:, 0:wdt],
                      w[c * 128:(c + 1) * 128, j * 512:j * 512 + wdt])
                p.barrier()
                o = dest_off + c * dout + j * 512
                eng = ("vector", "scalar", "gpsimd")[(c + j) % 3]
                self.cp(eng, dest[:, o:o + wdt], self.wst[:, 0:wdt])
                p.barrier()

    def load_rows_bf(self, dram_ap, rows, width, dest_ap_fn):
        p = self.p
        for j in range((width + 511) // 512):
            w = min(512, width - j * 512)
            p.dma("sync", self.wst[0:rows, 0:w],
                  dram_ap[:, j * 512:j * 512 + w])
            p.barrier()
            self.cp("vector", dest_ap_fn(j * 512, j * 512 + w),
                    self.wst[0:rows, 0:w])
            p.barrier()

    def emit_ln(self, src, out_bf):
        p = self.p
        mu_bf, rstd_bf = self.gb[4], self.gb[5]
        sq = [self.gb[2], self.gb[3]]
        for k in range(NK):
            eng = ("scalar", "vector", "gpsimd")[k % 3]
            self.cp(eng, out_bf[k][:], src[k][:])
        p.barrier()
        for k in range(NK):
            for s in range(NSUB):
                self.mm(self.ps[s][:, :], self.ones_bf,
                        out_bf[k][:, s * TS:(s + 1) * TS], k == 0, k == NK - 1)
        self.tt("vector", sq[0][:], out_bf[0][:], out_bf[0][:], ALU.mult)
        p.barrier()
        for s in range(NSUB):
            sl = slice(s * TS, (s + 1) * TS)
            self.act(self.gf[0][:, sl], self.ps[s][:, :], AF.Copy, scale=1.0 / D)
        p.barrier()
        for s in range(NSUB):
            sl = slice(s * TS, (s + 1) * TS)
            self.cp("scalar", mu_bf[:, sl], self.gf[0][:, sl])
        vbank = [self.ps[4], self.ps[5], self.ps[0], self.ps[1]]
        for k in range(NK):
            for s in range(NSUB):
                self.mm(vbank[s][:, :], self.ones_bf,
                        sq[k % 2][:, s * TS:(s + 1) * TS], k == 0, k == NK - 1)
            if k + 1 < NK:
                self.tt("vector", sq[(k + 1) % 2][:], out_bf[k + 1][:],
                        out_bf[k + 1][:], ALU.mult)
            p.barrier()
        for s in range(NSUB):
            sl = slice(s * TS, (s + 1) * TS)
            self.ts2("vector", self.gf[1][:, sl], vbank[s][:, :],
                     1.0 / D, LN_EPS, ALU.mult, ALU.add)
        self.stt("vector", self.gf[2][:], self.gf[0][:], -1.0, self.gf[0][:],
                 ALU.mult, ALU.mult)
        p.barrier()
        self.tt("vector", self.gf[1][:], self.gf[1][:], self.gf[2][:], ALU.add)
        p.barrier()
        self.recip(self.gf[1][:], self.gf[1][:])
        p.barrier()
        self.act(rstd_bf[:], self.gf[1][:], AF.Sqrt)
        p.barrier()
        for k in range(NK):
            eng = ("vector", "gpsimd")[k % 2]
            self.tt(eng, out_bf[k][:], out_bf[k][:], mu_bf[:], ALU.subtract)
        p.barrier()
        for k in range(NK):
            eng = ("vector", "gpsimd")[k % 2]
            self.tt(eng, out_bf[k][:], out_bf[k][:], rstd_bf[:], ALU.mult)
        p.barrier()

    def bcast_scalar_to_col(self, src_11, dest_col, idx=0):
        p = self.p
        p.dma("sync", self.i_scalar[0:1, idx:idx + 1], src_11)
        p.barrier()
        p.dma("sync", dest_col,
              self.i_scalar[0:1, idx:idx + 1].partition_broadcast(128))
        p.barrier()

    def quantize_l0(self):
        p = self.p
        w = self.inp["l0_kernel"]
        for c in range(NK):
            p.dma("sync", self.wst[:, 0:512], w[c * 128:(c + 1) * 128, 0:512])
            p.barrier()
            self.reduce("vector", self.colf[:, c:c + 1],
                        self.wst[:, 0:512], ALU.max, absval=True)
            p.barrier()
            p.dma("sync", self.wst[:, 0:512], w[c * 128:(c + 1) * 128, 512:D])
            p.barrier()
            self.reduce("vector", self.colf[:, 16:17],
                        self.wst[:, 0:512], ALU.max, absval=True)
            p.barrier()
            self.tt("vector", self.colf[:, c:c + 1], self.colf[:, c:c + 1],
                    self.colf[:, 16:17], ALU.max)
            p.barrier()
        self.reduce("vector", self.colf[:, 8:9], self.colf[:, 0:8], ALU.max)
        p.barrier()
        self.transpose(self.ps[0][0:1, 0:128], self.colf[:, 8:9], self.consf[:, 0:128])
        p.barrier()
        self.reduce("vector", self.colf[0:1, 9:10], self.ps[0][0:1, 0:128], ALU.max)
        p.barrier()
        self.tss("vector", self.colf[0:1, 10:11], self.colf[0:1, 9:10],
                 1.0 / 7.0, ALU.mult)
        self.recip(self.colf[0:1, 11:12], self.colf[0:1, 9:10])
        p.barrier()
        self.tss("vector", self.colf[0:1, 11:12], self.colf[0:1, 11:12],
                 7.0, ALU.mult)
        p.barrier()
        self.bcast_scalar_to_col(self.colf[0:1, 10:11], self.colf[:, 12:13], 0)
        self.bcast_scalar_to_col(self.colf[0:1, 11:12], self.colf[:, 13:14], 1)
        scale_col = self.colf[:, 12:13]
        inv_col = self.colf[:, 13:14]
        for c in range(NK):
          for half in range(2):
            p.dma("sync", self.wst[:, 0:512],
                  w[c * 128:(c + 1) * 128, half * 512:(half + 1) * 512])
            p.barrier()
            for hh2 in range(4):
                hh = half * 4 + hh2
                hsl = slice(hh2 * 128, (hh2 + 1) * 128)
                t = self.gf[0][:, 0:128]
                tf = self.gf[1][:, 0:128]
                df = self.gf[2][:, 0:128]
                self.stt("vector", t, self.wst[:, hsl], inv_col,
                         self.wst[:, hsl], ALU.mult, ALU.bypass)
                p.barrier()
                for _h2 in range(2):
                    _s2 = slice(_h2 * 64, (_h2 + 1) * 64)
                    self.cp("vector", self.iscr[:, 0:64], t[:, _s2])
                    p.barrier()
                    self.cp("vector", tf[:, _s2], self.iscr[:, 0:64])
                    p.barrier()
                self.tt("vector", df, t, tf, ALU.subtract)
                p.barrier()
                self.tss("vector", t, df, 0.5, ALU.is_gt)
                p.barrier()
                self.tt("vector", tf, tf, t, ALU.add)
                self.tss("vector", t, df, -0.5, ALU.is_lt)
                p.barrier()
                self.tt("vector", tf, tf, t, ALU.subtract)
                p.barrier()
                self.ts2("vector", tf, tf, -8.0, 7.0, ALU.max, ALU.min)
                p.barrier()
                self.stt("vector", self.wbf[:, c * D + hh * 128:c * D + (hh + 1) * 128],
                         tf, scale_col, tf, ALU.mult, ALU.bypass)
                p.barrier()
        p.dma("sync", self.i_wq[:, :], self.wbf[:, 0:NK * D])
        p.barrier()

    def load_l0_small(self):
        p = self.p
        self.load_weight("l0_a", D, 16, self.resw, 0)
        self.load_rows_bf(self.inp["l0_b"], 16, D,
                          lambda a, b: self.resw[0:16, 128 + a:128 + b])
        self.load_rows_bf(self.inp["l0_bias_row"], 1, D,
                          lambda a, b: self.resw[0:1, 1152 + a:1152 + b])

    def emit_ffn0(self, first):
        p = self.p
        self.emit_ln(self.xs, self.xb)
        if first:
            self.quantize_l0()
            self.load_l0_small()
        else:
            for piece in range(4):
                p.dma("sync", self.wbf[:, piece * 2048:(piece + 1) * 2048],
                      self.i_wq[:, piece * 2048:(piece + 1) * 2048])
            p.barrier()
        a_lhsT = self.resw[:, 0:128]
        b_lhsT = self.resw[0:16, 128:128 + D]
        bias_row = self.resw[0:1, 1152:1152 + D]
        for k in range(NK):
            for s in range(NSUB):
                self.mm(self.ps[s][0:16, :], a_lhsT[:, k * 16:(k + 1) * 16],
                        self.xb[k][:, s * TS:(s + 1) * TS], k == 0, k == NK - 1)
        p.barrier()
        rbf = self.gb[0]
        for s in range(NSUB):
            self.cp("scalar", rbf[0:16, s * TS:(s + 1) * TS], self.ps[s][0:16, :])
        p.barrier()
        for o in range(NK):
            for s in range(NSUB):
                sl = slice(s * TS, (s + 1) * TS)
                for k in range(NK):
                    self.mm(self.ps[s][:, :],
                            self.wbf[:, k * D + o * 128:k * D + (o + 1) * 128],
                            self.xb[k][:, sl], k == 0, False)
                self.mm(self.ps[s][:, :], b_lhsT[:, o * 128:(o + 1) * 128],
                        rbf[0:16, sl], False, False)
                self.mm(self.ps[s][:, :],
                        bias_row[0:1, o * 128:(o + 1) * 128],
                        self.onesrow, False, True)
            p.barrier()
            for s in range(NSUB):
                sl = slice(s * TS, (s + 1) * TS)
                self.stt("vector", self.xs[o][:, sl], self.ps[s][:, :], 0.5,
                         self.xs[o][:, sl], ALU.mult, ALU.add)
            p.barrier()



    def qt(self, h):
        if h < 6:
            return self.gb[h]
        return self.wbf2[:, (h - 6) * T:(h - 5) * T]

    @staticmethod
    def free_bcast(ap, n):
        return bass.AP(ap.tensor, ap.offset, list(ap.ap) + [[0, n]])

    def dma_bcast_small(self, sbuf_src, dest, r, c, src_dram=None):
        p = self.p
        if src_dram is None:
            if c == 1:
                p.dma("sync", self.i_small[0:1, 0:r], sbuf_src)
                p.barrier()
                p.dma("sync", dest,
                      self.i_small[0:1, 0:r].partition_broadcast(128))
                p.barrier()
                return
            p.dma("sync", self.i_small[0:r, 0:c], sbuf_src)
            p.barrier()
            src_dram = self.i_small[0:r, 0:c]
        dst = dest.rearrange("p (h c) -> p h c", h=r)
        p.dma("sync", dst, src_dram.partition_broadcast(128))
        p.barrier()

    def emit_l0_attn(self):
        p = self.p
        NH, C, NC = 8, 128, 16
        self.emit_ln(self.xs, self.xb)
        g0, g1, g2 = self.gf[0], self.gf[1], self.gf[2]
        self.load_weight("l0_wba", D, 16, self.resw, 1280)
        self.load_col(self.inp["l0_bba"][0:8, :], self.colf[0:8, 0:1])
        self.load_col(self.inp["l0_bba"][8:16, :], self.colf[0:8, 1:2])
        p.barrier()
        for part, coff, boff in ((0, 0, 0), (1, 8, 32)):
            for s in range(NSUB):
                for k in range(NK):
                    self.mm(self.ps[s][0:8, :],
                            self.resw[:, 1280 + k * 16 + coff:1280 + k * 16 + coff + 8],
                            self.xb[k][:, s * TS:(s + 1) * TS], k == 0, k == NK - 1)
            p.barrier()
            for s in range(NSUB):
                self.act(g0[boff:boff + 8, s * TS:(s + 1) * TS],
                         self.ps[s][0:8, :], AF.Sigmoid,
                         bias=self.colf[0:8, 0:1] if coff == 0
                         else self.colf[0:8, 1:2])
            p.barrier()
        self.ts2("vector", g0[0:8, :], g0[0:8, :], 0.149, 0.85, ALU.mult, ALU.add)
        p.barrier()
        if DBG:
            self.dbg_dump(g0[0:8, 0:256], 256)
            self.dbg_dump(g0[32:40, 0:256], 256)
        self.act(g1[0:8, :], g0[0:8, :], AF.Ln)
        self.memset("vector", g2[0:8, :], 0.0)
        p.barrier()
        self.scan("vector", g0[64:72, :], g1[0:8, :], g2[0:8, :], 0.0,
                  ALU.add, ALU.add)
        p.barrier()
        L3d = g0[64:72, :].rearrange("p (c t) -> p c t", t=128)
        self.memset("vector", self.colf[0:8, 16:17], 0.0)
        self.cp("vector",
                self.colf[0:8, 17:32].rearrange("p (a c) -> p a c", a=1),
                L3d[:, 0:15, 127:128].rearrange("p c t -> p t c"))
        self.cp("gpsimd",
                self.colf[0:8, 32:48].rearrange("p (a c) -> p a c", a=1),
                L3d[:, 0:16, 127:128].rearrange("p c t -> p t c"))
        p.barrier()
        self.tt("vector", self.colf[0:8, 48:64], self.colf[0:8, 32:48],
                self.colf[0:8, 16:32], ALU.subtract)
        p.barrier()
        self.act(self.colf[0:8, 48:64], self.colf[0:8, 48:64], AF.Exp)
        p.barrier()
        self.act(self.colf[0:8, 64:80], self.colf[0:8, 16:32], AF.Exp)
        p.barrier()
        self.cp("vector", self.colf[0:8, 80:96], self.colf[0:8, 48:64])
        self.cp("gpsimd", self.colf[0:8, 96:112], self.colf[0:8, 64:80])
        p.barrier()
        self.dma_bcast_small(self.colf[0:8, 80:112], self.colf[:, 128:384], 8, 32)
        for j in range(16):
            self.tss("vector", g2[64:72, j * 128:(j + 1) * 128],
                     g2[0:8, 0:128], self.colf[0:8, 16 + j:17 + j], ALU.add)
        p.barrier()
        self.tt("vector", g1[0:8, :], g0[64:72, :], g2[64:72, :], ALU.subtract)
        self.tt("gpsimd", g1[32:40, :], g2[64:72, :], g0[64:72, :], ALU.subtract)
        p.barrier()
        self.act(g1[0:8, :], g1[0:8, :], AF.Exp)
        p.barrier()
        self.act(g1[32:40, :], g1[32:40, :], AF.Exp)
        p.barrier()
        self.tt("vector", g1[32:40, :], g1[32:40, :], g0[32:40, :], ALU.mult)
        p.barrier()
        if DBG:
            self.dbg_dump(g0[64:72, 0:256], 256)
            self.dbg_dump(g2[64:72, 0:256], 256)
            self.dbg_dump(g1[0:8, 0:256], 256)
            self.dbg_dump(g1[32:40, 0:256], 256)
        self.memset("vector", self.resw[0:64, 512:2560], 0.0)
        p.barrier()
        self.cp("vector", self.resw[0:8, 512:2560], g1[0:8, :])
        self.cp("gpsimd", self.resw[32:40, 512:2560], g1[32:40, :])
        p.barrier()

        self.load_weight("l0_wv", D, D, self.wbf, 0)
        for j in range(NC):
            jc = slice(j * C, (j + 1) * C)
            for half in range(2):
                for k in range(NK):
                    self.mm(self.ps[half][:, :], self.xb[k][:, jc],
                            self.wbf[:, k * D + half * 512:k * D + (half + 1) * 512],
                            k == 0, k == NK - 1)
            p.barrier()
            for half in range(2):
                self.cp("scalar", self.gb[0][:, half * 512:(half + 1) * 512],
                        self.ps[half][:, :])
            p.barrier()
            p.dma("sync", self.i_v[:, j * 1024:(j + 1) * 1024],
                  self.gb[0][:, 0:1024])
            p.barrier()

        for which in ("k", "q"):
            self.load_weight("l0_w" + which, D, D, self.wbf, 0)
            rowbase = 8 if which == "k" else 0
            for s in range(NSUB):
                sl = slice(s * TS, (s + 1) * TS)
                p.dma("sync", self.ropecs[0:64, :],
                      self.inp["rope_cos"][:, sl])
                p.dma("sync", self.ropecs[64:128, :],
                      self.inp["rope_sin"][:, sl])
                p.dma("sync", self.ropecs2[64:128, :],
                      self.inp["rope_cos"][:, sl])
                p.dma("sync", self.ropecs2[0:64, :],
                      self.inp["rope_sin"][:, sl])
                p.barrier()
                for h in range(NH):
                    for k in range(NK):
                        self.mm(self.ps[0][:, :],
                                self.wbf[:, k * D + h * 128:k * D + (h + 1) * 128],
                                self.xb[k][:, sl], k == 0, k == NK - 1)
                    self.mm(self.ps[4][:, :],
                            self.sel64[:, (rowbase + h) * 128:(rowbase + h + 1) * 128],
                            self.resw[0:64, 512 + s * TS:512 + (s + 1) * TS],
                            True, True)
                    p.barrier()
                    self.act(g1[:, 0:512], self.ps[0][:, :], AF.Copy)
                    p.barrier()
                    self.tt("vector", g2[0:64, 0:512], g1[0:64, 0:512],
                            self.ropecs[0:64, :], ALU.mult)
                    self.tt("gpsimd", g2[0:64, 512:1024], g1[64:128, 0:512],
                            self.ropecs[64:128, :], ALU.mult)
                    self.tt("vector", g2[64:128, 0:512], g1[64:128, 0:512],
                            self.ropecs2[64:128, :], ALU.mult)
                    self.tt("gpsimd", g2[64:128, 512:1024], g1[0:64, 0:512],
                            self.ropecs2[0:64, :], ALU.mult)
                    p.barrier()
                    self.tt("vector", g1[0:64, 0:512], g2[0:64, 0:512],
                            g2[0:64, 512:1024], ALU.subtract)
                    self.tt("gpsimd", g1[64:128, 0:512], g2[64:128, 0:512],
                            g2[64:128, 512:1024], ALU.add)
                    p.barrier()
                    dest = (self.gb[1] if which == "k" else self.qt(h))
                    self.tt("vector", dest[:, sl], g1[:, 0:512], self.ps[4][:, :],
                            ALU.mult)
                    p.barrier()
                    if which == "k":
                        p.dma("sync",
                              self.i_k[:, h * T + s * TS:h * T + (s + 1) * TS],
                              self.gb[1][:, sl])
            p.barrier()

        if DBG:
            self.dbg_dump(self.qt(0)[:, 0:256], 256, cast=True)
            self.dbg_dump(self.gb[1][:, 0:256], 256, cast=True)
        self.load_weight("l0_wo", D, D, self.wbf, 8192)
        self.load_rows_bf(self.inp["l0_bo_row"], 1, D,
                          lambda a, b: self.resw[0:1, a:b])

        S = g2
        S_bf = self.wbf[:, 0:1024]
        self.memset("vector", S[:, 0:1024], 0.0)
        self.memset("gpsimd", S_bf, 0.0)
        p.barrier()
        ik3 = self.i_k.rearrange("p (h t) -> p h t", h=NH)
        for j in range(NC):
            jc = slice(j * C, (j + 1) * C)
            p.dma("sync", self.xb[0][:, 0:1024].rearrange("p (h c) -> p h c", h=NH),
                  ik3[:, :, j * C:(j + 1) * C])
            p.dma("gpsimd", self.xb[1][:, 0:1024], self.i_v[:, j * 1024:(j + 1) * 1024])
            p.barrier()
            for h in range(NH):
                self.mm(self.ps[h // 4][:, (h % 4) * 128:(h % 4 + 1) * 128],
                        self.xb[0][:, h * 128:(h + 1) * 128], self.qt(h)[:, jc],
                        True, True)
                self.transpose(self.psb[:, h * 128:(h + 1) * 128],
                               self.xb[0][:, h * 128:(h + 1) * 128], self.ident_bf)
            p.barrier()
            for h in range(NH):
                self.tt("vector", self.xb[2][:, h * 128:(h + 1) * 128],
                        self.ps[h // 4][:, (h % 4) * 128:(h % 4 + 1) * 128],
                        self.triu_incl, ALU.mult)
                self.cp("scalar", self.xb[3][:, h * 128:(h + 1) * 128],
                        self.psb[:, h * 128:(h + 1) * 128])
            p.barrier()
            for h in range(NH):
                po = self.ps[4 + h // 4][:, (h % 4) * 128:(h % 4 + 1) * 128]
                self.mm(po, self.xb[2][:, h * 128:(h + 1) * 128],
                        self.xb[1][:, h * 128:(h + 1) * 128], True, False)
                self.mm(po, self.qt(h)[:, jc], S_bf[:, h * 128:(h + 1) * 128],
                        False, True)
                self.mm(self.ps[2 + h // 4][:, (h % 4) * 128:(h % 4 + 1) * 128],
                        self.xb[3][:, h * 128:(h + 1) * 128],
                        self.xb[1][:, h * 128:(h + 1) * 128], True, True)
            p.barrier()
            for half in range(2):
                self.cp("scalar", self.xb[1][:, half * 512:(half + 1) * 512],
                        self.ps[4 + half][:, :])
            for h in range(NH):
                hs = slice(h * 128, (h + 1) * 128)
                self.tt("vector", S[:, hs], S[:, hs],
                        self.ps[2 + h // 4][:, (h % 4) * 128:(h % 4 + 1) * 128],
                        ALU.add)
            p.barrier()
            for h in range(NH):
                hs = slice(h * 128, (h + 1) * 128)
                self.tss("vector", S[:, hs], S[:, hs],
                         self.colf[:, 128 + h * 32 + j:129 + h * 32 + j], ALU.mult)
            p.barrier()
            self.cp("scalar", S_bf[:, 0:512], S[:, 0:512])
            self.cp("scalar", S_bf[:, 512:1024], S[:, 512:1024])
            p.dma("sync", self.i_o[:, j * 1024:(j + 1) * 1024], self.xb[1][:, 0:1024])
            p.barrier()
            if DBG and j == 0:
                self.dbg_dump(self.xb[1][:, 0:256], 256, cast=True)
                self.dbg_dump(S[:, 0:256], 256)

        if DBG:
            self.dbg_dump(S[:, 0:256], 256)
        self.act(self.colf[0:8, 112:113], g0[64:72, 2047:2048], AF.Exp)
        p.barrier()
        p.dma("sync", self.ag_in[:, 0:1024], S[:, 0:1024])
        p.dma("gpsimd", self.ag_in[0:1, 1024:1032], self.colf[0:8, 112:113])
        p.barrier()
        self.p.op("gpsimd", lambda q: q.collective_compute(
            "AllGather", ALU.bypass,
            replica_groups=[[0, 1, 2, 3, 4, 5, 6, 7]],
            ins=[self.ag_in], outs=[self.ag_out]))
        p.barrier()
        ago = self.ag_out.rearrange("(r p) c -> r p c", p=128)
        p.dma("sync", self.colf[0:8, 48:56].rearrange("p (a c) -> p a c", a=1),
              ago[:, 0:1, 1024:1032])
        self.load_col(self.inp["comb_alpha"], self.colf[0:8, 113:114])
        self.load_col(self.inp["comb_gamma"], self.colf[0:8, 114:115])
        self.load_col(self.inp["comb_a"], self.colf[0:8, 115:116])
        p.barrier()
        self.ts2("vector", self.colf[0:8, 56:64], self.colf[0:8, 48:56],
                 self.colf[0:8, 113:114], self.colf[0:8, 114:115],
                 ALU.mult, ALU.add)
        p.barrier()
        self.cp("vector", self.colf[0:8, 64:65], self.colf[0:8, 115:116])
        p.barrier()
        self.cp("vector", self.colf[0:8, 72:80], self.colf[0:8, 56:64])
        self.cp("gpsimd", self.colf[0:8, 80:81], self.colf[0:8, 64:65])
        p.barrier()
        self.cp("vector", self.colf[0:8, 88:96], self.colf[0:8, 72:80])
        self.cp("gpsimd", self.colf[0:8, 96:97], self.colf[0:8, 80:81])
        p.barrier()
        raise_ = False
        self.dma_bcast_small(self.colf[0:8, 88:97], self.colf[:, 384:456], 8, 9)
        Sin = g1
        self.memset("vector", Sin[:, 0:1024], 0.0)
        p.barrier()
        if DBG:
            self.dbg_dump(self.colf[0:8, 48:56], 8)
            self.dbg_dump(self.colf[0:8, 88:97], 9)
        for r in range(8):
            p.dma("sync", g0[:, 0:1024], ago[r, :, 0:1024])
            p.barrier()
            if DBG and r == 0:
                self.dbg_dump(g0[:, 0:256], 256)
            for h in range(NH):
                hs = slice(h * 128, (h + 1) * 128)
                self.tss("vector", Sin[:, hs], Sin[:, hs],
                         self.colf[:, 384 + r * 9 + h:385 + r * 9 + h], ALU.mult)
            p.barrier()
            for h in range(NH):
                hs = slice(h * 128, (h + 1) * 128)
                self.stt("vector", Sin[:, hs], g0[:, hs],
                         self.colf[:, 384 + r * 9 + 8:385 + r * 9 + 8],
                         Sin[:, hs], ALU.mult, ALU.add)
            p.barrier()

        if DBG:
            self.dbg_dump(Sin[:, 0:256], 256)
            self.dbg_dump(self.colf[:, 384:456], 72)
            self.dbg_dump(self.xs[0][:, 0:256], 256)
        scr_bf = self.wbf[:, 1024:1152]
        for s in range(NSUB):
            ssl = slice(s * TS, (s + 1) * TS)
            for jj in range(4):
                j = s * 4 + jj
                jc = slice(j * C, (j + 1) * C)
                p.dma("sync", self.xb[1][:, 0:1024],
                      self.i_o[:, j * 1024:(j + 1) * 1024])
                for h in range(NH):
                    hs = slice(h * 128, (h + 1) * 128)
                    self.tss("vector", g0[:, 0:128], Sin[:, hs],
                             self.colf[:, 128 + h * 32 + 16 + j:
                                        129 + h * 32 + 16 + j], ALU.mult)
                    p.barrier()
                    self.cp("vector", scr_bf, g0[:, 0:128])
                    p.barrier()
                    self.mm(self.ps[4 + h // 4][:, (h % 4) * 128:(h % 4 + 1) * 128],
                            self.qt(h)[:, jc], scr_bf, True, True)
                    p.barrier()
                if DBG and s == 0 and jj == 0:
                    self.dbg_dump(self.ps[4][:, 0:256], 256, cast=True)
                    self.dbg_dump(self.xb[1][:, 0:256], 256, cast=True)
                for half in range(2):
                    self.tt("vector", self.xb[2][:, half * 512:(half + 1) * 512],
                            self.ps[4 + half][:, :],
                            self.xb[1][:, half * 512:(half + 1) * 512], ALU.add)
                p.barrier()
                for h in range(NH):
                    self.transpose(self.psb[:, h * 128:(h + 1) * 128],
                                   self.xb[2][:, h * 128:(h + 1) * 128],
                                   self.ident_bf)
                p.barrier()
                for h in range(NH):
                    dst = self.xb[4 + h // 4]
                    self.cp("scalar",
                            dst[:, (h % 4) * 512 + jj * 128:
                                (h % 4) * 512 + (jj + 1) * 128],
                            self.psb[:, h * 128:(h + 1) * 128])
                p.barrier()
                if DBG and s == 0 and jj == 0:
                    self.dbg_dump(self.xb[2][:, 0:256], 256, cast=True)
                    self.dbg_dump(self.xb[4][:, 0:256], 256, cast=True)
            for o in range(NK):
                for k in range(NK):
                    self.mm(self.ps[0][:, :],
                            self.wbf[:, 8192 + k * D + o * 128:
                                      8192 + k * D + (o + 1) * 128],
                            self.xb[4 + k // 4][:, (k % 4) * 512:(k % 4 + 1) * 512],
                            k == 0, False)
                self.mm(self.ps[0][:, :], self.resw[0:1, o * 128:(o + 1) * 128],
                        self.onesrow, False, True)
                p.barrier()
                if DBG and s == 0 and o == 0:
                    self.dbg_dump(self.ps[0][:, 0:256], 256, cast=True)
                self.tt("vector", self.xs[o][:, ssl], self.ps[0][:, :],
                        self.xs[o][:, ssl], ALU.add)
                p.barrier()

    def dbg_xs(self):
        if DBG:
            self.dbg_dump(self.xs[0][:, 0:256], 256)
            self.dbg_dump(self.xs[7][:, 1792:2048], 256)

    def hslot(self, i):
        if i < 6:
            return self.gb[i]
        return self.wbf2[:, (i - 6) * T:(i - 5) * T]

    def emit_moe(self):
        p = self.p
        self.emit_ln(self.xs, self.xb)
        g0, g1, g2 = self.gf[0], self.gf[1], self.gf[2]
        self.load_weight("l1_gate", D, 8, self.resw, 1280)
        p.dma("sync", self.wst[0:1, 0:8], self.inp["l1_gateb_row"])
        p.barrier()
        self.cp("vector", self.resw[0:1, 1344:1352], self.wst[0:1, 0:8])
        p.barrier()
        for s in range(NSUB):
            for k in range(NK):
                self.mm(self.ps[s][0:8, :],
                        self.resw[:, 1280 + k * 8:1280 + (k + 1) * 8],
                        self.xb[k][:, s * TS:(s + 1) * TS], k == 0, False)
            self.mm(self.ps[s][0:8, :], self.resw[0:1, 1344:1352],
                    self.onesrow, False, True)
        p.barrier()
        for s in range(NSUB):
            self.act(g0[0:8, s * TS:(s + 1) * TS], self.ps[s][0:8, :], AF.Copy)
        p.barrier()
        if DBG:
            self.dbg_dump(g0[0:8, 0:512], 512, name="logits")
        for j in range(16):
            self.mm(self.ps[j % 2][0:128, 0:8],
                    g0[0:8, j * 128:(j + 1) * 128], self.consf[0:8, 0:8],
                    True, True)
            if j % 2 == 1 or j == 15:
                p.barrier()
                for jj in (j - 1, j) if j % 2 == 1 else (j,):
                    self.cp("vector", g1[:, jj * 8:(jj + 1) * 8],
                            self.ps[jj % 2][0:128, 0:8])
                p.barrier()
        for j in range(16):
            self.reduce("vector", g1[:, 128 + j:129 + j],
                        g1[:, j * 8:(j + 1) * 8], ALU.max)
        self.memset("gpsimd", self.colf[0:128, 452:460], -1e30)
        p.barrier()
        for j in range(16):
            jc = slice(j * 8, (j + 1) * 8)
            self.tss("vector", g2[0:128, 8:16], g1[:, jc],
                     g1[:, 128 + j:129 + j], ALU.is_ge)
            p.barrier()
            self.cp("vector", self.iscr[:, 0:8], g2[0:128, 8:16])
            p.barrier()
            self.p.op("vector", lambda q, jc=jc, j=j: q.select(
                g1[:, 144 + j * 8:152 + j * 8],
                self.iscr[:, 0:8], self.colf[0:128, 452:460], g1[:, jc]))
            p.barrier()
        for j in range(16):
            self.reduce("vector", g1[:, 272 + j:273 + j],
                        g1[:, 144 + j * 8:152 + j * 8], ALU.max)
        p.barrier()
        for j in range(16):
            jc = slice(j * 8, (j + 1) * 8)
            self.tss("vector", g1[:, 288 + j * 8:296 + j * 8], g1[:, jc],
                     g1[:, 272 + j:273 + j], ALU.is_ge)
        p.barrier()
        if DBG:
            self.dbg_dump(g1[:, 0:8], 8, name="lt0")
            self.dbg_dump(g1[:, 128:129], 1, name="m1c0")
            self.dbg_dump(g1[:, 144:152], 8, name="l2c0")
            self.dbg_dump(g1[:, 272:273], 1, name="m2c0")
            self.dbg_dump(g1[:, 288:296], 8, name="maskc0")
        for j in range(16):
            self.mm(self.ps[j % 2][0:8, 0:128],
                    g1[:, 288 + j * 8:296 + j * 8], self.consf[:, 0:128],
                    True, True)
            p.barrier()
            self.cp("vector", g0[0:8, j * 128:(j + 1) * 128],
                    self.ps[j % 2][0:8, 0:128])
            p.barrier()
        self.reduce("vector", self.colf[0:8, 2:3], g0[0:8, :], ALU.add)
        self.ts2("gpsimd", g1[0:8, :], g0[0:8, :], -1.0, 1.0, ALU.mult, ALU.add)
        p.barrier()
        self.memset("vector", g2[0:8, 0:1], 0.0)
        self.cp("gpsimd", g2[0:8, 1:T], g1[0:8, 0:T - 1])
        p.barrier()
        self.scan("vector", g1[0:8, :], g2[0:8, :], g2[0:8, :], 0.0,
                  ALU.add, ALU.bypass)
        p.barrier()
        p.dma("sync", self.ag2_in[0:1, 0:8], self.colf[0:8, 2:3])
        p.barrier()
        self.p.op("gpsimd", lambda q: q.collective_compute(
            "AllGather", ALU.bypass, replica_groups=[[0, 1, 2, 3, 4, 5, 6, 7]],
            ins=[self.ag2_in], outs=[self.ag2_out]))
        p.barrier()
        p.dma("sync", self.colf[0:8, 8:24], self.ag2_out)
        self.load_col(self.inp["comb2_all"], self.colf[0:8, 4:5])
        self.load_col(self.inp["comb2_prec"], self.colf[0:8, 5:6])
        p.barrier()
        self.mm(self.ps[0][0:1, 0:8], self.colf[0:8, 4:5], self.colf[0:8, 8:16],
                True, True)
        self.mm(self.ps[1][0:1, 0:8], self.colf[0:8, 5:6], self.colf[0:8, 8:16],
                True, True)
        p.barrier()
        self.load_col(self.inp["comb2_thrbase"], self.colf[0:1, 7:8])
        p.barrier()
        self.cp("vector", g2[0:1, 8:16], self.ps[0][0:1, 0:8])
        p.barrier()
        self.tt("vector", g2[0:1, 0:8], self.ps[1][0:1, 0:8],
                g2[0:1, 8:16], ALU.subtract)
        p.barrier()
        self.tss("vector", g2[0:1, 0:8], g2[0:1, 0:8], self.colf[0:1, 7:8],
                 ALU.add)
        p.barrier()
        p.dma("sync", self.i_small[0:1, 0:8], g2[0:1, 0:8])
        p.barrier()
        p.dma("sync", self.colf[0:8, 6:7], self.i_small[0:1, 0:8])
        p.barrier()
        self.tss("vector", g2[0:8, :], g1[0:8, :], self.colf[0:8, 6:7],
                 ALU.is_lt)
        p.barrier()
        self.tt("vector", g1[0:8, :], g2[0:8, :], g0[0:8, :], ALU.mult)
        p.barrier()
        self.tt("vector", g2[0:8, :], g2[0:8, :], g1[0:8, :], ALU.subtract)
        p.barrier()
        self.tt("vector", g0[0:8, :], g0[0:8, :], g2[0:8, :], ALU.max)
        p.barrier()
        if DBG:
            self.dbg_dump(g1[:, 0:128], 128, name="lt")
            self.dbg_dump(g0[0:8, 0:2048], 2048, name="sel")
            self.dbg_dump(g1[0:8, 0:2048], 2048, name="zexcl_after")
            self.dbg_dump(self.colf[0:8, 6:7], 1, name="thr")
            self.dbg_dump(self.colf[0:8, 2:3], 1, name="npos_loc")
        self.memset("vector", self.resw[0:64, 512:2560], 0.0)
        p.barrier()
        self.cp("vector", self.resw[0:8, 512:2560], g0[0:8, :])
        p.barrier()
        self.load_rows_bf(self.inp["l1_b2"], 8, D,
                          lambda a, b: self.resw2[0:8, a:b])
        self.load_col(self.inp["l1_b1cols"], self.bcolf[:, 0:16])
        p.barrier()
        for half in range(2):
            self.load_weight(self.inp["l1_W1"][:, half * 1024:(half + 1) * 1024],
                             D, 1024, self.wbf, 0)
            self.load_weight(self.inp["l1_W2"][half * 1024:(half + 1) * 1024, :],
                             1024, 1024, self.wbf, 8192)
            for ho in range(8):
                hg = half * 8 + ho
                e = hg // 2
                for s in range(NSUB):
                    sl = slice(s * TS, (s + 1) * TS)
                    for k in range(NK):
                        self.mm(self.ps[s % 4][:, :],
                                self.wbf[:, k * 1024 + ho * 128:
                                          k * 1024 + (ho + 1) * 128],
                                self.xb[k][:, sl], k == 0, k == NK - 1)
                    self.mm(self.ps[4][:, :],
                            self.sel64[:, e * 128:(e + 1) * 128],
                            self.resw[0:64, 512 + s * TS:512 + (s + 1) * TS],
                            True, True)
                    p.barrier()
                    self.act(g1[:, 0:512], self.ps[s % 4][:, :], AF.Gelu,
                             bias=self.bcolf[:, hg:hg + 1])
                    p.barrier()
                    self.tt("vector", self.hslot(ho)[:, sl], g1[:, 0:512],
                            self.ps[4][:, :], ALU.mult)
                    p.barrier()
            if DBG and half == 0:
                self.dbg_dump(self.hslot(0)[:, 0:512], 512, cast=True, name="h0")
                self.dbg_dump(self.hslot(3)[:, 0:512], 512, cast=True, name="h3")
            for o in range(NK):
                for s in range(NSUB):
                    sl = slice(s * TS, (s + 1) * TS)
                    for hc in range(8):
                        self.mm(self.ps[s % 4][:, :],
                                self.wbf[:, 8192 + hc * 1024 + o * 128:
                                          8192 + hc * 1024 + (o + 1) * 128],
                                self.hslot(hc)[:, sl], hc == 0,
                                half == 0 and hc == 7)
                    if half == 1:
                        self.mm(self.ps[s % 4][:, :],
                                self.resw2[0:8, o * 128:(o + 1) * 128],
                                self.resw[0:8, 512 + s * TS:512 + (s + 1) * TS],
                                False, True)
                p.barrier()
                for s in range(NSUB):
                    sl = slice(s * TS, (s + 1) * TS)
                    self.stt("vector", self.xs[o][:, sl], self.ps[s % 4][:, :],
                             0.5, self.xs[o][:, sl], ALU.mult, ALU.add)
                p.barrier()

    def emit_ffn2(self):
        p = self.p
        self.emit_ln(self.xs, self.xb)
        g1, g2 = self.gf[1], self.gf[2]
        self.load_col(self.inp["l2_theta"], self.colf[:, 128:160])
        self.load_col(self.inp["l2_wr"], self.colf[:, 160:192])
        self.load_col(self.inp["l2_wi"], self.colf[:, 192:224])
        p.barrier()
        TWO_PI = 6.2831853071795865
        self.tss("vector", self.colf[:, 224:256], self.colf[:, 128:160],
                 1.0 / TWO_PI, ALU.mult)
        p.barrier()
        self.cp("vector", self.iscr[:, 0:32], self.colf[:, 224:256])
        p.barrier()
        self.cp("vector", self.colf[:, 256:288], self.iscr[:, 0:32])
        p.barrier()
        self.tt("vector", self.colf[:, 288:320], self.colf[:, 224:256],
                self.colf[:, 256:288], ALU.subtract)
        p.barrier()
        self.tss("vector", self.colf[:, 320:352], self.colf[:, 288:320], 0.5,
                 ALU.is_gt)
        p.barrier()
        self.tt("vector", self.colf[:, 256:288], self.colf[:, 256:288],
                self.colf[:, 320:352], ALU.add)
        p.barrier()
        self.tss("vector", self.colf[:, 320:352], self.colf[:, 288:320], -0.5,
                 ALU.is_lt)
        p.barrier()
        self.tt("vector", self.colf[:, 256:288], self.colf[:, 256:288],
                self.colf[:, 320:352], ALU.subtract)
        p.barrier()
        self.stt("vector", self.colf[:, 224:256], self.colf[:, 256:288],
                 -TWO_PI, self.colf[:, 128:160], ALU.mult, ALU.add)
        p.barrier()
        self.ts2("vector", self.colf[:, 224:256], self.colf[:, 224:256],
                 -3.14159165, 3.14159165, ALU.max, ALU.min)
        p.barrier()
        self.act(self.colf[:, 256:288], self.colf[:, 224:256], AF.Sin)
        p.barrier()
        self.tss("vector", self.colf[:, 288:320], self.colf[:, 224:256],
                 1.5707963267948966, ALU.add)
        p.barrier()
        self.tss("vector", self.colf[:, 320:352], self.colf[:, 288:320],
                 3.14159265, ALU.is_gt)
        p.barrier()
        self.stt("vector", self.colf[:, 288:320], self.colf[:, 320:352],
                 -TWO_PI, self.colf[:, 288:320], ALU.mult, ALU.add)
        p.barrier()
        self.act(self.colf[:, 288:320], self.colf[:, 288:320], AF.Sin)
        p.barrier()
        self.tt("vector", self.colf[:, 320:352], self.colf[:, 288:320],
                self.colf[:, 160:192], ALU.mult)
        p.barrier()
        self.stt("vector", self.colf[:, 320:352], self.colf[:, 256:288],
                 1.0, self.colf[:, 320:352], ALU.bypass, ALU.bypass)
        p.barrier()
        self.tt("vector", self.colf[:, 352:384], self.colf[:, 256:288],
                self.colf[:, 192:224], ALU.mult)
        self.tt("gpsimd", self.colf[:, 320:352], self.colf[:, 288:320],
                self.colf[:, 160:192], ALU.mult)
        p.barrier()
        self.tt("vector", self.colf[:, 320:352], self.colf[:, 320:352],
                self.colf[:, 352:384], ALU.add)
        p.barrier()
        self.tt("vector", self.colf[:, 352:384], self.colf[:, 320:352],
                self.colf[:, 192:224], ALU.mult)
        p.barrier()
        self.tss("vector", self.colf[:, 352:384], self.colf[:, 352:384], 0.1,
                 ALU.mult)
        p.barrier()
        self.tt("vector", self.bcolf[:, 32:64], self.colf[:, 352:384],
                self.colf[:, 256:288], ALU.mult)
        self.tt("gpsimd", self.bcolf[:, 64:96], self.colf[:, 352:384],
                self.colf[:, 288:320], ALU.mult)
        p.barrier()
        for quarter in range(4):
            self.load_rows_bf(self.inp["l2_b1row"][:, quarter * 1024:(quarter + 1) * 1024],
                              1, 1024, lambda a, b: self.resw2[0:1, a:b])
            self.load_weight(self.inp["l2_W1"][:, quarter * 1024:(quarter + 1) * 1024],
                             D, 1024, self.wbf, 0)
            self.load_weight(self.inp["l2_W2"][quarter * 1024:(quarter + 1) * 1024, :],
                             1024, 1024, self.wbf, 8192)
            for ho in range(8):
                hg = quarter * 8 + ho
                for s in range(NSUB):
                    sl = slice(s * TS, (s + 1) * TS)
                    for k in range(NK):
                        self.mm(self.ps[s % 4][:, :],
                                self.wbf[:, k * 1024 + ho * 128:
                                          k * 1024 + (ho + 1) * 128],
                                self.xb[k][:, sl], k == 0, False)
                    self.mm(self.ps[s % 4][:, :],
                            self.resw2[0:1, ho * 128:(ho + 1) * 128],
                            self.onesrow, False, True)
                    p.barrier()
                    self.act(g1[:, 0:512], self.ps[s % 4][:, :], AF.Copy)
                    p.barrier()
                    self.tss("vector", g2[:, 0:512], g1[:, 0:512], 0.1, ALU.mult)
                    p.barrier()
                    self.ts2("vector", g2[:, 0:512], g2[:, 0:512],
                             -1.5707963, 1.5707963, ALU.max, ALU.min)
                    p.barrier()
                    self.act(g2[:, 512:1024], g2[:, 0:512], AF.Sin)
                    p.barrier()
                    self.tss("vector", g2[:, 1024:1536], g2[:, 0:512],
                             1.5707963267948966, ALU.add)
                    p.barrier()
                    self.act(g2[:, 1024:1536], g2[:, 1024:1536], AF.Sin)
                    p.barrier()
                    self.stt("vector", g1[:, 0:512], g2[:, 1024:1536],
                             self.bcolf[:, 32 + hg:33 + hg], g1[:, 0:512],
                             ALU.mult, ALU.add)
                    p.barrier()
                    self.stt("vector", g1[:, 0:512], g2[:, 512:1024],
                             self.bcolf[:, 64 + hg:65 + hg], g1[:, 0:512],
                             ALU.mult, ALU.add)
                    p.barrier()
                    self.tss("vector", self.hslot(ho)[:, sl], g1[:, 0:512],
                             0.0, ALU.max)
                    p.barrier()
            if quarter == 3:
                self.load_rows_bf(self.inp["l2_b2row"], 1, D,
                                  lambda a, b: self.resw2[0:1, a:b])
            for o in range(NK):
                for s in range(NSUB):
                    sl = slice(s * TS, (s + 1) * TS)
                    for hc in range(8):
                        self.mm(self.ps[s % 4][:, :],
                                self.wbf[:, 8192 + hc * 1024 + o * 128:
                                          8192 + hc * 1024 + (o + 1) * 128],
                                self.hslot(hc)[:, sl], hc == 0,
                                quarter < 3 and hc == 7)
                    if quarter == 3:
                        self.mm(self.ps[s % 4][:, :],
                                self.resw2[0:1, o * 128:(o + 1) * 128],
                                self.onesrow, False, True)
                p.barrier()
                for s in range(NSUB):
                    sl = slice(s * TS, (s + 1) * TS)
                    self.stt("vector", self.xs[o][:, sl], self.ps[s % 4][:, :],
                             0.5, self.xs[o][:, sl], ALU.mult, ALU.add)
                p.barrier()

    def emit_ffn3(self):
        p = self.p
        self.emit_ln(self.xs, self.xb)
        g0, g1 = self.gf[0], self.gf[1]
        slab = self.resw[:, 512:2560]
        for k in range(NK):
            self.cp("vector", g0[:, k:k + 1], self.xb[k][:, T - 1:T])
        p.barrier()
        p.dma("sync", self.ag3_in, g0[:, 0:8])
        p.barrier()
        self.p.op("gpsimd", lambda q: q.collective_compute(
            "AllGather", ALU.bypass, replica_groups=[[0, 1, 2, 3, 4, 5, 6, 7]],
            ins=[self.ag3_in], outs=[self.ag3_out]))
        p.barrier()
        for r in range(8):
            p.dma("sync", g0[:, 8 + r * 8:16 + r * 8],
                  self.ag3_out[r * 128:(r + 1) * 128, :])
        p.barrier()
        self.dma_bcast_small(
            None, self.colf[:, 384:392], 8, 1,
            src_dram=self.inp["comb3_prev"])
        self.memset("vector", g0[:, 72:80], 0.0)
        p.barrier()
        for r in range(8):
            self.stt("vector", g0[:, 72:80], g0[:, 8 + r * 8:16 + r * 8],
                     self.colf[:, 384 + r:385 + r], g0[:, 72:80],
                     ALU.mult, ALU.add)
            p.barrier()
        self.load_col(self.inp["l3f_mur"], self.bcolf[:, 0:8])
        self.load_col(self.inp["l3f_muk"], self.bcolf[:, 8:16])
        p.barrier()
        self.ts2("vector", self.bcolf[:, 16:24], self.bcolf[:, 0:8],
                 -1.0, 1.0, ALU.mult, ALU.add)
        self.ts2("gpsimd", self.bcolf[:, 24:32], self.bcolf[:, 8:16],
                 -1.0, 1.0, ALU.mult, ALU.add)
        p.barrier()
        for k in range(NK):
            self.cp("vector", slab[:, 1:T], self.xb[k][:, 0:T - 1])
            self.cp("gpsimd", slab[:, 0:1], g0[:, 72 + k:73 + k])
            p.barrier()
            self.tss("vector", self.hslot(k)[:], self.xb[k][:],
                     self.bcolf[:, k:k + 1] if False else self.bcolf[:, 0:1]
                     if False else self.bcolf[:, 0:8][:, k:k + 1], ALU.mult)
            p.barrier()
            self.stt("vector", self.hslot(k)[:], slab[:],
                     self.bcolf[:, 16 + k:17 + k] if False else
                     self.bcolf[:, 16:24][:, k:k + 1],
                     self.hslot(k)[:], ALU.mult, ALU.add)
            p.barrier()
            self.tss("vector", slab[:], slab[:],
                     self.bcolf[:, 24:32][:, k:k + 1], ALU.mult)
            p.barrier()
            self.stt("vector", slab[:], self.xb[k][:],
                     self.bcolf[:, 8:16][:, k:k + 1], slab[:],
                     ALU.mult, ALU.add)
            p.barrier()
            p.dma("sync", self.i_k[:, k * T:(k + 1) * T], slab[:])
            p.barrier()
        self.load_weight("l3f_wr", D, D, self.wbf, 0)
        for o in range(NK):
            for s in range(NSUB):
                sl = slice(s * TS, (s + 1) * TS)
                for k in range(NK):
                    self.mm(self.ps[s % 4][:, :],
                            self.wbf[:, k * D + o * 128:k * D + (o + 1) * 128],
                            self.hslot(k)[:, sl], k == 0, k == NK - 1)
            p.barrier()
            for s in range(NSUB):
                self.act(self.xb[o][:, s * TS:(s + 1) * TS],
                         self.ps[s % 4][:, :], AF.Sigmoid)
            p.barrier()
        for k in range(NK):
            p.dma("sync", self.hslot(k)[:], self.i_k[:, k * T:(k + 1) * T])
        p.barrier()
        self.load_weight("l3f_wk", D, D, self.wbf, 0)
        for o in range(NK):
            for s in range(NSUB):
                sl = slice(s * TS, (s + 1) * TS)
                for k in range(NK):
                    self.mm(self.ps[s % 4][:, :],
                            self.wbf[:, k * D + o * 128:k * D + (o + 1) * 128],
                            self.hslot(k)[:, sl], k == 0, k == NK - 1)
            p.barrier()
            for s in range(NSUB):
                sl = slice(s * TS, (s + 1) * TS)
                self.tss("vector", g1[:, 0:512], self.ps[s % 4][:, :], 0.0,
                         ALU.max)
                p.barrier()
                self.tt("vector", slab[:, sl], g1[:, 0:512], g1[:, 0:512],
                        ALU.mult)
                p.barrier()
            p.dma("sync", self.i_v[:, o * T:(o + 1) * T], slab[:])
            p.barrier()
        for k in range(NK):
            p.dma("sync", self.hslot(k)[:], self.i_v[:, k * T:(k + 1) * T])
        p.barrier()
        self.load_weight("l3f_wv", D, D, self.wbf, 0)
        for o in range(NK):
            for s in range(NSUB):
                sl = slice(s * TS, (s + 1) * TS)
                for k in range(NK):
                    self.mm(self.ps[s % 4][:, :],
                            self.wbf[:, k * D + o * 128:k * D + (o + 1) * 128],
                            self.hslot(k)[:, sl], k == 0, k == NK - 1)
            p.barrier()
            for s in range(NSUB):
                sl = slice(s * TS, (s + 1) * TS)
                self.tt("vector", g1[:, 0:512], self.ps[s % 4][:, :],
                        self.xb[o][:, sl], ALU.mult)
                p.barrier()
                self.stt("vector", self.xs[o][:, sl], g1[:, 0:512], 0.5,
                         self.xs[o][:, sl], ALU.mult, ALU.add)
                p.barrier()

    def xyslot(self, i):
        return self.hslot(i) if i < 8 else self.xb[i - 8]

    def emit_ln2_affine(self, xb, gcol, bcol):
        p = self.p
        mu_bf, rstd_bf = self.gb[4], self.gb[5]
        sq = [self.gb[2], self.gb[3]]
        for k in range(NK):
            for s in range(NSUB):
                self.mm(self.ps[s][:, :], self.ones_bf,
                        xb[k][:, s * TS:(s + 1) * TS], k == 0, k == NK - 1)
        self.tt("vector", sq[0][:], xb[0][:], xb[0][:], ALU.mult)
        p.barrier()
        for s in range(NSUB):
            sl = slice(s * TS, (s + 1) * TS)
            self.act(self.gf[0][:, sl], self.ps[s][:, :], AF.Copy, scale=1.0 / D)
        p.barrier()
        for s in range(NSUB):
            sl = slice(s * TS, (s + 1) * TS)
            self.cp("scalar", mu_bf[:, sl], self.gf[0][:, sl])
        vbank = [self.ps[4], self.ps[5], self.ps[0], self.ps[1]]
        for k in range(NK):
            for s in range(NSUB):
                self.mm(vbank[s][:, :], self.ones_bf,
                        sq[k % 2][:, s * TS:(s + 1) * TS], k == 0, k == NK - 1)
            if k + 1 < NK:
                self.tt("vector", sq[(k + 1) % 2][:], xb[k + 1][:],
                        xb[k + 1][:], ALU.mult)
            p.barrier()
        for s in range(NSUB):
            sl = slice(s * TS, (s + 1) * TS)
            self.ts2("vector", self.gf[1][:, sl], vbank[s][:, :],
                     1.0 / D, LN_EPS, ALU.mult, ALU.add)
        self.stt("vector", self.gf[2][:], self.gf[0][:], -1.0, self.gf[0][:],
                 ALU.mult, ALU.mult)
        p.barrier()
        self.tt("vector", self.gf[1][:], self.gf[1][:], self.gf[2][:], ALU.add)
        p.barrier()
        self.recip(self.gf[1][:], self.gf[1][:])
        p.barrier()
        self.act(rstd_bf[:], self.gf[1][:], AF.Sqrt)
        p.barrier()
        for k in range(NK):
            eng = ("vector", "gpsimd")[k % 2]
            self.tt(eng, xb[k][:], xb[k][:], mu_bf[:], ALU.subtract)
        p.barrier()
        for k in range(NK):
            eng = ("vector", "gpsimd")[k % 2]
            self.tt(eng, xb[k][:], xb[k][:], rstd_bf[:], ALU.mult)
        p.barrier()
        for k in range(NK):
            eng = ("vector", "gpsimd")[k % 2]
            self.ts2(eng, xb[k][:], xb[k][:], gcol[:, k:k + 1], bcol[:, k:k + 1],
                     ALU.mult, ALU.add)
        p.barrier()

    def emit_conv4(self, slot, prev3, dc):
        p = self.p
        slab = self.resw[:, 512:2560]
        wc = lambda j: self.colf[:, 384 + dc * 4 + j:385 + dc * 4 + j]
        self.tss("vector", slab[:], slot[:], wc(3), ALU.mult)
        p.barrier()
        for j, off in ((2, 1), (1, 2), (0, 3)):
            self.stt("vector", slab[:, off:T], slot[:, 0:T - off], wc(j),
                     slab[:, off:T], ALU.mult, ALU.add)
            p.barrier()
        for t in range(3):
            for j in range(3 - t):
                self.stt("vector", slab[:, t:t + 1], prev3[:, t + j:t + j + 1],
                         wc(j), slab[:, t:t + 1], ALU.mult, ALU.add)
            p.barrier()
        self.p.op("scalar", lambda q: q.activation(
            slot[:], slab[:], AF.Silu, bias=self.bcolf[:, 64 + dc:65 + dc]))
        p.barrier()

    def _mamba_gather_prev(self, tag):
        p = self.p
        p.dma("sync", self.ag4_in[:, 0:24], self.gf[0][:, 0:24])
        p.barrier()
        self.p.op("gpsimd", lambda q: q.collective_compute(
            "AllGather", ALU.bypass, replica_groups=[[0, 1, 2, 3, 4, 5, 6, 7]],
            ins=[self.ag4_in], outs=[self.ag4_out]))
        p.barrier()
        for r in range(8):
            p.dma("sync", self.gf[0][:, 24 + r * 24:48 + r * 24],
                  self.ag4_out[r * 128:(r + 1) * 128, 0:24])
        p.barrier()
        self.memset("vector", self.gf[0][:, 0:24], 0.0)
        p.barrier()
        for r in range(8):
            self.stt("vector", self.gf[0][:, 0:24],
                     self.gf[0][:, 24 + r * 24:48 + r * 24],
                     self.colf[:, 448 + r:449 + r], self.gf[0][:, 0:24],
                     ALU.mult, ALU.add)
            p.barrier()
        p.barrier()

    def emit_mamba(self):
        p = self.p
        TH = 1024
        g0, g1, g2 = self.gf[0], self.gf[1], self.gf[2]
        slab = self.resw[:, 512:2560]
        self.emit_ln(self.xs, self.xb)
        for k in range(NK):
            p.dma("sync", self.i_k[:, k * T:(k + 1) * T], self.xb[k][:])
        p.barrier()
        self.load_col(self.inp["l2a_lng"], self.bcolf[:, 0:8])
        self.load_col(self.inp["l2a_lnb"], self.bcolf[:, 8:16])
        p.barrier()
        self.emit_ln2_affine(self.xb, self.bcolf[:, 0:8], self.bcolf[:, 8:16])
        self.load_col(self.inp["l2a_convw"], self.colf[:, 384:448])
        self.load_col(self.inp["l2a_wdt"], self.bcolf[:, 16:32])
        self.load_col(self.inp["l2a_bdt"], self.bcolf[:, 32:48])
        self.load_col(self.inp["l2a_alog"], self.colf[:, 128:384])
        self.load_col(self.inp["l2a_dpar"], self.bcolf[:, 48:64])
        self.load_col(self.inp["l2a_convb"], self.bcolf[:, 64:80])
        self.dma_bcast_small(None, self.colf[:, 448:456], 8, 1,
                             src_dram=self.inp["comb3_prev"])
        p.barrier()
        self.act(self.colf[:, 128:384], self.colf[:, 128:384], AF.Exp)
        p.barrier()
        self.tss("vector", self.colf[:, 128:384], self.colf[:, 128:384], -1.0,
                 ALU.mult)
        p.barrier()
        for half in range(2):
            self.load_weight(self.inp["l2a_Win"][:, half * 1024:(half + 1) * 1024],
                             D, 1024, self.wbf, 0)
            for i in range(8):
                dc = half * 8 + i
                for s in range(NSUB):
                    sl = slice(s * TS, (s + 1) * TS)
                    for k in range(NK):
                        self.mm(self.ps[s % 4][:, :],
                                self.wbf[:, k * 1024 + i * 128:
                                          k * 1024 + (i + 1) * 128],
                                self.xb[k][:, sl], k == 0, k == NK - 1)
                p.barrier()
                for s in range(NSUB):
                    self.cp("scalar", self.hslot(i)[:, s * TS:(s + 1) * TS],
                            self.ps[s % 4][:, :])
                p.barrier()
            for i in range(8):
                self.cp("vector", g0[:, i * 3:i * 3 + 3],
                        self.hslot(i)[:, T - 3:T])
            p.barrier()
            self._mamba_gather_prev("cv%d" % half)
            for i in range(8):
                dc = half * 8 + i
                self.emit_conv4(self.hslot(i), g0[:, i * 3:i * 3 + 3], dc)
                p.dma("sync", self.i_xc[:, dc * T:(dc + 1) * T], self.hslot(i)[:])
            p.barrier()
        for zh in range(2):
            self.load_weight(
                self.inp["l2a_Win"][:, 2048 + zh * 1024:2048 + (zh + 1) * 1024],
                D, 1024, self.wbf, 0)
            for i in range(8):
                dc = zh * 8 + i
                for s in range(NSUB):
                    sl = slice(s * TS, (s + 1) * TS)
                    for k in range(NK):
                        self.mm(self.ps[s % 4][:, :],
                                self.wbf[:, k * 1024 + i * 128:
                                          k * 1024 + (i + 1) * 128],
                                self.xb[k][:, sl], k == 0, k == NK - 1)
                p.barrier()
                for s in range(NSUB):
                    self.cp("scalar", slab[:, s * TS:(s + 1) * TS],
                            self.ps[s % 4][:, :])
                p.barrier()
                p.dma("sync", self.i_z[:, dc * T:(dc + 1) * T], slab[:])
                p.barrier()
        self.load_weight("l2a_Wx", 2048, 33, self.resw2, 0)
        for gdc in range(2):
            for i in range(8):
                dc = gdc * 8 + i
                p.dma("sync", self.hslot(i)[:], self.i_xc[:, dc * T:(dc + 1) * T])
            p.barrier()
            for s in range(NSUB):
                sl = slice(s * TS, (s + 1) * TS)
                for i in range(8):
                    c = gdc * 8 + i
                    self.mm(self.ps[s][0:1, :], self.resw2[:, c * 33:c * 33 + 1],
                            self.hslot(i)[:, sl], gdc == 0 and i == 0,
                            gdc == 1 and i == 7)
                    self.mm(self.ps[s][32:48, :],
                            self.resw2[:, c * 33 + 1:c * 33 + 17],
                            self.hslot(i)[:, sl], gdc == 0 and i == 0,
                            gdc == 1 and i == 7)
                    self.mm(self.ps[s][64:80, :],
                            self.resw2[:, c * 33 + 17:c * 33 + 33],
                            self.hslot(i)[:, sl], gdc == 0 and i == 0,
                            gdc == 1 and i == 7)
            p.barrier()
        for s in range(NSUB):
            sl = slice(s * TS, (s + 1) * TS)
            self.cp("scalar", g0[0:1, sl], self.ps[s][0:1, :])
            self.cp("vector", slab[0:16, sl], self.ps[s][32:48, :])
            self.cp("scalar", self.wbf[0:16, sl], self.ps[s][64:80, :])
        p.barrier()
        p.dma("sync", self.i_bcB, slab[0:16, 0:T])
        p.dma("sync", self.i_bcC, self.wbf[0:16, 0:T])
        p.dma("sync", self.i_bcD, g0[0:1, 0:T])
        p.barrier()
        self.memset("vector", g2[:, 0:256], 0.0)
        p.barrier()
        for half in range(2):
            hsl = slice(half * TH, (half + 1) * TH)
            p.dma("sync", g1[:, 0:TH],
                  self.free_bcast(self.i_bcD[0:1, hsl], 1) if False else
                  self.i_bcD[0:1, hsl].partition_broadcast(128))
            p.barrier()
            for dc in range(16):
                xcs = self.hslot(dc // 2)[:, (dc % 2) * TH:(dc % 2 + 1) * TH]
                p.dma("sync", xcs, self.i_xc[:, dc * T + half * TH:
                                             dc * T + (half + 1) * TH])
            p.barrier()
            for dc in range(16):
                dts = self.xb[dc // 2][:, (dc % 2) * TH:(dc % 2 + 1) * TH]
                xcs = self.hslot(dc // 2)[:, (dc % 2) * TH:(dc % 2 + 1) * TH]
                self.p.op("scalar", lambda q, dts=dts, dc=dc:
                          q.activation(g2[:, 1024:1024 + TH], g1[:, 0:TH], AF.Exp,
                                       scale=self.bcolf[:, 16 + dc:17 + dc],
                                       bias=self.bcolf[:, 32 + dc:33 + dc]))
                p.barrier()
                self.tss("vector", g2[:, 1024:1024 + TH], g2[:, 1024:1024 + TH],
                         1.0, ALU.add)
                p.barrier()
                self.act(g2[:, 1024:1024 + TH], g2[:, 1024:1024 + TH], AF.Ln)
                p.barrier()
                self.cp("vector", dts, g2[:, 1024:1024 + TH])
                p.barrier()
                self.tt("vector", xcs, xcs, dts, ALU.mult)
                p.barrier()
            for gdc in range(2):
                for s in range(16):
                    p.dma("sync", slab[:, 0:TH],
                          self.i_bcB[s:s + 1, hsl].partition_broadcast(128))
                    p.dma("gpsimd", slab[:, 1024:1024 + TH],
                          self.i_bcC[s:s + 1, hsl].partition_broadcast(128))
                    p.barrier()
                    for i in range(8):
                        dc = gdc * 8 + i
                        dts = self.xb[dc // 2][:, (dc % 2) * TH:(dc % 2 + 1) * TH]
                        xcs = self.hslot(dc // 2)[:, (dc % 2) * TH:(dc % 2 + 1) * TH]
                        dA = self.wbf[:, 0:TH]
                        dBx = self.wbf[:, TH:2 * TH]
                        hh = self.wbf[:, 2 * TH:3 * TH]
                        ysl = self.wbf[:, 8192 + i * TH:8192 + (i + 1) * TH]
                        acol = self.colf[:, 128 + dc * 16 + s:129 + dc * 16 + s]
                        hcol = g2[:, dc * 16 + s:dc * 16 + s + 1]
                        self.p.op("scalar", lambda q, dts=dts, dA=dA, acol=acol:
                                  q.activation(dA, dts, AF.Exp, scale=acol))
                        p.barrier()
                        self.tt("vector", dBx, xcs, slab[:, 0:TH], ALU.mult)
                        p.barrier()
                        self.scan("vector", hh, dA, dBx, hcol, ALU.mult, ALU.add)
                        p.barrier()
                        self.cp("vector", hcol, hh[:, TH - 1:TH])
                        self.tt("gpsimd", dBx, hh, slab[:, TH:2 * TH], ALU.mult)
                        p.barrier()
                        if s == 0:
                            self.cp("vector", ysl, dBx)
                        else:
                            self.tt("vector", ysl, ysl, dBx, ALU.add)
                        p.barrier()
                    p.barrier()
                for i in range(8):
                    dc = gdc * 8 + i
                    p.dma("sync",
                          self.i_o[:, (dc * 2 + half) * TH:(dc * 2 + half + 1) * TH],
                          self.wbf[:, 8192 + i * TH:8192 + (i + 1) * TH])
                p.barrier()
        p.dma("sync", self.ag4_in[:, 0:256], g2[:, 0:256])
        p.barrier()
        self.p.op("gpsimd", lambda q: q.collective_compute(
            "AllGather", ALU.bypass, replica_groups=[[0, 1, 2, 3, 4, 5, 6, 7]],
            ins=[self.ag4_in], outs=[self.ag4_out]))
        p.barrier()
        self.memset("vector", g2[:, 256:512], 0.0)
        p.barrier()
        for r in range(8):
            p.dma("sync", g2[:, 512:768], self.ag4_out[r * 128:(r + 1) * 128, :])
            p.barrier()
            self.stt("vector", g2[:, 256:512], g2[:, 512:768],
                     self.colf[:, 448 + r:449 + r], g2[:, 256:512],
                     ALU.mult, ALU.add)
            p.barrier()
        ycorr = self.wbf[:, 3072:4096]
        CW = 64
        self.memset("vector", ycorr, 0.0)
        p.barrier()
        p.dma("sync", g1[:, 0:CW],
              self.i_bcD[0:1, 0:CW].partition_broadcast(128))
        p.barrier()
        for dc in range(16):
            self.p.op("scalar", lambda q, dc=dc: q.activation(
                g2[:, 1024:1024 + CW], g1[:, 0:CW], AF.Exp,
                scale=self.bcolf[:, 16 + dc:17 + dc],
                bias=self.bcolf[:, 32 + dc:33 + dc]))
            p.barrier()
            self.tss("vector", g2[:, 1024:1024 + CW], g2[:, 1024:1024 + CW],
                     1.0, ALU.add)
            p.barrier()
            self.act(g2[:, 1024:1024 + CW], g2[:, 1024:1024 + CW], AF.Ln)
            p.barrier()
            for s in range(16):
                acol = self.colf[:, 128 + dc * 16 + s:129 + dc * 16 + s]
                hin = g2[:, 256 + dc * 16 + s:257 + dc * 16 + s]
                self.p.op("scalar", lambda q, acol=acol: q.activation(
                    self.wbf[:, 0:CW], g2[:, 1024:1024 + CW], AF.Exp, scale=acol))
                p.dma("sync", slab[:, 0:CW],
                      self.i_bcC[s:s + 1, 0:CW].partition_broadcast(128))
                p.barrier()
                self.scan("vector", self.wbf[:, TH:TH + CW], self.wbf[:, 0:CW],
                          self.wbf[:, 0:CW], 1.0, ALU.mult, ALU.bypass)
                p.barrier()
                self.tss("vector", self.wbf[:, TH:TH + CW], self.wbf[:, TH:TH + CW],
                         hin, ALU.mult)
                p.barrier()
                self.tt("vector", self.wbf[:, TH:TH + CW], self.wbf[:, TH:TH + CW],
                        slab[:, 0:CW], ALU.mult)
                p.barrier()
                self.tt("vector", ycorr[:, dc * CW:(dc + 1) * CW],
                        ycorr[:, dc * CW:(dc + 1) * CW],
                        self.wbf[:, TH:TH + CW], ALU.add)
                p.barrier()
        for dc in range(16):
            sl2 = self.xyslot(dc)
            for half in range(2):
                p.dma("sync", sl2[:, half * TH:(half + 1) * TH],
                      self.i_o[:, (dc * 2 + half) * TH:(dc * 2 + half + 1) * TH])
            p.barrier()
            self.tt("vector", sl2[:, 0:CW], sl2[:, 0:CW],
                    ycorr[:, dc * CW:(dc + 1) * CW], ALU.add)
            p.barrier()
            p.dma("sync", slab[:], self.i_xc[:, dc * T:(dc + 1) * T])
            p.barrier()
            self.stt("vector", sl2[:], slab[:], self.bcolf[:, 48 + dc:49 + dc],
                     sl2[:], ALU.mult, ALU.add)
            p.barrier()
            p.dma("sync", slab[:], self.i_z[:, dc * T:(dc + 1) * T])
            p.barrier()
            self.p.op("scalar", lambda q: q.activation(slab[:], slab[:], AF.Silu))
            p.barrier()
            self.tt("vector", sl2[:], sl2[:], slab[:], ALU.mult)
            p.barrier()
        self.load_weight("l2a_Wout", 2048, D, self.wbf, 0)
        for o in range(NK):
            p.dma("sync", slab[:], self.i_k[:, o * T:(o + 1) * T])
            p.barrier()
            for s in range(NSUB):
                sl = slice(s * TS, (s + 1) * TS)
                for k in range(16):
                    self.mm(self.ps[s % 4][:, :],
                            self.wbf[:, k * D + o * 128:k * D + (o + 1) * 128],
                            self.xyslot(k)[:, sl], k == 0, k == 15)
            p.barrier()
            for s in range(NSUB):
                sl = slice(s * TS, (s + 1) * TS)
                self.tt("vector", g1[:, 0:512], self.ps[s % 4][:, :],
                        slab[:, sl], ALU.add)
                p.barrier()
                self.tt("vector", self.xs[o][:, sl], self.xs[o][:, sl],
                        g1[:, 0:512], ALU.add)
                p.barrier()
    def _shift_mix(self, dest, srcpool, k, mixcol, prevcol):
        p = self.p
        slab = self.resw[:, 512:2560]
        self.cp("vector", slab[:, 1:T], srcpool[k][:, 0:T - 1])
        self.cp("gpsimd", slab[:, 0:1], prevcol)
        p.barrier()
        self.tss("vector", dest[:], srcpool[k][:], mixcol, ALU.mult)
        p.barrier()
        self.stt("vector", dest[:], slab[:], mixcol, dest[:], ALU.mult,
                 ALU.subtract_rev if False else ALU.add)
        p.barrier()
        self.tt("vector", dest[:], dest[:], slab[:], ALU.add)
        p.barrier()

    def shift_mix(self, dest, srcpool, k, mixcol, prevcol):
        p = self.p
        slab = self.resw[:, 512:2560]
        self.cp("vector", slab[:, 1:T], srcpool[k][:, 0:T - 1])
        self.cp("gpsimd", slab[:, 0:1], prevcol)
        p.barrier()
        self.tt("vector", slab[:], slab[:], srcpool[k][:], ALU.subtract)
        p.barrier()
        self.ts2("vector", slab[:], slab[:], -1.0, 0.0, ALU.mult, ALU.add)
        p.barrier()
        self.tss("vector", slab[:], slab[:], mixcol, ALU.mult)
        p.barrier()
        self.cp("vector", dest[:], slab[:])
        p.barrier()
        self.cp("gpsimd", slab[:, 1:T], srcpool[k][:, 0:T - 1])
        self.cp("vector", slab[:, 0:1], prevcol)
        p.barrier()
        self.tt("vector", dest[:], dest[:], slab[:], ALU.add)
        p.barrier()

    def gather_lastcols(self, srcpool, width=8):
        p = self.p
        g0 = self.gf[0]
        for k in range(NK):
            self.cp("vector", g0[:, k:k + 1], srcpool[k][:, T - 1:T])
        p.barrier()
        p.dma("sync", self.ag3_in, g0[:, 0:8])
        p.barrier()
        self.p.op("gpsimd", lambda q: q.collective_compute(
            "AllGather", ALU.bypass, replica_groups=[[0, 1, 2, 3, 4, 5, 6, 7]],
            ins=[self.ag3_in], outs=[self.ag3_out]))
        p.barrier()
        for r in range(8):
            p.dma("sync", g0[:, 8 + r * 8:16 + r * 8],
                  self.ag3_out[r * 128:(r + 1) * 128, :])
        p.barrier()
        self.memset("vector", g0[:, 72:80], 0.0)
        p.barrier()
        for r in range(8):
            self.stt("vector", g0[:, 72:80], g0[:, 8 + r * 8:16 + r * 8],
                     self.colf[:, 448 + r:449 + r], g0[:, 72:80],
                     ALU.mult, ALU.add)
        p.barrier()

    def emit_l3_attn(self):
        p = self.p
        g0, g1, g2 = self.gf[0], self.gf[1], self.gf[2]
        slab = self.resw[:, 512:2560]
        EPS2 = 1e-8
        self.emit_ln(self.xs, self.xb)
        for k in range(NK):
            p.dma("sync", self.i_k[:, k * T:(k + 1) * T], self.xb[k][:])
        p.barrier()
        self.load_col(self.inp["l3_ln1g"], self.bcolf[:, 0:8])
        self.load_col(self.inp["l3_ln1b"], self.bcolf[:, 8:16])
        self.load_col(self.inp["l3_mixk"], self.bcolf[:, 16:24])
        self.load_col(self.inp["l3_mixv"], self.bcolf[:, 24:32])
        self.load_col(self.inp["l3_mixr"], self.bcolf[:, 32:40])
        self.load_col(self.inp["l3_tf"], self.bcolf[:, 40:48])
        self.load_col(self.inp["l3_ln2g"], self.bcolf[:, 48:56])
        self.load_col(self.inp["l3_ln2b"], self.bcolf[:, 56:64])
        self.load_col(self.inp["l3_cmixk"], self.bcolf[:, 64:72])
        self.load_col(self.inp["l3_cmixr"], self.bcolf[:, 72:80])
        self.dma_bcast_small(None, self.colf[:, 448:456], 8, 1,
                             src_dram=self.inp["comb3_prev"])
        p.barrier()
        self.emit_ln2_affine(self.xb, self.bcolf[:, 0:8], self.bcolf[:, 8:16])
        self.gather_lastcols(self.xb)
        for name, mix_off, spill, sig in (("l3_wkt", 16, self.i_o, False),
                                          ("l3_wvt", 24, None, False),
                                          ("l3_wrt", 32, self.i_v, True)):
            for k in range(NK):
                self.shift_mix(self.hslot(k), self.xb, k,
                               self.bcolf[:, mix_off + k:mix_off + k + 1],
                               g0[:, 72 + k:73 + k])
            self.load_weight(name, D, D, self.wbf, 0)
            for o in range(NK):
                for s in range(NSUB):
                    sl = slice(s * TS, (s + 1) * TS)
                    for k in range(NK):
                        self.mm(self.ps[s % 4][:, :],
                                self.wbf[:, k * D + o * 128:k * D + (o + 1) * 128],
                                self.hslot(k)[:, sl], k == 0, k == NK - 1)
                p.barrier()
                for s in range(NSUB):
                    sl = slice(s * TS, (s + 1) * TS)
                    if sig:
                        self.act(slab[:, sl], self.ps[s % 4][:, :], AF.Sigmoid)
                    else:
                        self.cp("scalar", slab[:, sl], self.ps[s % 4][:, :])
                p.barrier()
                if DBG and o == 0:
                    self.dbg_dump(slab[:, 0:256], 256, cast=True,
                                  name=name + "_c0")
                dst = spill if spill is not None else self.i_o
                off = (o * T) if name != "l3_wvt" else (8192 * 2 + o * T)
                if name == "l3_wkt":
                    p.dma("sync", self.i_o[:, o * T:(o + 1) * T], slab[:])
                elif name == "l3_wvt":
                    p.dma("sync", self.i_o[:, 16384 + o * T:16384 + (o + 1) * T],
                          slab[:])
                else:
                    p.dma("sync", self.i_v[:, o * T:(o + 1) * T], slab[:])
                p.barrier()
        EPSW = 1e-08
        LOG1PEPS = 9.99999993922529e-09
        for o in range(NK):
            p.dma("sync", self.gb[0][:], self.i_o[:, o * T:(o + 1) * T])
            p.barrier()
            self.act(self.gb[0][:], self.gb[0][:], AF.Exp)
            p.barrier()
            self.reduce("vector", g2[:, 512 + o:513 + o], self.gb[0][:], ALU.add)
            p.barrier()
        p.dma("sync", self.ag3_in, g2[:, 512:520])
        p.barrier()
        self.p.op("gpsimd", lambda q: q.collective_compute(
            "AllGather", ALU.bypass, replica_groups=[[0, 1, 2, 3, 4, 5, 6, 7]],
            ins=[self.ag3_in], outs=[self.ag3_out]))
        p.barrier()
        for r in range(8):
            p.dma("sync", g0[:, 8 + r * 8:16 + r * 8],
                  self.ag3_out[r * 128:(r + 1) * 128, :])
        self.load_col(self.inp["comb2_prec"], self.colf[0:8, 30:31])
        p.barrier()
        self.dma_bcast_small(self.colf[0:8, 30:31], self.colf[:, 440:448], 8, 1)
        self.memset("vector", g2[:, 520:528], 0.0)
        p.barrier()
        for r in range(8):
            self.stt("vector", g2[:, 520:528], g0[:, 8 + r * 8:16 + r * 8],
                     self.colf[:, 440 + r:441 + r], g2[:, 520:528],
                     ALU.mult, ALU.add)
        p.barrier()
        for o in range(NK):
            p.dma("sync", self.gb[0][:], self.i_o[:, o * T:(o + 1) * T])
            p.barrier()
            self.act(self.gb[0][:], self.gb[0][:], AF.Exp)
            p.barrier()
            self.scan("vector", self.gb[1][:], self.gb[0][:], self.gb[0][:],
                      g2[:, 520 + o:521 + o], ALU.add, ALU.bypass)
            p.barrier()
            if DBG and o == 0:
                self.dbg_dump(self.gb[0][:, 0:256], 256, cast=True, name="ek0")
                self.dbg_dump(self.gb[1][:, 0:256], 256, cast=True, name="Zap0")
            self.cp("vector", self.gb[2][:, 1:T], self.gb[1][:, 0:T - 1])
            self.cp("gpsimd", self.gb[2][:, 0:1], g2[:, 520 + o:521 + o])
            p.barrier()
            self.tt("vector", self.gb[3][:], self.gb[2][:], self.gb[0][:],
                    ALU.is_ge)
            p.barrier()
            self.reduce("vector", g2[:, 528 + o:529 + o], self.gb[3][:], ALU.add)
            self.ts2("gpsimd", self.gb[4][:], self.gb[3][:], EPSW, 1.0,
                     ALU.mult, ALU.add)
            p.barrier()
            self.ts2("vector", self.gb[3][:], self.gb[3][:], -EPSW, 1.0 + EPSW,
                     ALU.mult, ALU.add)
            p.barrier()
            self.tt("vector", self.gb[3][:], self.gb[0][:], self.gb[3][:],
                    ALU.mult)
            p.barrier()
            self.scan("vector", self.gb[1][:], self.gb[4][:], self.gb[3][:],
                      0.0, ALU.mult, ALU.add)
            p.barrier()
            if DBG and o == 0:
                self.dbg_dump(self.gb[1][:, 0:256], 256, cast=True,
                              name="Zexloc0")
            self.scan("vector", self.gb[5][:], self.gb[4][:], self.gb[4][:],
                      1.0, ALU.mult, ALU.bypass)
            p.barrier()
            self.cp("vector", g2[:, 536 + o:537 + o], self.gb[1][:, T - 1:T])
            p.dma("sync", self.i_xc[:, o * T:(o + 1) * T], self.gb[1][:])
            p.dma("gpsimd", self.i_z[:, o * T:(o + 1) * T], self.gb[5][:])
            p.barrier()
        self.p.op("scalar", lambda q: q.activation(
            g2[:, 528:536], g2[:, 528:536], AF.Exp, scale=LOG1PEPS))
        p.barrier()
        self.cp("vector", g0[:, 0:8], g2[:, 536:544])
        self.cp("gpsimd", g0[:, 80:88], g2[:, 528:536])
        p.barrier()
        p.dma("sync", self.ag3_in, g0[:, 0:8])
        p.barrier()
        self.p.op("gpsimd", lambda q: q.collective_compute(
            "AllGather", ALU.bypass, replica_groups=[[0, 1, 2, 3, 4, 5, 6, 7]],
            ins=[self.ag3_in], outs=[self.ag3_out]))
        p.barrier()
        for r in range(8):
            p.dma("sync", g0[:, 8 + r * 8:16 + r * 8],
                  self.ag3_out[r * 128:(r + 1) * 128, :])
        p.barrier()
        p.dma("sync", self.ag3_in, g0[:, 80:88])
        p.barrier()
        self.p.op("gpsimd", lambda q: q.collective_compute(
            "AllGather", ALU.bypass, replica_groups=[[0, 1, 2, 3, 4, 5, 6, 7]],
            ins=[self.ag3_in], outs=[self.ag3_out]))
        p.barrier()
        for r in range(8):
            p.dma("sync", g0[:, 88 + r * 8:96 + r * 8],
                  self.ag3_out[r * 128:(r + 1) * 128, :])
        self.load_col(self.inp["comb_alpha"], self.colf[0:8, 24:25])
        self.load_col(self.inp["comb_gamma"], self.colf[0:8, 25:26])
        self.load_col(self.inp["comb_a"], self.colf[0:8, 26:27])
        p.barrier()
        self.dma_bcast_small(self.colf[0:8, 24:27], self.colf[:, 416:440], 8, 3)
        self.memset("vector", g2[:, 544:552], 0.0)
        p.barrier()
        for r in range(8):
            self.tss("vector", g0[:, 160:168], g0[:, 88 + r * 8:96 + r * 8],
                     self.colf[:, 416 + r * 3:417 + r * 3], ALU.mult)
            p.barrier()
            self.tss("vector", g0[:, 160:168], g0[:, 160:168],
                     self.colf[:, 417 + r * 3:418 + r * 3], ALU.add)
            p.barrier()
            self.tt("vector", g2[:, 544:552], g2[:, 544:552], g0[:, 160:168],
                    ALU.mult)
            p.barrier()
            self.stt("vector", g2[:, 544:552], g0[:, 8 + r * 8:16 + r * 8],
                     self.colf[:, 418 + r * 3:419 + r * 3], g2[:, 544:552],
                     ALU.mult, ALU.add)
            p.barrier()
        if DBG:
            self.dbg_dump(g2[:, 520:528], 8, name="Zinap")
            self.dbg_dump(g2[:, 544:552], 8, name="Zinex")
        for o in range(NK):
            p.dma("sync", self.gb[0][:], self.i_o[:, o * T:(o + 1) * T])
            p.dma("gpsimd", self.gb[1][:], self.i_xc[:, o * T:(o + 1) * T])
            p.barrier()
            self.act(self.gb[0][:], self.gb[0][:], AF.Exp)
            p.barrier()
            p.dma("sync", self.gb[5][:], self.i_z[:, o * T:(o + 1) * T])
            p.barrier()
            self.stt("vector", self.gb[1][:], self.gb[5][:],
                     g2[:, 544 + o:545 + o], self.gb[1][:], ALU.mult, ALU.add)
            p.barrier()
            self.cp("vector", self.gb[2][:, 1:T], self.gb[1][:, 0:T - 1])
            self.cp("gpsimd", self.gb[2][:, 0:1], g2[:, 544 + o:545 + o])
            p.barrier()
            self.tt("vector", self.gb[3][:], self.gb[2][:], self.gb[0][:],
                    ALU.max)
            p.barrier()
            self.recip(g1[:, 0:T], self.gb[3][:])
            p.barrier()
            self.tt("vector", self.gb[4][:], self.gb[2][:], g1[:, 0:T],
                    ALU.mult)
            p.barrier()
            self.tt("vector", self.gb[3][:], self.gb[0][:], g1[:, 0:T],
                    ALU.mult)
            p.barrier()
            p.dma("sync", self.gb[0][:], self.i_o[:, 16384 + o * T:
                                                  16384 + (o + 1) * T])
            p.barrier()
            self.tt("vector", self.gb[3][:], self.gb[3][:], self.gb[0][:],
                    ALU.mult)
            p.barrier()
            self.scan("vector", self.gb[0][:], self.gb[4][:], self.gb[3][:],
                      0.0, ALU.mult, ALU.add)
            p.barrier()
            if DBG and o == 0:
                self.dbg_dump(self.gb[4][:, 0:256], 256, cast=True, name="e1b0")
                self.dbg_dump(self.gb[0][:, 0:256], 256, cast=True,
                              name="aaloc0")
            self.scan("vector", self.gb[5][:], self.gb[4][:], self.gb[4][:],
                      1.0, ALU.mult, ALU.bypass)
            p.barrier()
            self.cp("vector", g2[:, 552 + o:553 + o], self.gb[0][:, T - 1:T])
            self.cp("gpsimd", g2[:, 560 + o:561 + o], self.gb[5][:, T - 1:T])
            p.dma("sync", self.i_xc[:, 16384 + o * T:16384 + (o + 1) * T],
                  self.gb[0][:])
            p.dma("gpsimd", self.i_z[:, 16384 + o * T:16384 + (o + 1) * T],
                  self.gb[5][:])
            p.barrier()
        p.dma("sync", self.ag3_in, g2[:, 552:560])
        p.barrier()
        self.p.op("gpsimd", lambda q: q.collective_compute(
            "AllGather", ALU.bypass, replica_groups=[[0, 1, 2, 3, 4, 5, 6, 7]],
            ins=[self.ag3_in], outs=[self.ag3_out]))
        p.barrier()
        for r in range(8):
            p.dma("sync", g0[:, 8 + r * 8:16 + r * 8],
                  self.ag3_out[r * 128:(r + 1) * 128, :])
        p.barrier()
        p.dma("sync", self.ag3_in, g2[:, 560:568])
        p.barrier()
        self.p.op("gpsimd", lambda q: q.collective_compute(
            "AllGather", ALU.bypass, replica_groups=[[0, 1, 2, 3, 4, 5, 6, 7]],
            ins=[self.ag3_in], outs=[self.ag3_out]))
        p.barrier()
        for r in range(8):
            p.dma("sync", g0[:, 88 + r * 8:96 + r * 8],
                  self.ag3_out[r * 128:(r + 1) * 128, :])
        p.barrier()
        self.memset("vector", g2[:, 568:576], 0.0)
        p.barrier()
        for r in range(8):
            self.tss("vector", g0[:, 160:168], g0[:, 88 + r * 8:96 + r * 8],
                     self.colf[:, 416 + r * 3:417 + r * 3], ALU.mult)
            p.barrier()
            self.tss("vector", g0[:, 160:168], g0[:, 160:168],
                     self.colf[:, 417 + r * 3:418 + r * 3], ALU.add)
            p.barrier()
            self.tt("vector", g2[:, 568:576], g2[:, 568:576], g0[:, 160:168],
                    ALU.mult)
            p.barrier()
            self.stt("vector", g2[:, 568:576], g0[:, 8 + r * 8:16 + r * 8],
                     self.colf[:, 418 + r * 3:419 + r * 3], g2[:, 568:576],
                     ALU.mult, ALU.add)
            p.barrier()
        if DBG:
            self.dbg_dump(g2[:, 568:576], 8, name="aain")
        for o in range(NK):
            p.dma("sync", self.gb[0][:], self.i_o[:, o * T:(o + 1) * T])
            p.dma("gpsimd", self.gb[1][:], self.i_xc[:, o * T:(o + 1) * T])
            p.barrier()
            self.act(self.gb[0][:], self.gb[0][:], AF.Exp)
            p.barrier()
            p.dma("sync", self.gb[5][:], self.i_z[:, o * T:(o + 1) * T])
            p.barrier()
            self.stt("vector", self.gb[1][:], self.gb[5][:],
                     g2[:, 544 + o:545 + o], self.gb[1][:], ALU.mult, ALU.add)
            p.barrier()
            self.cp("vector", self.gb[2][:, 1:T], self.gb[1][:, 0:T - 1])
            self.cp("gpsimd", self.gb[2][:, 0:1], g2[:, 544 + o:545 + o])
            p.barrier()
            p.dma("sync", self.gb[1][:], self.i_xc[:, 16384 + o * T:
                                                   16384 + (o + 1) * T])
            p.dma("gpsimd", self.gb[5][:], self.i_z[:, 16384 + o * T:
                                                     16384 + (o + 1) * T])
            p.barrier()
            self.stt("vector", self.gb[1][:], self.gb[5][:],
                     g2[:, 568 + o:569 + o], self.gb[1][:], ALU.mult, ALU.add)
            p.barrier()
            self.cp("vector", self.gb[4][:, 1:T], self.gb[1][:, 0:T - 1])
            self.cp("gpsimd", self.gb[4][:, 0:1], g2[:, 568 + o:569 + o])
            p.barrier()
            p.dma("sync", self.gb[1][:], self.i_o[:, o * T:(o + 1) * T])
            p.barrier()
            self.p.op("scalar", lambda q, o=o: q.activation(
                self.gb[1][:], self.gb[1][:], AF.Exp,
                bias=self.bcolf[:, 40 + o:41 + o]))
            p.barrier()
            self.tt("vector", self.gb[3][:], self.gb[2][:], self.gb[1][:],
                    ALU.max)
            p.barrier()
            self.recip(g1[:, 0:T], self.gb[3][:])
            p.barrier()
            self.tt("vector", self.gb[2][:], self.gb[2][:], g1[:, 0:T],
                    ALU.mult)
            p.barrier()
            self.tt("vector", self.gb[1][:], self.gb[1][:], g1[:, 0:T],
                    ALU.mult)
            p.barrier()
            self.tt("vector", self.gb[4][:], self.gb[4][:], self.gb[2][:],
                    ALU.mult)
            p.barrier()
            p.dma("sync", self.gb[5][:], self.i_o[:, 16384 + o * T:
                                                  16384 + (o + 1) * T])
            p.barrier()
            self.tt("vector", self.gb[5][:], self.gb[5][:], self.gb[1][:],
                    ALU.mult)
            p.barrier()
            self.tt("vector", self.gb[4][:], self.gb[4][:], self.gb[5][:],
                    ALU.add)
            p.barrier()
            self.tt("vector", self.gb[1][:], self.gb[1][:], self.gb[2][:],
                    ALU.add)
            p.barrier()
            self.tss("vector", self.gb[1][:], self.gb[1][:], EPSW, ALU.add)
            p.barrier()
            self.recip(g1[:, 0:T], self.gb[1][:])
            p.barrier()
            self.tt("vector", self.gb[4][:], self.gb[4][:], g1[:, 0:T],
                    ALU.mult)
            p.barrier()
            if DBG and o == 0:
                self.dbg_dump(self.gb[2][:, 0:256], 256, cast=True, name="zsh0")
                self.dbg_dump(self.gb[4][:, 0:256], 256, cast=True, name="wkv0")
            p.dma("sync", self.gb[1][:], self.i_v[:, o * T:(o + 1) * T])
            p.barrier()
            self.tt("vector", self.gb[4][:], self.gb[4][:], self.gb[1][:],
                    ALU.mult)
            p.barrier()
            p.dma("sync", self.gb[1][:], self.i_k[:, o * T:(o + 1) * T])
            p.barrier()
            self.tt("vector", self.xb[o][:], self.gb[4][:], self.gb[1][:],
                    ALU.add)
            p.barrier()
            if DBG and o == 0:
                self.dbg_dump(self.xb[o][:, 0:256], 256, cast=True, name="x1_0")
        for k in range(NK):
            p.dma("sync", self.i_o[:, k * T:(k + 1) * T], self.xb[k][:])
        p.barrier()
        self.emit_ln2_affine(self.xb, self.bcolf[:, 48:56], self.bcolf[:, 56:64])
        if DBG:
            self.dbg_dump(self.xb[0][:, 0:256], 256, cast=True, name="h2_0")
        self.gather_lastcols(self.xb)
        for k in range(NK):
            self.shift_mix(self.hslot(k), self.xb, k,
                           self.bcolf[:, 72 + k:73 + k], g0[:, 72 + k:73 + k])
        self.load_weight("l3_wrc", D, D, self.wbf, 0)
        for o in range(NK):
            for s in range(NSUB):
                sl = slice(s * TS, (s + 1) * TS)
                for k in range(NK):
                    self.mm(self.ps[s % 4][:, :],
                            self.wbf[:, k * D + o * 128:k * D + (o + 1) * 128],
                            self.hslot(k)[:, sl], k == 0, k == NK - 1)
            p.barrier()
            for s in range(NSUB):
                self.act(slab[:, s * TS:(s + 1) * TS], self.ps[s % 4][:, :],
                         AF.Sigmoid)
            p.barrier()
            if DBG and o == 0:
                self.dbg_dump(slab[:, 0:256], 256, cast=True, name="rc0")
            p.dma("sync", self.i_v[:, o * T:(o + 1) * T], slab[:])
            p.barrier()
        for k in range(NK):
            self.shift_mix(self.hslot(k), self.xb, k,
                           self.bcolf[:, 64 + k:65 + k], g0[:, 72 + k:73 + k])
        if DBG:
            self.dbg_dump(self.hslot(0)[:, 0:256], 256, cast=True, name="xk2_0")
        for quarter in range(4):
            self.load_weight(self.inp["l3_wkc"][:, quarter * 1024:(quarter + 1) * 1024],
                             D, 1024, self.wbf, 0)
            self.load_weight(self.inp["l3_wvc"][quarter * 1024:(quarter + 1) * 1024, :],
                             1024, 1024, self.wbf, 8192)
            for ho in range(8):
                for s in range(NSUB):
                    sl = slice(s * TS, (s + 1) * TS)
                    for k in range(NK):
                        self.mm(self.ps[s % 4][:, :],
                                self.wbf[:, k * 1024 + ho * 128:
                                          k * 1024 + (ho + 1) * 128],
                                self.hslot(k)[:, sl], k == 0, k == NK - 1)
                p.barrier()
                for s in range(NSUB):
                    sl = slice(s * TS, (s + 1) * TS)
                    self.tss("vector", g1[:, 0:512], self.ps[s % 4][:, :],
                             0.0, ALU.max)
                    p.barrier()
                    self.tt("vector", self.xb[ho][:, sl], g1[:, 0:512],
                            g1[:, 0:512], ALU.mult)
                    p.barrier()
                if DBG and quarter == 0 and ho == 0:
                    self.dbg_dump(self.xb[0][:, 0:256], 256, cast=True,
                                  name="kc2_0")
            for o in range(NK):
                p.dma("sync", slab[:], self.i_v[:, o * T:(o + 1) * T])
                if quarter == 0:
                    p.dma("gpsimd", g2[:], self.i_o[:, o * T:(o + 1) * T])
                p.barrier()
                for s in range(NSUB):
                    sl = slice(s * TS, (s + 1) * TS)
                    for hc in range(8):
                        self.mm(self.ps[s % 4][:, :],
                                self.wbf[:, 8192 + hc * 1024 + o * 128:
                                          8192 + hc * 1024 + (o + 1) * 128],
                                self.xb[hc][:, sl], hc == 0, hc == 7)
                p.barrier()
                for s in range(NSUB):
                    sl = slice(s * TS, (s + 1) * TS)
                    self.tt("vector", g1[:, 0:512], self.ps[s % 4][:, :],
                            slab[:, sl], ALU.mult)
                    p.barrier()
                    if quarter == 0:
                        self.tt("vector", g1[:, 0:512], g1[:, 0:512],
                                g2[:, sl], ALU.add)
                        p.barrier()
                    self.tt("vector", self.xs[o][:, sl], self.xs[o][:, sl],
                            g1[:, 0:512], ALU.add)
                    p.barrier()
                if DBG and o == 0:
                    self.dbg_dump(self.xs[0][:, 0:256], 256,
                                  name="xsq%d" % quarter)


    def emit_conv4_l1(self, slot, prev3, f, tapoff, bias_col):
        p = self.p
        slab = self.resw[:, 512:2560]
        wc = lambda j: self.bcolf[:, tapoff + f * 4 + j:tapoff + f * 4 + j + 1]
        self.tss("vector", slab[:], slot[:], wc(3), ALU.mult)
        p.barrier()
        for j, off in ((2, 1), (1, 2), (0, 3)):
            self.stt("vector", slab[:, off:T], slot[:, 0:T - off], wc(j),
                     slab[:, off:T], ALU.mult, ALU.add)
            p.barrier()
        for t in range(3):
            for j in range(3 - t):
                self.stt("vector", slab[:, t:t + 1], prev3[:, t + j:t + j + 1],
                         wc(j), slab[:, t:t + 1], ALU.mult, ALU.add)
            p.barrier()
        self.tss("vector", slot[:], slab[:], bias_col, ALU.add)
        p.barrier()

    def emit_l1_attn(self):
        p = self.p
        g0, g1, g2 = self.gf[0], self.gf[1], self.gf[2]
        slab = self.resw[:, 512:2560]
        NC, C = 16, 128
        trilx = self.l1c[:, 0:128]
        triux = self.l1c[:, 128:256]
        bd16 = self.l1c[:, 386:514]
        id64 = self.cons[0:64, 128:192]
        id16 = self.cons[0:16, 128:144]
        p.dma("sync", self.l1c[:], self.inp["l1a_cons"])
        self.load_col(self.inp["l1a_cqw"], self.bcolf[:, 0:32])
        self.load_col(self.inp["l1a_ckw"], self.bcolf[:, 32:64])
        self.load_col(self.inp["l1a_cqb"], self.bcolf[:, 64:72])
        self.load_col(self.inp["l1a_ckb"], self.bcolf[:, 72:80])
        self.load_col(self.inp["l1a_lnw"], self.bcolf[:, 80:88])
        self.load_col(self.inp["l1a_lnb"], self.bcolf[:, 88:96])
        self.load_col(self.inp["l1a_bbcol"], self.bcolf[0:16, 96:97])
        self.dma_bcast_small(None, self.colf[:, 448:456], 8, 1,
                             src_dram=self.inp["comb3_prev"])
        self.load_col(self.inp["comb_alpha"], self.colf[0:8, 24:25])
        self.load_col(self.inp["comb_gamma"], self.colf[0:8, 25:26])
        self.load_col(self.inp["comb_a"], self.colf[0:8, 26:27])
        self.load_col(self.inp["comb2_prec"], self.colf[0:8, 30:31])
        p.barrier()
        self.dma_bcast_small(self.colf[0:8, 24:27], self.colf[:, 416:440], 8, 3)
        self.dma_bcast_small(self.colf[0:8, 30:31], self.colf[:, 440:448], 8, 1)
        bd2f = self.l1cf[:, 0:2]
        bd2tf = self.l1cf[0:2, 2:130]
        p.dma("sync", self.l1cf[:], self.inp["l1a_consf"])
        p.barrier()
        self.memset("vector", self.colf[:, 16:17], 1.0)
        p.barrier()
        for s in range(NSUB):
            sl = slice(s * TS, (s + 1) * TS)
            for k in range(NK):
                self.tt("vector", g0[:, 0:TS], self.xs[k][:, sl],
                        self.xs[k][:, sl], ALU.mult)
                p.barrier()
                self.mm(self.ps[0][0:1, 0:TS], self.colf[:, 16:17],
                        self.xs[k][:, sl], k == 0, k == NK - 1)
                self.mm(self.ps[1][0:1, 0:TS], self.colf[:, 16:17],
                        g0[:, 0:TS], k == 0, k == NK - 1)
                p.barrier()
            self.cp("vector", g0[0:1, 0:TS], self.ps[0][0:1, 0:TS])
            self.cp("scalar", g0[0:1, TS:2 * TS], self.ps[1][0:1, 0:TS])
            p.barrier()
            self.mm(self.ps[2][:, 0:TS], self.onesf_row[0:1, :], g0[0:1, 0:TS],
                    True, True)
            self.mm(self.ps[3][:, 0:TS], self.onesf_row[0:1, :], g0[0:1, TS:2 * TS],
                    True, True)
            p.barrier()
            self.act(g1[:, sl], self.ps[2][:, 0:TS], AF.Copy, scale=1.0 / D)
            p.barrier()
            self.ts2("vector", g2[:, sl], self.ps[3][:, 0:TS], 1.0 / D,
                     LN_EPS, ALU.mult, ALU.add)
            p.barrier()
        self.stt("vector", g0[:], g1[:], -1.0, g1[:], ALU.mult, ALU.mult)
        p.barrier()
        self.tt("vector", g2[:], g2[:], g0[:], ALU.add)
        p.barrier()
        self.recip(g2[:], g2[:])
        p.barrier()
        self.act(g2[:], g2[:], AF.Sqrt)
        p.barrier()
        for k in range(NK):
            self.tt("vector", g0[:], self.xs[k][:], g1[:], ALU.subtract)
            p.barrier()
            self.tt("vector", g0[:], g0[:], g2[:], ALU.mult)
            p.barrier()
            self.cp("vector", self.xb[k][:], g0[:])
            p.barrier()
            p.dma("sync", self.i_xf32[:, k * T:(k + 1) * T], g0[:])
            p.barrier()
        for s8 in range(8):
            s256 = s8 * 256
            for k in range(NK):
                p.dma("sync", g0[:, 0:256],
                      self.i_xf32[:, k * T + s256:k * T + s256 + 256])
                p.dma("gpsimd", g1[:, 0:16],
                      self.inp["l1a_wb"][k * 128:(k + 1) * 128, :])
                p.barrier()
                self.mm(self.ps[0][0:16, 0:256], g1[:, 0:16], g0[:, 0:256],
                        k == 0, k == NK - 1)
                p.barrier()
            self.p.op("scalar", lambda q, s256=s256: q.activation(
                g2[0:16, s256:s256 + 256], self.ps[0][0:16, 0:256],
                AF.Sigmoid, bias=self.bcolf[0:16, 96:97]))
            p.barrier()
        p.dma("sync", self.i_bf32[:, :], g2[0:16, 0:T])
        self.cp("vector", self.wbf[0:16, 14336:16384], g2[0:16, 0:T])
        p.barrier()
        for wname, dstf, dstb, tapoff, cb, isk in (
                ("l1a_wq", self.i_qf32, self.i_z, 0, 64, False),
                ("l1a_wk", self.i_kf32, self.i_k, 32, 72, True)):
            for s8 in range(8):
                s256 = s8 * 256
                for half in range(2):
                    ob = half * 4
                    for k in range(NK):
                        p.dma("sync", g0[:, 0:256],
                              self.i_xf32[:, k * T + s256:k * T + s256 + 256])
                        p.dma("gpsimd", g1[:, 0:512],
                              self.inp[wname][k * 128:(k + 1) * 128,
                                              ob * 128:ob * 128 + 512])
                        p.barrier()
                        for oo in range(4):
                            self.mm(self.ps[oo][:, 0:256],
                                    g1[:, oo * 128:(oo + 1) * 128],
                                    g0[:, 0:256], k == 0, k == NK - 1)
                        p.barrier()
                    for oo in range(4):
                        self.cp("vector", g2[:, oo * 256:(oo + 1) * 256],
                                self.ps[oo][:, 0:256])
                    p.barrier()
                    for oo in range(4):
                        o = ob + oo
                        p.dma("sync",
                              dstf[:, o * T + s256:o * T + s256 + 256],
                              g2[:, oo * 256:(oo + 1) * 256])
                    p.barrier()
            for f in range(NK):
                p.dma("sync", g0[:, f * 3:(f + 1) * 3],
                      dstf[:, f * T + T - 3:f * T + T])
            p.barrier()
            self._mamba_gather_prev("l1f" + wname)
            wc = lambda j, f: self.bcolf[:, tapoff + f * 4 + j:
                                         tapoff + f * 4 + j + 1]
            for f in range(NK):
                p.dma("sync", g1[:, :], dstf[:, f * T:(f + 1) * T])
                p.barrier()
                self.tss("vector", g2[:], g1[:], wc(3, f), ALU.mult)
                p.barrier()
                for j, off in ((2, 1), (1, 2), (0, 3)):
                    self.stt("vector", g2[:, off:T], g1[:, 0:T - off],
                             wc(j, f), g2[:, off:T], ALU.mult, ALU.add)
                    p.barrier()
                for t in range(3):
                    for j in range(3 - t):
                        self.stt("vector", g2[:, t:t + 1],
                                 g0[:, f * 3 + t + j:f * 3 + t + j + 1],
                                 wc(j, f), g2[:, t:t + 1], ALU.mult, ALU.add)
                    p.barrier()
                self.tss("vector", g2[:], g2[:],
                         self.bcolf[:, cb + f:cb + f + 1], ALU.add)
                p.barrier()
                if isk:
                    self.tt("vector", g1[:], g2[:], g2[:], ALU.mult)
                    p.barrier()
                    for s in range(NSUB):
                        sl = slice(s * TS, (s + 1) * TS)
                        self.mm(self.ps[4][0:2, 0:TS], bd2f, g1[:, sl],
                                True, True)
                        p.barrier()
                        self.act(g0[0:2, 32:32 + TS], self.ps[4][0:2, 0:TS],
                                 AF.Sqrt)
                        p.barrier()
                        self.tss("vector", g0[0:2, 32:32 + TS],
                                 g0[0:2, 32:32 + TS], 1e-12, ALU.max)
                        p.barrier()
                        self.recip(g0[0:2, 32:32 + TS], g0[0:2, 32:32 + TS])
                        p.barrier()
                        self.mm(self.ps[5][:, 0:TS], bd2tf,
                                g0[0:2, 32:32 + TS], True, True)
                        p.barrier()
                        self.tt("vector", g2[:, sl], g2[:, sl],
                                self.ps[5][:, 0:TS], ALU.mult)
                        p.barrier()
                if DBG and f == 0:
                    self.dbg_dump(g2[:, 0:256], 256,
                                  name="knorm0" if isk else "qconv0")
                p.dma("sync", dstf[:, f * T:(f + 1) * T], g2[:])
                p.barrier()
                self.cp("vector", slab[:], g2[:])
                p.barrier()
                p.dma("sync", dstb[:, f * T:(f + 1) * T], slab[:])
                p.barrier()
        self.load_weight("l1a_wv", D, D, self.wbf, 0)
        for o in range(NK):
            for s in range(NSUB):
                sl = slice(s * TS, (s + 1) * TS)
                for k in range(NK):
                    self.mm(self.ps[s % 4][:, :],
                            self.wbf[:, k * D + o * 128:k * D + (o + 1) * 128],
                            self.xb[k][:, sl], k == 0, k == NK - 1)
            p.barrier()
            for s in range(NSUB):
                sl = slice(s * TS, (s + 1) * TS)
                self.cp("scalar", slab[:, sl], self.ps[s % 4][:, :])
            p.barrier()
            p.dma("sync", self.i_v[:, o * T:(o + 1) * T], slab[:])
            p.barrier()
        if L1CUT == 1:
            return
        W = self.wbf
        kch = lambda h: W[0:64, h * C:(h + 1) * C]
        qch = lambda h: W[0:64, 2048 + h * C:2048 + (h + 1) * C]
        vch = lambda h: W[0:64, 4096 + h * C:4096 + (h + 1) * C]
        S0bf = lambda h: W[0:64, 6144 + h * 64:6208 + h * 64]
        zbf = lambda h: W[0:64, 7168 + h:7169 + h]
        RT = lambda h: W[0:64, 7184 + h * 64:7248 + h * 64]
        bcb = lambda h: W[:, 8208 + h:8209 + h]
        Pt, PTt = W[:, 8224:8352], W[:, 8352:8480]
        Dsb, GTsb = W[:, 8480:8608], W[:, 8608:8736]
        Ktm, Vtm = W[:, 8736:8800], W[:, 8800:8864]
        gw = W[0:64, 8864:8992]
        Qt = W[0:64, 8992:9120]
        WtK = W[0:64, 9120:9184]
        dscr = W[:, 9184:9248]
        Xo = W[:, 13344:13472]
        Y2sb = W[:, 13472:13600]
        Y4sb = W[:, 13600:13728]
        ych = lambda h: W[0:64, 9248 + h * C:9248 + (h + 1) * C]
        qtch = lambda h: W[0:64, 11296 + h * C:11296 + (h + 1) * C]
        S0f = lambda h: g1[0:64, h * 64:(h + 1) * 64]
        zf = lambda h: g1[0:64, 1024 + h:1025 + h]
        bcf = lambda h: g1[:, 1040 + h:1041 + h]
        bcn = lambda h: g1[:, 1056 + h:1057 + h]
        T1 = self.ps[0][:, 0:128]
        M1 = self.ps[1][:, 0:128]
        M2 = self.ps[2][:, 0:128]
        M3 = self.ps[3]
        M4 = self.ps[4][:, 0:128]
        DEN = self.ps[5][0:1, 0:128]
        if DBG:
            self.dbg_dump(W[0:16, 14336:14592], 256, cast=True, name="beta0")
        self.memset("vector", g1[0:64, 0:1040], 0.0)
        p.barrier()
        self.cp("vector", W[0:64, 6144:7184], g1[0:64, 0:1040])
        for h in range(NC):
            self.cp("gpsimd", RT(h), id64)
        p.barrier()
        for c in range(NC):
            for h in range(NC):
                b, f = 64 * (h % 2), h // 2
                p.dma("sync", kch(h),
                      self.i_k[b:b + 64, f * T + c * C:f * T + (c + 1) * C])
                p.dma("gpsimd", qch(h),
                      self.i_z[b:b + 64, f * T + c * C:f * T + (c + 1) * C])
                p.dma("sync", vch(h),
                      self.i_v[b:b + 64, f * T + c * C:f * T + (c + 1) * C])
            p.barrier()
            self.transpose(self.psb[:, 256:272],
                           W[0:16, 14336 + c * C:14336 + (c + 1) * C], id16)
            p.barrier()
            self.cp("vector", g1[:, 1040:1056], self.psb[:, 256:272])
            p.barrier()
            self.ts2("vector", g1[:, 1056:1072], g1[:, 1040:1056], -1.0, 0.0,
                     ALU.mult, ALU.add)
            self.cp("gpsimd", W[:, 8208:8224], g1[:, 1040:1056])
            p.barrier()
            for h in range(NC):
                self.transpose(self.psb[:, 0:64], kch(h), id64)
                self.transpose(self.psb[:, 64:128], vch(h), id64)
                p.barrier()
                self.cp("vector", Ktm[:], self.psb[:, 0:64])
                self.cp("scalar", Vtm[:], self.psb[:, 64:128])
                p.barrier()
                self.mm(M3[:, 0:64], kch(h), S0bf(h), True, True)
                self.mm(M1, kch(h), kch(h), True, True)
                p.barrier()
                self.tt("vector", dscr[:], Vtm[:], M3[:, 0:64], ALU.subtract)
                self.stt("vector", Pt[:], M1, bcn(h), trilx, ALU.mult,
                         ALU.mult)
                p.barrier()
                self.tss("vector", Dsb[:, 0:64], dscr[:], bcf(h), ALU.mult)
                self.tss("gpsimd", Dsb[:, 64:128], Ktm[:], bcf(h), ALU.mult)
                if DBG and c == 0 and h == 0:
                    self.dbg_dump(Pt[:], 128, cast=True, name="X0")
                    self.dbg_dump(Dsb[:], 128, cast=True, name="D0")
                self.tt("vector", Xo[:], Pt[:], bd16, ALU.mult)
                p.barrier()
                self.tt("vector", Xo[:], Pt[:], Xo[:], ALU.subtract)
                self.tt("gpsimd", Pt[:], Pt[:], bd16, ALU.mult)
                p.barrier()
                self.transpose(self.psb[:, 128:256], Pt, self.ident_bf)
                p.barrier()
                self.cp("vector", PTt[:], self.psb[:, 128:256])
                p.barrier()
                for l in range(4):
                    self.mm(M1, PTt, Dsb, True, True)
                    self.mm(M2, PTt, Xo, True, True)
                    p.barrier()
                    self.tt("vector", Dsb[:], Dsb[:], M1, ALU.add)
                    self.tt("vector", Xo[:], Xo[:], M2, ALU.add)
                    p.barrier()
                    if l < 3:
                        self.mm(M1, PTt, Pt, True, True)
                        p.barrier()
                        self.cp("vector", Pt[:], M1)
                        p.barrier()
                        self.transpose(self.psb[:, 128:256], Pt,
                                       self.ident_bf)
                        p.barrier()
                        self.cp("vector", PTt[:], self.psb[:, 128:256])
                        p.barrier()
                self.transpose(self.psb[:, 128:256], Xo, self.ident_bf)
                p.barrier()
                self.cp("vector", PTt[:], self.psb[:, 128:256])
                p.barrier()
                self.mm(M1, PTt, Dsb, True, True)
                self.mm(M2, PTt, Xo, True, True)
                p.barrier()
                self.tt("vector", Dsb[:], Dsb[:], M1, ALU.add)
                self.cp("scalar", Y2sb[:], M2)
                p.barrier()
                self.transpose(self.psb[:, 128:256], Y2sb, self.ident_bf)
                p.barrier()
                self.cp("vector", PTt[:], self.psb[:, 128:256])
                p.barrier()
                self.mm(M1, PTt, Dsb, True, True)
                self.mm(M2, PTt, Y2sb, True, True)
                p.barrier()
                self.tt("vector", Dsb[:], Dsb[:], M1, ALU.add)
                self.cp("scalar", Y4sb[:], M2)
                p.barrier()
                self.transpose(self.psb[:, 128:256], Y4sb, self.ident_bf)
                p.barrier()
                self.cp("vector", PTt[:], self.psb[:, 128:256])
                p.barrier()
                self.mm(M1, PTt, Dsb, True, True)
                p.barrier()
                self.tt("vector", Dsb[:], Dsb[:], M1, ALU.add)
                p.barrier()
                if DBG and c == 0 and h == 0:
                    self.dbg_dump(Dsb[:], 128, cast=True, name="UW0")
                self.mm(M1, kch(h), qch(h), True, True)
                p.barrier()
                self.tt("vector", GTsb[:], M1, triux, ALU.mult)
                p.barrier()
                self.mm(M2, Dsb, GTsb, True, True)
                self.mm(M4[0:64, :], S0bf(h), qch(h), True, True)
                p.barrier()
                self.cp("vector", ych(h), M2[0:64, :])
                self.cp("scalar", gw[:], M2[64:128, :])
                p.barrier()
                self.tt("vector", ych(h), ych(h), M4[0:64, :], ALU.add)
                p.barrier()
                if DBG and c == 0 and h == 0:
                    self.dbg_dump(GTsb[:], 128, cast=True, name="GT0")
                    self.dbg_dump(ych(0), 128, cast=True, name="yloc0")
                self.tt("vector", Qt[:], qch(h), gw[:], ALU.subtract)
                p.barrier()
                if c == 0:
                    self.cp("vector", qtch(h), Qt[:])
                else:
                    self.mm(M4[0:64, :], RT(h), Qt[:], True, True)
                    p.barrier()
                    self.cp("vector", qtch(h), M4[0:64, :])
                p.barrier()
                self.mm(DEN, bcb(h), GTsb, True, False)
                self.mm(DEN, zbf(h), qch(h), False, True)
                p.barrier()
                self.cp("vector", g2[0:1, h * C:(h + 1) * C], DEN)
                p.barrier()
                self.mm(M3[0:64, 64:128], Ktm, Dsb[:, 0:64], True, True)
                self.mm(M3[0:64, 384:385], Ktm, bcb(h), True, True)
                p.barrier()
                self.tt("vector", S0f(h), S0f(h), M3[0:64, 64:128], ALU.add)
                self.tt("vector", zf(h), zf(h), M3[0:64, 384:385], ALU.add)
                p.barrier()
                self.cp("vector", S0bf(h), S0f(h))
                self.cp("gpsimd", zbf(h), zf(h))
                p.barrier()
                if DBG and c == 0 and h == 0:
                    self.dbg_dump(g2[0:1, 0:128], 128, name="den0")
                if DBG and c == 1 and h == 0:
                    self.dbg_dump(S0bf(0), 64, cast=True, name="S0c1")
                self.mm(M4[0:64, 0:64], Dsb[:, 64:128], Ktm, True, True)
                p.barrier()
                self.cp("vector", WtK[:], M4[0:64, 0:64])
                p.barrier()
                self.mm(M4[0:64, 64:128], WtK, RT(h), True, True)
                p.barrier()
                self.tt("vector", RT(h), RT(h), M4[0:64, 64:128],
                        ALU.subtract)
                p.barrier()
            for h in range(NC):
                p.dma("sync", self.i_o[0:64, h * T + c * C:h * T + (c + 1) * C],
                      ych(h))
                p.dma("gpsimd",
                      self.i_o[64:128, h * T + c * C:h * T + (c + 1) * C],
                      qtch(h))
            self.cp("vector", self.resw[0:1, 0:T], g2[0:1, 0:T])
            p.barrier()
            p.dma("sync", self.i_xc[0:1, c * T:(c + 1) * T],
                  self.resw[0:1, 0:T])
            p.barrier()
        if L1CUT == 2:
            return
        for h in range(NC):
            self.transpose(self.psb[0:64, h * 64:(h + 1) * 64], RT(h), id64)
        p.barrier()
        for h in range(NC):
            self.cp("vector", g2[0:64, h * 64:(h + 1) * 64],
                    self.psb[0:64, h * 64:(h + 1) * 64])
        p.barrier()
        p.dma("sync", self.ag_in[0:64, 0:1024], g1[0:64, 0:1024])
        p.dma("sync", self.ag_in[64:128, 0:1024], g2[0:64, 0:1024])
        p.dma("sync", self.ag_in[0:64, 1024:1032], g1[0:64, 1024:1032])
        p.dma("sync", self.ag_in[64:128, 1024:1032], g1[0:64, 1032:1040])
        p.barrier()
        self.p.op("gpsimd", lambda q: q.collective_compute(
            "AllGather", ALU.bypass, replica_groups=[[0, 1, 2, 3, 4, 5, 6, 7]],
            ins=[self.ag_in], outs=[self.ag_out]))
        p.barrier()
        self.memset("vector", g1[0:64, 0:1024], 0.0)
        self.memset("gpsimd", g1[0:64, 1136:1152], 0.0)
        p.barrier()
        self.cp("vector", W[0:64, 6144:7168], g1[0:64, 0:1024])
        p.barrier()
        scrf = g1[0:64, 1072:1136]
        scrb = W[0:64, 0:64]
        for r in range(8):
            p.dma("sync", g2[0:64, 0:1024],
                  self.ag_out[r * 128:r * 128 + 64, 0:1024])
            p.dma("sync", g2[0:64, 1024:2048],
                  self.ag_out[r * 128 + 64:r * 128 + 128, 0:1024])
            p.dma("gpsimd", g1[0:64, 1152:1160],
                  self.ag_out[r * 128:r * 128 + 64, 1024:1032])
            p.dma("gpsimd", g1[0:64, 1160:1168],
                  self.ag_out[r * 128 + 64:r * 128 + 128, 1024:1032])
            p.barrier()
            for h in range(NC):
                self.tss("vector", scrf, g2[0:64, 1024 + h * 64:1088 + h * 64],
                         self.colf[0:64, 416 + r * 3:417 + r * 3], ALU.mult)
                p.barrier()
                self.stt("vector", scrf, id64,
                         self.colf[0:64, 417 + r * 3:418 + r * 3], scrf,
                         ALU.mult, ALU.add)
                p.barrier()
                self.cp("vector", scrb, scrf)
                p.barrier()
                self.mm(M4[0:64, 0:64], scrb, S0bf(h), True, True)
                p.barrier()
                self.stt("vector", g1[0:64, h * 64:(h + 1) * 64],
                         g2[0:64, h * 64:(h + 1) * 64],
                         self.colf[0:64, 418 + r * 3:419 + r * 3],
                         M4[0:64, 0:64], ALU.mult, ALU.add)
                p.barrier()
                self.cp("vector", S0bf(h), g1[0:64, h * 64:(h + 1) * 64])
                p.barrier()
            self.stt("vector", g1[0:64, 1136:1152], g1[0:64, 1152:1168],
                     self.colf[0:64, 440 + r:441 + r], g1[0:64, 1136:1152],
                     ALU.mult, ALU.add)
            p.barrier()
        self.cp("vector", W[0:64, 7168:7184], g1[0:64, 1136:1152])
        p.barrier()
        if L1CUT == 3:
            return
        qch2 = lambda h: W[0:64, h * C:(h + 1) * C]
        ylch = lambda h: W[0:64, 2048 + h * C:2048 + (h + 1) * C]
        qtld = lambda h: W[0:64, 4096 + h * C:4096 + (h + 1) * C]
        numb = W[0:64, 9184:9312]
        denb = W[0:1, 9312:9440]
        yout = W[0:64, 13728:13856]
        for c in range(NC):
            for h in range(NC):
                b, f = 64 * (h % 2), h // 2
                p.dma("sync", qch2(h),
                      self.i_z[b:b + 64, f * T + c * C:f * T + (c + 1) * C])
                p.dma("gpsimd", ylch(h),
                      self.i_o[0:64, h * T + c * C:h * T + (c + 1) * C])
                p.dma("sync", qtld(h),
                      self.i_o[64:128, h * T + c * C:h * T + (c + 1) * C])
            p.barrier()
            for h in range(NC):
                b, f = 64 * (h % 2), h // 2
                self.mm(M4[0:64, :], S0bf(h), qtld(h), True, True)
                p.barrier()
                self.tt("vector", yout[:], ylch(h), M4[0:64, :], ALU.add)
                p.barrier()
                self.cp("scalar", self.xb[f][b:b + 64, c * C:(c + 1) * C],
                        yout[:])
                p.barrier()
        for f in range(NK):
            p.dma("sync", g0[:], self.i_kf32[:, f * T:(f + 1) * T])
            p.dma("gpsimd", g2[0:2, 0:T], self.i_bf32[2 * f:2 * f + 2, :])
            p.barrier()
            for s in range(NSUB):
                sl = slice(s * TS, (s + 1) * TS)
                self.mm(self.ps[4][:, 0:TS], bd2tf, g2[0:2, sl], True, True)
                p.barrier()
                self.tt("vector", g1[:, sl], g0[:, sl], self.ps[4][:, 0:TS],
                        ALU.mult)
                p.barrier()
            self.reduce("vector", self.colf[:, f:f + 1], g1[:], ALU.add)
            p.barrier()
            p.dma("sync", self.i_bk[:, f * T:(f + 1) * T], g1[:])
            p.barrier()
        p.dma("sync", self.ag4_in[:, 0:8], self.colf[:, 0:8])
        p.barrier()
        self.p.op("gpsimd", lambda q: q.collective_compute(
            "AllGather", ALU.bypass, replica_groups=[[0, 1, 2, 3, 4, 5, 6, 7]],
            ins=[self.ag4_in], outs=[self.ag4_out]))
        p.barrier()
        for r in range(8):
            p.dma("sync", g0[:, 32 + r * 8:40 + r * 8],
                  self.ag4_out[r * 128:(r + 1) * 128, 0:8])
        p.barrier()
        self.memset("vector", self.colf[:, 8:16], 0.0)
        p.barrier()
        for r in range(8):
            self.stt("vector", self.colf[:, 8:16], g0[:, 32 + r * 8:40 + r * 8],
                     self.colf[:, 440 + r:441 + r], self.colf[:, 8:16],
                     ALU.mult, ALU.add)
            p.barrier()
        for f in range(NK):
            p.dma("sync", g1[:], self.i_bk[:, f * T:(f + 1) * T])
            p.dma("gpsimd", g2[:], self.i_qf32[:, f * T:(f + 1) * T])
            p.barrier()
            self.scan("vector", g0[:], g1[:], g1[:],
                      self.colf[:, 8 + f:9 + f], ALU.add, ALU.bypass)
            p.barrier()
            self.tt("vector", g0[:], g0[:], g1[:], ALU.subtract)
            p.barrier()
            self.tt("vector", g0[:], g0[:], g2[:], ALU.mult)
            p.barrier()
            for s in range(NSUB):
                sl = slice(s * TS, (s + 1) * TS)
                self.mm(self.ps[4][0:2, 0:TS], bd2f, g0[:, sl], True, True)
                p.barrier()
                self.cp("vector", g1[0:2, 0:TS], self.ps[4][0:2, 0:TS])
                p.barrier()
                self.tss("vector", g1[0:2, 0:TS], g1[0:2, 0:TS], 1e-06,
                         ALU.add)
                p.barrier()
                self.recip(g1[0:2, 0:TS], g1[0:2, 0:TS])
                p.barrier()
                self.mm(self.ps[5][:, 0:TS], bd2tf, g1[0:2, 0:TS], True, True)
                p.barrier()
                self.tt("vector", self.xb[f][:, sl], self.xb[f][:, sl],
                        self.ps[5][:, 0:TS], ALU.mult)
                p.barrier()
        if DBG:
            self.dbg_dump(self.xb[0][:, 0:256], 256, cast=True, name="yfin0")
            self.dbg_dump(self.xb[0][:, 1024:1280], 256, cast=True,
                          name="yfin8")
        if L1CUT == 4:
            return
        self.load_weight("l1a_wo", D, D, self.wbf, 0)
        if L1CUT == 51:
            return
        for o in range(NK):
            for s in range(NSUB):
                sl = slice(s * TS, (s + 1) * TS)
                for k in range(NK):
                    self.mm(self.ps[s % 4][:, :],
                            self.wbf[:, k * D + o * 128:k * D + (o + 1) * 128],
                            self.xb[k][:, sl], k == 0, k == NK - 1)
            p.barrier()
            for s in range(NSUB):
                sl = slice(s * TS, (s + 1) * TS)
                self.cp("vector", slab[:, sl], self.ps[s % 4][:, :])
            p.barrier()
            p.dma("sync", self.i_o[:, o * T:(o + 1) * T], slab[:])
            p.barrier()
        for k in range(NK):
            p.dma("sync", self.xb[k][:], self.i_o[:, k * T:(k + 1) * T])
        p.barrier()
        if L1CUT == 5:
            return
        self.emit_ln2_affine(self.xb, self.bcolf[:, 80:88],
                             self.bcolf[:, 88:96])
        if L1CUT == 6:
            return
        for k in range(NK):
            self.tt(("vector", "gpsimd")[k % 2], self.xs[k][:], self.xs[k][:],
                    self.xb[k][:], ALU.add)
        p.barrier()

    def store_out(self):
        p = self.p
        for k in range(NK):
            p.dma("sync", self.out[k * 128:(k + 1) * 128, :], self.xs[k][:])
        p.barrier()

    def _emit(self):
        self.load_consts()
        ffn0_done = [False]

        def ffn0():
            self.emit_ffn0(first=not ffn0_done[0])
            ffn0_done[0] = True
        emitters = {0: ffn0, 2: ffn0, 1: self.emit_l0_attn,
                    3: self.emit_moe, 5: self.emit_moe,
                    6: self.emit_ffn2, 8: self.emit_ffn2,
                    9: self.emit_ffn3, 11: self.emit_ffn3,
                    7: self.emit_mamba, 10: self.emit_l3_attn,
                    4: self.emit_l1_attn}
        for i in self.blocks:
            emitters[i]()
        self.dbg_xs()
        self.store_out()


_BUILDER_CACHE = {}


def _get_builder(blocks):
    key = tuple(blocks)
    if key not in _BUILDER_CACHE:
        _BUILDER_CACHE[key] = Builder(blocks)
    return _BUILDER_CACHE[key]


def _consts():
    import ml_dtypes
    bf = ml_dtypes.bfloat16
    c = {}
    c["ones_bf"] = np.ones((128, 128), dtype=bf)
    c["ones_row_bf"] = np.ones((1, 512), dtype=bf)
    c["ident_bf"] = np.eye(128, dtype=np.float32).astype(bf)
    c["ident_f32"] = np.eye(128, dtype=np.float32)
    s, t = np.meshgrid(np.arange(128), np.arange(128), indexing="ij")
    c["triu_incl_bf"] = (s <= t).astype(np.float32).astype(bf)
    sel = np.zeros((64, 2048), np.float32)
    for v in range(16):
        row = v if v < 8 else 32 + (v - 8)
        sel[row, v * 128:(v + 1) * 128] = 1.0
    c["sel64_bf"] = sel.astype(bf)
    lc = np.zeros((128, 514), dtype=np.float32)
    for _i in range(8):
        lc[_i * 16:(_i + 1) * 16, 386 + _i * 16:386 + (_i + 1) * 16] = 1.0
    lc[:, 0:128] = np.tril(np.ones((128, 128)), -1)
    lc[:, 128:256] = np.triu(np.ones((128, 128)), 1)
    lc[0:64, 256] = 1.0
    lc[64:128, 257] = 1.0
    lc[0, 258:322] = 1.0
    lc[1, 322:386] = 1.0
    c["l1a_cons"] = lc.astype(bf)
    lf = np.zeros((128, 130), dtype=np.float32)
    lf[0:64, 0] = 1.0
    lf[64:128, 1] = 1.0
    lf[0, 2:66] = 1.0
    lf[1, 66:130] = 1.0
    c["l1a_consf"] = lf
    return c


def _rope_tables():
    half = 64
    inv_freq = 1.0 / (10000.0 ** (np.arange(half, dtype=np.float64) / half))
    tpos = np.arange(8192, dtype=np.float64)
    fr = tpos[:, None] * inv_freq[None, :]
    return (np.cos(fr).astype(np.float32).T.copy(),
            np.sin(fr).astype(np.float32).T.copy())


def _rope_perm():
    perm = []
    for h in range(8):
        base = h * 128
        perm += [base + 2 * i for i in range(64)]
        perm += [base + 2 * i + 1 for i in range(64)]
    return np.array(perm)


def prepare_in_maps(inputs):
    x = np.ascontiguousarray(inputs["L0_attn_x"])
    consts = _consts()
    cosT, sinT = _rope_tables()
    perm = _rope_perm()
    f32 = lambda a: np.ascontiguousarray(a, dtype=np.float32)
    shared = dict(consts)
    shared["l0_kernel"] = f32(inputs["L0_ffn_kernel"])
    shared["l0_bias_row"] = f32(inputs["L0_ffn_bias"])[None, :]
    shared["l0_a"] = f32(inputs["L0_ffn_a"])
    shared["l0_b"] = f32(inputs["L0_ffn_b"])
    shared["l0_wq"] = f32(inputs["L0_attn_Wq"].T[:, perm])
    shared["l0_wk"] = f32(inputs["L0_attn_Wk"].T[:, perm])
    shared["l0_wv"] = f32(inputs["L0_attn_Wv"].T)
    shared["l0_wo"] = f32(inputs["L0_attn_Wo"].T)
    shared["l0_bo_row"] = f32(inputs["L0_attn_bo"])[None, :]
    shared["l1_gate"] = f32(inputs["L1_ffn_w_gate_w"])
    shared["l1_gateb_row"] = f32(inputs["L1_ffn_w_gate_b"])[None, :]
    shared["l1_W1"] = f32(inputs["L1_ffn_W1"].transpose(1, 0, 2).reshape(1024, 2048))
    shared["l1_b1cols"] = f32(np.ascontiguousarray(
        inputs["L1_ffn_b1"].reshape(2048)[None, :].reshape(16, 128).T))
    shared["l1_W2"] = f32(inputs["L1_ffn_W2"].reshape(2048, 1024))
    shared["l1_b2"] = f32(inputs["L1_ffn_b2"])
    shared["l2_W1"] = f32(inputs["L2_ffn_W1"])
    shared["l2_b1row"] = f32(inputs["L2_ffn_b1"])[None, :]
    shared["l2_theta"] = f32(np.ascontiguousarray(inputs["L2_ffn_quantum_phases"].reshape(32, 128).T))
    shared["l2_wr"] = f32(np.ascontiguousarray(inputs["L2_ffn_quantum_weights_real"].reshape(32, 128).T))
    shared["l2_wi"] = f32(np.ascontiguousarray(inputs["L2_ffn_quantum_weights_imag"].reshape(32, 128).T))
    shared["l2_W2"] = f32(inputs["L2_ffn_W2"])
    shared["l2_b2row"] = f32(inputs["L2_ffn_b2"])[None, :]
    shared["l3f_wr"] = f32(inputs["L3_ffn_W_r"].T)
    shared["l3f_wk"] = f32(inputs["L3_ffn_W_k"].T)
    shared["l3f_wv"] = f32(inputs["L3_ffn_W_v"].T)
    shared["l3f_mur"] = f32(np.ascontiguousarray(inputs["L3_ffn_mu_r"].reshape(8, 128).T))
    shared["l3f_muk"] = f32(np.ascontiguousarray(inputs["L3_ffn_mu_k"].reshape(8, 128).T))
    shared["l2a_lng"] = f32(np.ascontiguousarray(inputs["L2_attn_ln_g"].reshape(8, 128).T))
    shared["l2a_lnb"] = f32(np.ascontiguousarray(inputs["L2_attn_ln_b"].reshape(8, 128).T))
    shared["l2a_Win"] = f32(inputs["L2_attn_W_in"])
    shared["l2a_convw"] = f32(np.ascontiguousarray(
        inputs["L2_attn_conv_w"].reshape(16, 128, 4).transpose(1, 0, 2).reshape(128, 64)))
    shared["l2a_convb"] = f32(np.ascontiguousarray(
        inputs["L2_attn_conv_b"].reshape(16, 128).T))
    shared["l2a_Wx"] = f32(inputs["L2_attn_W_x"])
    shared["l2a_wdt"] = f32(np.ascontiguousarray(inputs["L2_attn_w_dt"].reshape(16, 128).T))
    shared["l2a_bdt"] = f32(np.ascontiguousarray(inputs["L2_attn_b_dt"].reshape(16, 128).T))
    shared["l2a_alog"] = f32(np.ascontiguousarray(
        inputs["L2_attn_A_log"].reshape(16, 128, 16).transpose(1, 0, 2).reshape(128, 256)))
    shared["l2a_dpar"] = f32(np.ascontiguousarray(inputs["L2_attn_D_param"].reshape(16, 128).T))
    shared["l2a_Wout"] = f32(inputs["L2_attn_W_out"])
    shared["ones_f32"] = np.ones((1, 128), np.float32)
    shared["l1a_wq"] = f32(inputs["L1_attn_Wq"].T)
    shared["l1a_wk"] = f32(inputs["L1_attn_Wk"].T)
    shared["l1a_wv"] = f32(inputs["L1_attn_Wv"].T)
    shared["l1a_wo"] = f32(inputs["L1_attn_Wo"].T)
    shared["l1a_wb"] = f32(inputs["L1_attn_Wb"].T)
    shared["l1a_bbcol"] = f32(np.asarray(inputs["L1_attn_bb"]).reshape(16, 1))
    tapf = lambda w: f32(np.ascontiguousarray(
        np.asarray(w).reshape(8, 128, 4).transpose(1, 0, 2).reshape(128, 32)))
    shared["l1a_cqw"] = tapf(inputs["L1_attn_conv_qw"])
    shared["l1a_ckw"] = tapf(inputs["L1_attn_conv_kw"])
    ccl = lambda v: f32(np.ascontiguousarray(np.asarray(v).reshape(8, 128).T))
    shared["l1a_cqb"] = ccl(inputs["L1_attn_conv_qb"])
    shared["l1a_ckb"] = ccl(inputs["L1_attn_conv_kb"])
    shared["l1a_lnw"] = ccl(inputs["L1_attn_ln_w"])
    shared["l1a_lnb"] = ccl(inputs["L1_attn_ln_b"])
    shared["l3_wkt"] = f32(inputs["L3_attn_Wk_t"].T)
    shared["l3_wvt"] = f32(inputs["L3_attn_Wv_t"].T)
    shared["l3_wrt"] = f32(inputs["L3_attn_Wr_t"].T)
    shared["l3_wkc"] = f32(inputs["L3_attn_Wk_c"].T)
    shared["l3_wvc"] = f32(inputs["L3_attn_Wv_c"].T)
    shared["l3_wrc"] = f32(inputs["L3_attn_Wr_c"].T)
    cc = lambda v: f32(np.ascontiguousarray(np.asarray(v).reshape(8, 128).T))
    shared["l3_ln1g"] = cc(inputs["L3_attn_ln1_g"])
    shared["l3_ln1b"] = cc(inputs["L3_attn_ln1_b"])
    shared["l3_ln2g"] = cc(inputs["L3_attn_ln2_g"])
    shared["l3_ln2b"] = cc(inputs["L3_attn_ln2_b"])
    shared["l3_mixk"] = cc(inputs["L3_attn_mix_k"])
    shared["l3_mixv"] = cc(inputs["L3_attn_mix_v"])
    shared["l3_mixr"] = cc(inputs["L3_attn_mix_r"])
    shared["l3_cmixk"] = cc(inputs["L3_attn_cmix_k"])
    shared["l3_cmixr"] = cc(inputs["L3_attn_cmix_r"])
    shared["l3_tf"] = cc(inputs["L3_attn_time_first"])
    shared["l0_wba"] = f32(np.concatenate(
        [inputs["L0_attn_Wbeta"].T, inputs["L0_attn_Walpha"].T], axis=1))
    shared["l0_bba"] = f32(np.concatenate(
        [inputs["L0_attn_bbeta"], inputs["L0_attn_balpha"]]))[:, None]

    maps = []
    for c in range(8):
        b, q = c // 4, c % 4
        m = dict(shared)
        bstart = 4 * b
        alpha = np.zeros((8, 1), np.float32)
        gamma = np.zeros((8, 1), np.float32)
        acol = np.zeros((8, 1), np.float32)
        for r in range(8):
            pred = (r >= bstart) and (r < c)
            acol[r] = 1.0 if pred else 0.0
            alpha[r] = 1.0 if (pred and r != bstart) else 0.0
            gamma[r] = 0.0 if pred else 1.0
        m["comb_alpha"] = alpha; m["comb_gamma"] = gamma; m["comb_a"] = acol
        m_all = np.zeros((8, 1), np.float32)
        m_prec = np.zeros((8, 1), np.float32)
        for r in range(8):
            if r // 4 == b:
                m_all[r] = 1.0
                if r < c:
                    m_prec[r] = 1.0
        m["comb2_all"] = m_all; m["comb2_prec"] = m_prec
        prev1 = np.zeros((8, 1), np.float32)
        if q > 0:
            prev1[c - 1] = 1.0
        m["comb3_prev"] = prev1
        m["comb2_thrbase"] = np.full((1, 1), 4096.0 - 2048.0 * float(m_prec.sum()), np.float32)
        xs = x[b, q * T:(q + 1) * T, :]
        m["x_fm"] = np.ascontiguousarray(xs.T)
        m["rope_cos"] = np.ascontiguousarray(cosT[:, q * T:(q + 1) * T])
        m["rope_sin"] = np.ascontiguousarray(sinT[:, q * T:(q + 1) * T])
        maps.append(m)
    return maps


def kernel(blocks=None, **inputs):
    blocks = list(range(N_SUB_BLOCKS)) if blocks is None else list(blocks)
    b = _get_builder(blocks)
    maps = prepare_in_maps(inputs)
    maps = [{k: m[k] for k in b.input_names} for m in maps]
    res = run_bass_kernel_spmd(b.nc, maps, core_ids=list(range(8)))
    out = np.empty((2, 8192, 1024), dtype=np.float32)
    for c in range(8):
        bb, q = c // 4, c % 4
        out[bb, q * T:(q + 1) * T, :] = res.results[c]["out_fm"].T
    return out


# version 03
# speedup vs baseline: 1.0089x; 1.0089x over previous
import numpy as np
import concourse.bass as bass
import concourse.mybir as mybir
from concourse.bass_utils import run_bass_kernel_spmd

F32 = mybir.dt.float32
BF16 = mybir.dt.bfloat16
I32 = mybir.dt.int32
AF = mybir.ActivationFunctionType
ALU = mybir.AluOpType
AX = mybir.AxisListType

D = 1024
T = 2048
NK = D // 128
TS = 512
NSUB = T // TS
N_SUB_BLOCKS = 12
DBG = False
L1CUT = 0

LN_EPS = 1e-5


class Prog:

    ENGS = ("sync", "scalar", "vector", "tensor", "gpsimd")

    def __init__(self):
        self.entries = []
        self.phval = 0
        self.phgval = 0
        self.bar = (0, 0)
        self.engwait = {e: (0, 0) for e in self.ENGS}
        self.dirty = {e: None for e in self.ENGS}

    def _wait_for(self, eng):
        w = self.engwait[eng]
        wa = self.bar[0] if w[0] < self.bar[0] else None
        wb = self.bar[1] if w[1] < self.bar[1] else None
        if wa is not None or wb is not None:
            self.engwait[eng] = (max(w[0], self.bar[0]),
                                 max(w[1], self.bar[1]))
        return (wa, wb)

    def op(self, eng, fn):
        e = {"eng": eng, "fn": fn, "wait": self._wait_for(eng), "inc": 0,
             "sem": "ph"}
        self.entries.append(e)
        self.dirty[eng] = e
        return e

    def dma(self, eng, out, in_):
        sem = "phg" if eng == "gpsimd" else "ph"
        e = {"eng": eng, "fn": (lambda q, o=out, i=in_: q.dma_start(out=o, in_=i)),
             "wait": self._wait_for(eng), "inc": 16, "sem": sem}
        self.entries.append(e)
        if sem == "ph":
            self.phval += 16
        else:
            self.phgval += 16
        return e

    def barrier(self):
        for eng in self.ENGS:
            d = self.dirty[eng]
            if d is not None:
                d["inc"] += 1
                self.phval += 1
                self.dirty[eng] = None
        self.bar = (self.phval, self.phgval)

    def replay(self, block, ph, phg):
        self.barrier()
        per = {e: [en for en in self.entries if en["eng"] == e]
               for e in self.ENGS}
        fa, fb = self.phval, self.phgval

        def run(q, eng):
            for en in per[eng]:
                wa, wb = en["wait"]
                if wa is not None:
                    q.wait_ge(ph, wa)
                if wb is not None:
                    q.wait_ge(phg, wb)
                inst = en["fn"](q)
                if en["inc"]:
                    inst.then_inc(ph if en["sem"] == "ph" else phg, en["inc"])
            if eng == "sync":
                q.wait_ge(ph, fa)
                if fb:
                    q.wait_ge(phg, fb)

        block.sync(lambda q: run(q, "sync"))
        block.scalar(lambda q: run(q, "scalar"))
        block.vector(lambda q: run(q, "vector"))
        block.tensor(lambda q: run(q, "tensor"))
        block.gpsimd(lambda q: run(q, "gpsimd"))


class Builder:
    def __init__(self, blocks=None):
        self.blocks = list(range(N_SUB_BLOCKS)) if blocks is None else list(blocks)
        nc = bass.Bass("TRN2", target_bir_lowering=False,
                       detect_race_conditions=False)
        self.nc = nc
        self.p = Prog()
        self._ctx = []
        self.inp = {}
        self.input_names = []
        self._declare()
        try:
            self._emit()
            sem_cm = nc.semaphore("ph")
            ph = sem_cm.__enter__(); self._ctx.append(sem_cm)
            semg_cm = nc.semaphore("phg")
            phg = semg_cm.__enter__(); self._ctx.append(semg_cm)
            blk_cm = nc.Block()
            blk = blk_cm.__enter__(); self._ctx.append(blk_cm)
            self.p.replay(blk, ph, phg)
        finally:
            for cm in reversed(self._ctx):
                cm.__exit__(None, None, None)

    def din(self, name, shape, dtype=F32):
        ap = self.nc.dram_tensor(name, shape, dtype, kind="ExternalInput").ap()
        self.inp[name] = ap
        self.input_names.append(name)
        return ap

    def internal(self, name, shape, dtype=F32, shared=False):
        kw = {"kind": "Internal"}
        if shared:
            kw["addr_space"] = "Shared"
        return self.nc.dram_tensor(name, shape, dtype, **kw).ap()

    def sb(self, name, shape, dtype):
        cm = self.nc.sbuf_tensor(name, shape, dtype)
        t = cm.__enter__(); self._ctx.append(cm)
        return t

    def psum_t(self, name, shape, dtype=F32):
        cm = self.nc.psum_tensor(name, shape, dtype)
        t = cm.__enter__(); self._ctx.append(cm)
        return t

    def _declare(self):
        d = self.din
        self.x_in = d("x_fm", [D, T])
        self.out = self.nc.dram_tensor("out_fm", [D, T], F32,
                                       kind="ExternalOutput").ap()
        self.dbg = self.nc.dram_tensor("dbg_fm", [128, 8192], F32,
                                       kind="ExternalOutput").ap()
        self.dbg_off = 0
        d("ones_bf", [128, 128], BF16)
        d("ones_row_bf", [1, 512], BF16)
        d("ident_bf", [128, 128], BF16)
        d("ident_f32", [128, 128], F32)
        d("rope_cos", [64, T])
        d("rope_sin", [64, T])
        d("triu_incl_bf", [128, 128], BF16)
        d("sel64_bf", [64, 2048], BF16)
        d("l0_kernel", [D, D]); d("l0_bias_row", [1, D])
        d("l0_a", [D, 16]); d("l0_b", [16, D])
        d("l0_wq", [D, D]); d("l0_wk", [D, D]); d("l0_wv", [D, D])
        d("l0_wo", [D, D]); d("l0_bo_row", [1, D])
        d("l0_wba", [D, 16]); d("l0_bba", [16, 1])
        d("l1_gate", [D, 8]); d("l1_gateb_row", [1, 8])
        d("l1_W1", [D, 2048]); d("l1_b1cols", [128, 16])
        d("l1_W2", [2048, D]); d("l1_b2", [8, D])
        d("comb2_all", [8, 1]); d("comb2_prec", [8, 1]); d("comb2_thrbase", [1, 1])
        d("l2_W1", [D, 4096]); d("l2_b1row", [1, 4096])
        d("l2_theta", [128, 32]); d("l2_wr", [128, 32]); d("l2_wi", [128, 32])
        d("l2_W2", [4096, D]); d("l2_b2row", [1, D])
        d("l3_wkt", [D, D]); d("l3_wvt", [D, D]); d("l3_wrt", [D, D])
        d("l3_wkc", [D, 4096]); d("l3_wvc", [4096, D]); d("l3_wrc", [D, D])
        d("l3_ln1g", [128, 8]); d("l3_ln1b", [128, 8])
        d("l3_ln2g", [128, 8]); d("l3_ln2b", [128, 8])
        d("l3_mixk", [128, 8]); d("l3_mixv", [128, 8]); d("l3_mixr", [128, 8])
        d("l3_cmixk", [128, 8]); d("l3_cmixr", [128, 8])
        d("l3_tf", [128, 8])
        d("l3f_wr", [D, D]); d("l3f_wk", [D, D]); d("l3f_wv", [D, D])
        d("l3f_mur", [128, 8]); d("l3f_muk", [128, 8])
        d("comb3_prev", [8, 1])
        d("l2a_lng", [128, 8]); d("l2a_lnb", [128, 8])
        d("l2a_Win", [D, 4096]); d("l2a_convw", [128, 64]); d("l2a_convb", [128, 16])
        d("l2a_Wx", [2048, 33]); d("l2a_wdt", [128, 16]); d("l2a_bdt", [128, 16])
        d("l2a_alog", [128, 256]); d("l2a_dpar", [128, 16])
        d("l2a_Wout", [2048, D])
        d("ones_f32", [1, 128])
        self.i_scalar = self.internal("i_scalar", [1, 16], F32)
        self.i_wq = self.internal("i_wq", [128, NK * D], BF16)
        self.i_k = self.internal("i_k", [128, 8 * T], BF16)
        self.i_v = self.internal("i_v", [128, 16 * 1024], BF16)
        self.i_o = self.internal("i_o", [128, 32768], BF16)
        self.i_small = self.internal("i_small", [8, 32], F32)
        self.i_lcs = self.internal("i_lcs", [8, 16], F32)
        self.i_z = self.internal("i_z", [128, 32768], BF16)
        self.i_bcB = self.internal("i_bcB", [16, 2048], BF16)
        self.i_bcC = self.internal("i_bcC", [16, 2048], BF16)
        self.i_bcD = self.internal("i_bcD", [1, 2048], F32)
        self.ag4_in = self.internal("ag4_in", [128, 256], F32)
        self.ag4_out = self.internal("ag4_out", [1024, 256], F32, shared=True)
        self.i_xc = self.internal("i_xc", [128, 32768], BF16)
        self.ag3_in = self.internal("ag3_in", [128, 8], F32)
        self.ag3_out = self.internal("ag3_out", [1024, 8], F32, shared=True)
        self.ag2_in = self.internal("ag2_in", [1, 16], F32)
        self.ag2_out = self.internal("ag2_out", [8, 16], F32, shared=True)
        self.i_xf32 = self.internal("i_xf32", [128, 16384], F32)
        self.i_qf32 = self.internal("i_qf32", [128, 16384], F32)
        self.i_kf32 = self.internal("i_kf32", [128, 16384], F32)
        self.i_bk = self.internal("i_bk", [128, 16384], F32)
        self.i_bf32 = self.internal("i_bf32", [16, 2048], F32)
        self.ag_in = self.internal("ag_in", [128, 1032], F32)
        self.ag_out = self.internal("ag_out", [8 * 128, 1032], F32, shared=True)
        d("l1a_wq", [D, D]); d("l1a_wk", [D, D]); d("l1a_wv", [D, D])
        d("l1a_wo", [D, D]); d("l1a_wb", [D, 16]); d("l1a_bbcol", [16, 1])
        d("l1a_cqw", [128, 32]); d("l1a_ckw", [128, 32])
        d("l1a_cqb", [128, 8]); d("l1a_ckb", [128, 8])
        d("l1a_lnw", [128, 8]); d("l1a_lnb", [128, 8])
        d("l1a_cons", [128, 514], BF16)
        d("l1a_consf", [128, 130], F32)
        d("comb_alpha", [8, 1])
        d("comb_gamma", [8, 1])
        d("comb_a", [8, 1])

        self.xs = [self.sb(f"xs{k}", [128, T], F32) for k in range(NK)]
        self.xb = [self.sb(f"xb{k}", [128, T], BF16) for k in range(NK)]
        self.gf = [self.sb(f"gf{k}", [128, T], F32) for k in range(3)]
        self.gb = [self.sb(f"gb{k}", [128, T], BF16) for k in range(6)]
        self.wst = self.sb("wst", [128, 512], F32)
        self.wbf = self.sb("wbf", [128, 16384], BF16)
        self.wbf2 = self.sb("wbf2", [128, 4096], BF16)
        self.resw = self.sb("resw", [128, 2560], BF16)
        self.resw2 = self.sb("resw2", [128, 1024], BF16)
        self.cons = self.sb("cons", [128, 896], BF16)
        self.consf = self.sb("consf", [128, 128], F32)
        self.onesf_row = self.sb("onesf_row", [1, 128], F32)
        self.sel64 = self.sb("sel64", [64, 2048], BF16)
        self.colf = self.sb("colf", [128, 460], F32)
        self.l1c = self.sb("l1c", [128, 514], BF16)
        self.l1cf = self.sb("l1cf", [128, 130], F32)
        self.bcolf = self.sb("bcolf", [128, 104], F32)
        self.iscr = self.sb("iscr", [128, 64], I32)
        self.ropecs = self.sb("ropecs", [128, 512], F32)
        self.ropecs2 = self.sb("ropecs2", [128, 512], F32)
        self.ps = [self.psum_t(f"ps{i}", [128, 512], F32) for i in range(6)]
        self.psb = self.psum_t("psb", [128, 1024], BF16)

    @property
    def ones_bf(self): return self.cons[:, 0:128]
    @property
    def ident_bf(self): return self.cons[:, 128:256]
    @property
    def triu_incl(self): return self.cons[:, 256:384]
    @property
    def onesrow(self): return self.cons[0:1, 384:896]

    def tt(self, eng, out, a, b, op):
        self.p.op(eng, lambda q: q.tensor_tensor(out, a, b, op))

    def ts2(self, eng, out, a, s1, s2, op0, op1):
        self.p.op(eng, lambda q: q.tensor_scalar(out, a, s1, s2, op0, op1))

    def tss(self, eng, out, a, s, op):
        self.p.op(eng, lambda q: q.tensor_single_scalar(out, a, s, op))

    def stt(self, eng, out, in0, s, in1, op0, op1):
        self.p.op(eng, lambda q: q.scalar_tensor_tensor(out, in0, s, in1,
                                                        op0, op1))

    def act(self, out, in_, func, bias=0.0, scale=1.0):
        self.p.op("scalar", lambda q: q.activation(out, in_, func,
                                                   bias=bias, scale=scale))

    def cp(self, eng, out, in_):
        if eng == "scalar":
            self.p.op(eng, lambda q: q.activation(out, in_, AF.Copy))
        else:
            self.p.op(eng, lambda q: q.tensor_copy(out, in_))

    def mm(self, out, lhsT, rhs, start, stop):
        self.p.op("tensor", lambda q: q.matmul(out, lhsT, rhs,
                                               start=start, stop=stop))

    def transpose(self, out_ps, in_, ident):
        self.p.op("tensor", lambda q: q.transpose(out_ps, in_, ident))

    def scan(self, eng, out, d0, d1, init, op0, op1):
        self.p.op(eng, lambda q: q.tensor_tensor_scan(out, d0, d1, init,
                                                      op0, op1))

    def recip(self, out, in_):
        self.p.op("vector", lambda q: q.reciprocal(out, in_))

    def reduce(self, eng, out, in_, op, absval=False):
        self.p.op(eng, lambda q: q.tensor_reduce(out, in_, AX.X, op,
                                                 apply_absolute_value=absval))

    def memset(self, eng, ap, v):
        self.p.op(eng, lambda q: q.memset(ap, v))

    def dbg_dump(self, ap, cols, cast=False, name=None):
        if cast:
            self.cp("vector", self.gf[2][0:ap.shape[0], 0:cols], ap)
            self.p.barrier()
            ap = self.gf[2][0:ap.shape[0], 0:cols]
        rows = ap.shape[0]
        dst = self.dbg[0:rows, self.dbg_off:self.dbg_off + cols]
        e = {"eng": "sync", "fn": (lambda q, o=dst, i=ap: q.dma_start(
            out=o, in_=i, allow_slow_non_contiguous=True)),
             "wait": self.p._wait_for("sync"), "inc": 16, "sem": "ph"}
        self.p.entries.append(e)
        self.p.phval += 16
        self.p.barrier()
        if not hasattr(self, "dbg_log"):
            self.dbg_log = []
        self.dbg_log.append((name, self.dbg_off, cols))
        self.dbg_off += cols

    def load_consts(self):
        p, i = self.p, self.inp
        p.dma("sync", self.cons[:, 0:128], i["ones_bf"])
        p.dma("sync", self.cons[:, 128:256], i["ident_bf"])
        p.dma("sync", self.cons[:, 256:384], i["triu_incl_bf"])
        p.dma("sync", self.cons[0:1, 384:896], i["ones_row_bf"])
        p.dma("sync", self.consf[:, 0:128], i["ident_f32"])
        p.dma("sync", self.onesf_row[0:1, :], i["ones_f32"])
        p.dma("sync", self.sel64[:], i["sel64_bf"])
        for k in range(NK):
            p.dma("sync", self.xs[k][:], self.x_in[k * 128:(k + 1) * 128, :])
        p.barrier()
        self.memset("vector", self.gf[0][:, 0:1032], 0.0)
        p.barrier()
        p.dma("sync", self.ag_in[:, :], self.gf[0][:, 0:1032])
        p.dma("sync", self.ag4_in[:, :], self.gf[0][:, 0:256])
        p.dma("sync", self.ag3_in[:, :], self.gf[0][:, 0:8])
        p.dma("sync", self.ag2_in[0:1, :], self.gf[0][0:1, 0:16])
        p.barrier()

    def load_col(self, dram_ap, dest_col):
        self.p.dma("sync", dest_col, dram_ap)

    def load_weight(self, name, din, dout, dest, dest_off=0):
        p = self.p
        w = self.inp[name] if isinstance(name, str) else name
        nchunk = din // 128
        for c in range(nchunk):
            npiece = (dout + 511) // 512
            for j in range(npiece):
                wdt = min(512, dout - j * 512)
                p.dma("sync", self.wst[:, 0:wdt],
                      w[c * 128:(c + 1) * 128, j * 512:j * 512 + wdt])
                p.barrier()
                o = dest_off + c * dout + j * 512
                eng = ("vector", "scalar", "gpsimd")[(c + j) % 3]
                self.cp(eng, dest[:, o:o + wdt], self.wst[:, 0:wdt])
                p.barrier()

    def load_rows_bf(self, dram_ap, rows, width, dest_ap_fn):
        p = self.p
        for j in range((width + 511) // 512):
            w = min(512, width - j * 512)
            p.dma("sync", self.wst[0:rows, 0:w],
                  dram_ap[:, j * 512:j * 512 + w])
            p.barrier()
            self.cp("vector", dest_ap_fn(j * 512, j * 512 + w),
                    self.wst[0:rows, 0:w])
            p.barrier()

    def emit_ln(self, src, out_bf):
        p = self.p
        mu_bf, rstd_bf = self.gb[4], self.gb[5]
        sq = [self.gb[2], self.gb[3]]
        for k in range(NK):
            eng = ("scalar", "vector", "gpsimd")[k % 3]
            self.cp(eng, out_bf[k][:], src[k][:])
        p.barrier()
        for k in range(NK):
            for s in range(NSUB):
                self.mm(self.ps[s][:, :], self.ones_bf,
                        out_bf[k][:, s * TS:(s + 1) * TS], k == 0, k == NK - 1)
        self.tt("vector", sq[0][:], out_bf[0][:], out_bf[0][:], ALU.mult)
        p.barrier()
        for s in range(NSUB):
            sl = slice(s * TS, (s + 1) * TS)
            self.act(self.gf[0][:, sl], self.ps[s][:, :], AF.Copy, scale=1.0 / D)
        p.barrier()
        for s in range(NSUB):
            sl = slice(s * TS, (s + 1) * TS)
            self.cp("scalar", mu_bf[:, sl], self.gf[0][:, sl])
        vbank = [self.ps[4], self.ps[5], self.ps[0], self.ps[1]]
        for k in range(NK):
            for s in range(NSUB):
                self.mm(vbank[s][:, :], self.ones_bf,
                        sq[k % 2][:, s * TS:(s + 1) * TS], k == 0, k == NK - 1)
            if k + 1 < NK:
                self.tt("vector", sq[(k + 1) % 2][:], out_bf[k + 1][:],
                        out_bf[k + 1][:], ALU.mult)
            p.barrier()
        for s in range(NSUB):
            sl = slice(s * TS, (s + 1) * TS)
            self.ts2("vector", self.gf[1][:, sl], vbank[s][:, :],
                     1.0 / D, LN_EPS, ALU.mult, ALU.add)
        self.stt("vector", self.gf[2][:], self.gf[0][:], -1.0, self.gf[0][:],
                 ALU.mult, ALU.mult)
        p.barrier()
        self.tt("vector", self.gf[1][:], self.gf[1][:], self.gf[2][:], ALU.add)
        p.barrier()
        self.recip(self.gf[1][:], self.gf[1][:])
        p.barrier()
        self.act(rstd_bf[:], self.gf[1][:], AF.Sqrt)
        p.barrier()
        for k in range(NK):
            eng = ("vector", "gpsimd")[k % 2]
            self.tt(eng, out_bf[k][:], out_bf[k][:], mu_bf[:], ALU.subtract)
        p.barrier()
        for k in range(NK):
            eng = ("vector", "gpsimd")[k % 2]
            self.tt(eng, out_bf[k][:], out_bf[k][:], rstd_bf[:], ALU.mult)
        p.barrier()

    def bcast_scalar_to_col(self, src_11, dest_col, idx=0):
        p = self.p
        p.dma("sync", self.i_scalar[0:1, idx:idx + 1], src_11)
        p.barrier()
        p.dma("sync", dest_col,
              self.i_scalar[0:1, idx:idx + 1].partition_broadcast(128))
        p.barrier()

    def quantize_l0(self):
        p = self.p
        w = self.inp["l0_kernel"]
        for c in range(NK):
            p.dma("sync", self.wst[:, 0:512], w[c * 128:(c + 1) * 128, 0:512])
            p.barrier()
            self.reduce("vector", self.colf[:, c:c + 1],
                        self.wst[:, 0:512], ALU.max, absval=True)
            p.barrier()
            p.dma("sync", self.wst[:, 0:512], w[c * 128:(c + 1) * 128, 512:D])
            p.barrier()
            self.reduce("vector", self.colf[:, 16:17],
                        self.wst[:, 0:512], ALU.max, absval=True)
            p.barrier()
            self.tt("vector", self.colf[:, c:c + 1], self.colf[:, c:c + 1],
                    self.colf[:, 16:17], ALU.max)
            p.barrier()
        self.reduce("vector", self.colf[:, 8:9], self.colf[:, 0:8], ALU.max)
        p.barrier()
        self.transpose(self.ps[0][0:1, 0:128], self.colf[:, 8:9], self.consf[:, 0:128])
        p.barrier()
        self.reduce("vector", self.colf[0:1, 9:10], self.ps[0][0:1, 0:128], ALU.max)
        p.barrier()
        self.tss("vector", self.colf[0:1, 10:11], self.colf[0:1, 9:10],
                 1.0 / 7.0, ALU.mult)
        self.recip(self.colf[0:1, 11:12], self.colf[0:1, 9:10])
        p.barrier()
        self.tss("vector", self.colf[0:1, 11:12], self.colf[0:1, 11:12],
                 7.0, ALU.mult)
        p.barrier()
        self.bcast_scalar_to_col(self.colf[0:1, 10:11], self.colf[:, 12:13], 0)
        self.bcast_scalar_to_col(self.colf[0:1, 11:12], self.colf[:, 13:14], 1)
        scale_col = self.colf[:, 12:13]
        inv_col = self.colf[:, 13:14]
        for c in range(NK):
          for half in range(2):
            p.dma("sync", self.wst[:, 0:512],
                  w[c * 128:(c + 1) * 128, half * 512:(half + 1) * 512])
            p.barrier()
            for hh2 in range(4):
                hh = half * 4 + hh2
                hsl = slice(hh2 * 128, (hh2 + 1) * 128)
                t = self.gf[0][:, 0:128]
                tf = self.gf[1][:, 0:128]
                df = self.gf[2][:, 0:128]
                self.stt("vector", t, self.wst[:, hsl], inv_col,
                         self.wst[:, hsl], ALU.mult, ALU.bypass)
                p.barrier()
                for _h2 in range(2):
                    _s2 = slice(_h2 * 64, (_h2 + 1) * 64)
                    self.cp("vector", self.iscr[:, 0:64], t[:, _s2])
                    p.barrier()
                    self.cp("vector", tf[:, _s2], self.iscr[:, 0:64])
                    p.barrier()
                self.tt("vector", df, t, tf, ALU.subtract)
                p.barrier()
                self.tss("vector", t, df, 0.5, ALU.is_gt)
                p.barrier()
                self.tt("vector", tf, tf, t, ALU.add)
                self.tss("vector", t, df, -0.5, ALU.is_lt)
                p.barrier()
                self.tt("vector", tf, tf, t, ALU.subtract)
                p.barrier()
                self.ts2("vector", tf, tf, -8.0, 7.0, ALU.max, ALU.min)
                p.barrier()
                self.stt("vector", self.wbf[:, c * D + hh * 128:c * D + (hh + 1) * 128],
                         tf, scale_col, tf, ALU.mult, ALU.bypass)
                p.barrier()
        p.dma("sync", self.i_wq[:, :], self.wbf[:, 0:NK * D])
        p.barrier()

    def load_l0_small(self):
        p = self.p
        self.load_weight("l0_a", D, 16, self.resw, 0)
        self.load_rows_bf(self.inp["l0_b"], 16, D,
                          lambda a, b: self.resw[0:16, 128 + a:128 + b])
        self.load_rows_bf(self.inp["l0_bias_row"], 1, D,
                          lambda a, b: self.resw[0:1, 1152 + a:1152 + b])

    def emit_ffn0(self, first):
        p = self.p
        self.emit_ln(self.xs, self.xb)
        if first:
            self.quantize_l0()
            self.load_l0_small()
        else:
            for piece in range(4):
                p.dma("sync", self.wbf[:, piece * 2048:(piece + 1) * 2048],
                      self.i_wq[:, piece * 2048:(piece + 1) * 2048])
            p.barrier()
        a_lhsT = self.resw[:, 0:128]
        b_lhsT = self.resw[0:16, 128:128 + D]
        bias_row = self.resw[0:1, 1152:1152 + D]
        for k in range(NK):
            for s in range(NSUB):
                self.mm(self.ps[s][0:16, :], a_lhsT[:, k * 16:(k + 1) * 16],
                        self.xb[k][:, s * TS:(s + 1) * TS], k == 0, k == NK - 1)
        p.barrier()
        rbf = self.gb[0]
        for s in range(NSUB):
            self.cp("scalar", rbf[0:16, s * TS:(s + 1) * TS], self.ps[s][0:16, :])
        p.barrier()
        for o in range(NK):
            for s in range(NSUB):
                sl = slice(s * TS, (s + 1) * TS)
                for k in range(NK):
                    self.mm(self.ps[s][:, :],
                            self.wbf[:, k * D + o * 128:k * D + (o + 1) * 128],
                            self.xb[k][:, sl], k == 0, False)
                self.mm(self.ps[s][:, :], b_lhsT[:, o * 128:(o + 1) * 128],
                        rbf[0:16, sl], False, False)
                self.mm(self.ps[s][:, :],
                        bias_row[0:1, o * 128:(o + 1) * 128],
                        self.onesrow, False, True)
            p.barrier()
            for s in range(NSUB):
                sl = slice(s * TS, (s + 1) * TS)
                self.stt("vector", self.xs[o][:, sl], self.ps[s][:, :], 0.5,
                         self.xs[o][:, sl], ALU.mult, ALU.add)
            p.barrier()



    def qt(self, h):
        if h < 6:
            return self.gb[h]
        return self.wbf2[:, (h - 6) * T:(h - 5) * T]

    @staticmethod
    def free_bcast(ap, n):
        return bass.AP(ap.tensor, ap.offset, list(ap.ap) + [[0, n]])

    def dma_bcast_small(self, sbuf_src, dest, r, c, src_dram=None):
        p = self.p
        if src_dram is None:
            if c == 1:
                p.dma("sync", self.i_small[0:1, 0:r], sbuf_src)
                p.barrier()
                p.dma("sync", dest,
                      self.i_small[0:1, 0:r].partition_broadcast(128))
                p.barrier()
                return
            p.dma("sync", self.i_small[0:r, 0:c], sbuf_src)
            p.barrier()
            src_dram = self.i_small[0:r, 0:c]
        dst = dest.rearrange("p (h c) -> p h c", h=r)
        p.dma("sync", dst, src_dram.partition_broadcast(128))
        p.barrier()

    def emit_l0_attn(self):
        p = self.p
        NH, C, NC = 8, 128, 16
        self.emit_ln(self.xs, self.xb)
        g0, g1, g2 = self.gf[0], self.gf[1], self.gf[2]
        self.load_weight("l0_wba", D, 16, self.resw, 1280)
        self.load_col(self.inp["l0_bba"][0:8, :], self.colf[0:8, 0:1])
        self.load_col(self.inp["l0_bba"][8:16, :], self.colf[0:8, 1:2])
        p.barrier()
        for part, coff, boff in ((0, 0, 0), (1, 8, 32)):
            for s in range(NSUB):
                for k in range(NK):
                    self.mm(self.ps[s][0:8, :],
                            self.resw[:, 1280 + k * 16 + coff:1280 + k * 16 + coff + 8],
                            self.xb[k][:, s * TS:(s + 1) * TS], k == 0, k == NK - 1)
            p.barrier()
            for s in range(NSUB):
                self.act(g0[boff:boff + 8, s * TS:(s + 1) * TS],
                         self.ps[s][0:8, :], AF.Sigmoid,
                         bias=self.colf[0:8, 0:1] if coff == 0
                         else self.colf[0:8, 1:2])
            p.barrier()
        self.ts2("vector", g0[0:8, :], g0[0:8, :], 0.149, 0.85, ALU.mult, ALU.add)
        p.barrier()
        if DBG:
            self.dbg_dump(g0[0:8, 0:256], 256)
            self.dbg_dump(g0[32:40, 0:256], 256)
        self.act(g1[0:8, :], g0[0:8, :], AF.Ln)
        self.memset("vector", g2[0:8, :], 0.0)
        p.barrier()
        self.scan("vector", g0[64:72, :], g1[0:8, :], g2[0:8, :], 0.0,
                  ALU.add, ALU.add)
        p.barrier()
        L3d = g0[64:72, :].rearrange("p (c t) -> p c t", t=128)
        self.memset("vector", self.colf[0:8, 16:17], 0.0)
        self.cp("vector",
                self.colf[0:8, 17:32].rearrange("p (a c) -> p a c", a=1),
                L3d[:, 0:15, 127:128].rearrange("p c t -> p t c"))
        self.cp("gpsimd",
                self.colf[0:8, 32:48].rearrange("p (a c) -> p a c", a=1),
                L3d[:, 0:16, 127:128].rearrange("p c t -> p t c"))
        p.barrier()
        self.tt("vector", self.colf[0:8, 48:64], self.colf[0:8, 32:48],
                self.colf[0:8, 16:32], ALU.subtract)
        p.barrier()
        self.act(self.colf[0:8, 48:64], self.colf[0:8, 48:64], AF.Exp)
        p.barrier()
        self.act(self.colf[0:8, 64:80], self.colf[0:8, 16:32], AF.Exp)
        p.barrier()
        self.cp("vector", self.colf[0:8, 80:96], self.colf[0:8, 48:64])
        self.cp("gpsimd", self.colf[0:8, 96:112], self.colf[0:8, 64:80])
        p.barrier()
        self.dma_bcast_small(self.colf[0:8, 80:112], self.colf[:, 128:384], 8, 32)
        for j in range(16):
            self.tss("vector", g2[64:72, j * 128:(j + 1) * 128],
                     g2[0:8, 0:128], self.colf[0:8, 16 + j:17 + j], ALU.add)
        p.barrier()
        self.tt("vector", g1[0:8, :], g0[64:72, :], g2[64:72, :], ALU.subtract)
        self.tt("gpsimd", g1[32:40, :], g2[64:72, :], g0[64:72, :], ALU.subtract)
        p.barrier()
        self.act(g1[0:8, :], g1[0:8, :], AF.Exp)
        p.barrier()
        self.act(g1[32:40, :], g1[32:40, :], AF.Exp)
        p.barrier()
        self.tt("vector", g1[32:40, :], g1[32:40, :], g0[32:40, :], ALU.mult)
        p.barrier()
        if DBG:
            self.dbg_dump(g0[64:72, 0:256], 256)
            self.dbg_dump(g2[64:72, 0:256], 256)
            self.dbg_dump(g1[0:8, 0:256], 256)
            self.dbg_dump(g1[32:40, 0:256], 256)
        self.memset("vector", self.resw[0:64, 512:2560], 0.0)
        p.barrier()
        self.cp("vector", self.resw[0:8, 512:2560], g1[0:8, :])
        self.cp("gpsimd", self.resw[32:40, 512:2560], g1[32:40, :])
        p.barrier()

        self.load_weight("l0_wv", D, D, self.wbf, 0)
        for j in range(NC):
            jc = slice(j * C, (j + 1) * C)
            for half in range(2):
                for k in range(NK):
                    self.mm(self.ps[half][:, :], self.xb[k][:, jc],
                            self.wbf[:, k * D + half * 512:k * D + (half + 1) * 512],
                            k == 0, k == NK - 1)
            p.barrier()
            for half in range(2):
                self.cp("scalar", self.gb[0][:, half * 512:(half + 1) * 512],
                        self.ps[half][:, :])
            p.barrier()
            p.dma("sync", self.i_v[:, j * 1024:(j + 1) * 1024],
                  self.gb[0][:, 0:1024])
            p.barrier()

        for which in ("k", "q"):
            self.load_weight("l0_w" + which, D, D, self.wbf, 0)
            rowbase = 8 if which == "k" else 0
            for s in range(NSUB):
                sl = slice(s * TS, (s + 1) * TS)
                p.dma("sync", self.ropecs[0:64, :],
                      self.inp["rope_cos"][:, sl])
                p.dma("sync", self.ropecs[64:128, :],
                      self.inp["rope_sin"][:, sl])
                p.dma("sync", self.ropecs2[64:128, :],
                      self.inp["rope_cos"][:, sl])
                p.dma("sync", self.ropecs2[0:64, :],
                      self.inp["rope_sin"][:, sl])
                p.barrier()
                for h in range(NH):
                    for k in range(NK):
                        self.mm(self.ps[0][:, :],
                                self.wbf[:, k * D + h * 128:k * D + (h + 1) * 128],
                                self.xb[k][:, sl], k == 0, k == NK - 1)
                    self.mm(self.ps[4][:, :],
                            self.sel64[:, (rowbase + h) * 128:(rowbase + h + 1) * 128],
                            self.resw[0:64, 512 + s * TS:512 + (s + 1) * TS],
                            True, True)
                    p.barrier()
                    self.act(g1[:, 0:512], self.ps[0][:, :], AF.Copy)
                    p.barrier()
                    self.tt("vector", g2[0:64, 0:512], g1[0:64, 0:512],
                            self.ropecs[0:64, :], ALU.mult)
                    self.tt("gpsimd", g2[0:64, 512:1024], g1[64:128, 0:512],
                            self.ropecs[64:128, :], ALU.mult)
                    self.tt("vector", g2[64:128, 0:512], g1[64:128, 0:512],
                            self.ropecs2[64:128, :], ALU.mult)
                    self.tt("gpsimd", g2[64:128, 512:1024], g1[0:64, 0:512],
                            self.ropecs2[0:64, :], ALU.mult)
                    p.barrier()
                    self.tt("vector", g1[0:64, 0:512], g2[0:64, 0:512],
                            g2[0:64, 512:1024], ALU.subtract)
                    self.tt("gpsimd", g1[64:128, 0:512], g2[64:128, 0:512],
                            g2[64:128, 512:1024], ALU.add)
                    p.barrier()
                    dest = (self.gb[1] if which == "k" else self.qt(h))
                    self.tt("vector", dest[:, sl], g1[:, 0:512], self.ps[4][:, :],
                            ALU.mult)
                    p.barrier()
                    if which == "k":
                        p.dma("sync",
                              self.i_k[:, h * T + s * TS:h * T + (s + 1) * TS],
                              self.gb[1][:, sl])
            p.barrier()

        if DBG:
            self.dbg_dump(self.qt(0)[:, 0:256], 256, cast=True)
            self.dbg_dump(self.gb[1][:, 0:256], 256, cast=True)
        self.load_weight("l0_wo", D, D, self.wbf, 8192)
        self.load_rows_bf(self.inp["l0_bo_row"], 1, D,
                          lambda a, b: self.resw[0:1, a:b])

        S = g2
        S_bf = self.wbf[:, 0:1024]
        self.memset("vector", S[:, 0:1024], 0.0)
        self.memset("gpsimd", S_bf, 0.0)
        p.barrier()
        ik3 = self.i_k.rearrange("p (h t) -> p h t", h=NH)
        for j in range(NC):
            jc = slice(j * C, (j + 1) * C)
            p.dma("sync", self.xb[0][:, 0:1024].rearrange("p (h c) -> p h c", h=NH),
                  ik3[:, :, j * C:(j + 1) * C])
            p.dma("gpsimd", self.xb[1][:, 0:1024], self.i_v[:, j * 1024:(j + 1) * 1024])
            p.barrier()
            for h in range(NH):
                self.mm(self.ps[h // 4][:, (h % 4) * 128:(h % 4 + 1) * 128],
                        self.xb[0][:, h * 128:(h + 1) * 128], self.qt(h)[:, jc],
                        True, True)
                self.transpose(self.psb[:, h * 128:(h + 1) * 128],
                               self.xb[0][:, h * 128:(h + 1) * 128], self.ident_bf)
            p.barrier()
            for h in range(NH):
                self.tt("vector", self.xb[2][:, h * 128:(h + 1) * 128],
                        self.ps[h // 4][:, (h % 4) * 128:(h % 4 + 1) * 128],
                        self.triu_incl, ALU.mult)
                self.cp("scalar", self.xb[3][:, h * 128:(h + 1) * 128],
                        self.psb[:, h * 128:(h + 1) * 128])
            p.barrier()
            for h in range(NH):
                po = self.ps[4 + h // 4][:, (h % 4) * 128:(h % 4 + 1) * 128]
                self.mm(po, self.xb[2][:, h * 128:(h + 1) * 128],
                        self.xb[1][:, h * 128:(h + 1) * 128], True, False)
                self.mm(po, self.qt(h)[:, jc], S_bf[:, h * 128:(h + 1) * 128],
                        False, True)
                self.mm(self.ps[2 + h // 4][:, (h % 4) * 128:(h % 4 + 1) * 128],
                        self.xb[3][:, h * 128:(h + 1) * 128],
                        self.xb[1][:, h * 128:(h + 1) * 128], True, True)
            p.barrier()
            for half in range(2):
                self.cp("scalar", self.xb[1][:, half * 512:(half + 1) * 512],
                        self.ps[4 + half][:, :])
            for h in range(NH):
                hs = slice(h * 128, (h + 1) * 128)
                self.tt("vector", S[:, hs], S[:, hs],
                        self.ps[2 + h // 4][:, (h % 4) * 128:(h % 4 + 1) * 128],
                        ALU.add)
            p.barrier()
            for h in range(NH):
                hs = slice(h * 128, (h + 1) * 128)
                self.tss("vector", S[:, hs], S[:, hs],
                         self.colf[:, 128 + h * 32 + j:129 + h * 32 + j], ALU.mult)
            p.barrier()
            self.cp("scalar", S_bf[:, 0:512], S[:, 0:512])
            self.cp("scalar", S_bf[:, 512:1024], S[:, 512:1024])
            p.dma("sync", self.i_o[:, j * 1024:(j + 1) * 1024], self.xb[1][:, 0:1024])
            p.barrier()
            if DBG and j == 0:
                self.dbg_dump(self.xb[1][:, 0:256], 256, cast=True)
                self.dbg_dump(S[:, 0:256], 256)

        if DBG:
            self.dbg_dump(S[:, 0:256], 256)
        self.act(self.colf[0:8, 112:113], g0[64:72, 2047:2048], AF.Exp)
        p.barrier()
        p.dma("sync", self.ag_in[:, 0:1024], S[:, 0:1024])
        p.dma("gpsimd", self.ag_in[0:1, 1024:1032], self.colf[0:8, 112:113])
        p.barrier()
        self.p.op("gpsimd", lambda q: q.collective_compute(
            "AllGather", ALU.bypass,
            replica_groups=[[0, 1, 2, 3, 4, 5, 6, 7]],
            ins=[self.ag_in], outs=[self.ag_out]))
        p.barrier()
        ago = self.ag_out.rearrange("(r p) c -> r p c", p=128)
        p.dma("sync", self.colf[0:8, 48:56].rearrange("p (a c) -> p a c", a=1),
              ago[:, 0:1, 1024:1032])
        self.load_col(self.inp["comb_alpha"], self.colf[0:8, 113:114])
        self.load_col(self.inp["comb_gamma"], self.colf[0:8, 114:115])
        self.load_col(self.inp["comb_a"], self.colf[0:8, 115:116])
        p.barrier()
        self.ts2("vector", self.colf[0:8, 56:64], self.colf[0:8, 48:56],
                 self.colf[0:8, 113:114], self.colf[0:8, 114:115],
                 ALU.mult, ALU.add)
        p.barrier()
        self.cp("vector", self.colf[0:8, 64:65], self.colf[0:8, 115:116])
        p.barrier()
        self.cp("vector", self.colf[0:8, 72:80], self.colf[0:8, 56:64])
        self.cp("gpsimd", self.colf[0:8, 80:81], self.colf[0:8, 64:65])
        p.barrier()
        self.cp("vector", self.colf[0:8, 88:96], self.colf[0:8, 72:80])
        self.cp("gpsimd", self.colf[0:8, 96:97], self.colf[0:8, 80:81])
        p.barrier()
        raise_ = False
        self.dma_bcast_small(self.colf[0:8, 88:97], self.colf[:, 384:456], 8, 9)
        Sin = g1
        self.memset("vector", Sin[:, 0:1024], 0.0)
        p.barrier()
        if DBG:
            self.dbg_dump(self.colf[0:8, 48:56], 8)
            self.dbg_dump(self.colf[0:8, 88:97], 9)
        for r in range(8):
            p.dma("sync", g0[:, 0:1024], ago[r, :, 0:1024])
            p.barrier()
            if DBG and r == 0:
                self.dbg_dump(g0[:, 0:256], 256)
            for h in range(NH):
                hs = slice(h * 128, (h + 1) * 128)
                self.tss("vector", Sin[:, hs], Sin[:, hs],
                         self.colf[:, 384 + r * 9 + h:385 + r * 9 + h], ALU.mult)
            p.barrier()
            for h in range(NH):
                hs = slice(h * 128, (h + 1) * 128)
                self.stt("vector", Sin[:, hs], g0[:, hs],
                         self.colf[:, 384 + r * 9 + 8:385 + r * 9 + 8],
                         Sin[:, hs], ALU.mult, ALU.add)
            p.barrier()

        if DBG:
            self.dbg_dump(Sin[:, 0:256], 256)
            self.dbg_dump(self.colf[:, 384:456], 72)
            self.dbg_dump(self.xs[0][:, 0:256], 256)
        scr_bf = self.wbf[:, 1024:1152]
        for s in range(NSUB):
            ssl = slice(s * TS, (s + 1) * TS)
            for jj in range(4):
                j = s * 4 + jj
                jc = slice(j * C, (j + 1) * C)
                p.dma("sync", self.xb[1][:, 0:1024],
                      self.i_o[:, j * 1024:(j + 1) * 1024])
                for h in range(NH):
                    hs = slice(h * 128, (h + 1) * 128)
                    self.tss("vector", g0[:, 0:128], Sin[:, hs],
                             self.colf[:, 128 + h * 32 + 16 + j:
                                        129 + h * 32 + 16 + j], ALU.mult)
                    p.barrier()
                    self.cp("vector", scr_bf, g0[:, 0:128])
                    p.barrier()
                    self.mm(self.ps[4 + h // 4][:, (h % 4) * 128:(h % 4 + 1) * 128],
                            self.qt(h)[:, jc], scr_bf, True, True)
                    p.barrier()
                if DBG and s == 0 and jj == 0:
                    self.dbg_dump(self.ps[4][:, 0:256], 256, cast=True)
                    self.dbg_dump(self.xb[1][:, 0:256], 256, cast=True)
                for half in range(2):
                    self.tt("vector", self.xb[2][:, half * 512:(half + 1) * 512],
                            self.ps[4 + half][:, :],
                            self.xb[1][:, half * 512:(half + 1) * 512], ALU.add)
                p.barrier()
                for h in range(NH):
                    self.transpose(self.psb[:, h * 128:(h + 1) * 128],
                                   self.xb[2][:, h * 128:(h + 1) * 128],
                                   self.ident_bf)
                p.barrier()
                for h in range(NH):
                    dst = self.xb[4 + h // 4]
                    self.cp("scalar",
                            dst[:, (h % 4) * 512 + jj * 128:
                                (h % 4) * 512 + (jj + 1) * 128],
                            self.psb[:, h * 128:(h + 1) * 128])
                p.barrier()
                if DBG and s == 0 and jj == 0:
                    self.dbg_dump(self.xb[2][:, 0:256], 256, cast=True)
                    self.dbg_dump(self.xb[4][:, 0:256], 256, cast=True)
            for o in range(NK):
                for k in range(NK):
                    self.mm(self.ps[0][:, :],
                            self.wbf[:, 8192 + k * D + o * 128:
                                      8192 + k * D + (o + 1) * 128],
                            self.xb[4 + k // 4][:, (k % 4) * 512:(k % 4 + 1) * 512],
                            k == 0, False)
                self.mm(self.ps[0][:, :], self.resw[0:1, o * 128:(o + 1) * 128],
                        self.onesrow, False, True)
                p.barrier()
                if DBG and s == 0 and o == 0:
                    self.dbg_dump(self.ps[0][:, 0:256], 256, cast=True)
                self.tt("vector", self.xs[o][:, ssl], self.ps[0][:, :],
                        self.xs[o][:, ssl], ALU.add)
                p.barrier()

    def dbg_xs(self):
        if DBG:
            self.dbg_dump(self.xs[0][:, 0:256], 256)
            self.dbg_dump(self.xs[7][:, 1792:2048], 256)

    def hslot(self, i):
        if i < 6:
            return self.gb[i]
        return self.wbf2[:, (i - 6) * T:(i - 5) * T]

    def emit_moe(self):
        p = self.p
        self.emit_ln(self.xs, self.xb)
        g0, g1, g2 = self.gf[0], self.gf[1], self.gf[2]
        self.load_weight("l1_gate", D, 8, self.resw, 1280)
        p.dma("sync", self.wst[0:1, 0:8], self.inp["l1_gateb_row"])
        p.barrier()
        self.cp("vector", self.resw[0:1, 1344:1352], self.wst[0:1, 0:8])
        p.barrier()
        for s in range(NSUB):
            for k in range(NK):
                self.mm(self.ps[s][0:8, :],
                        self.resw[:, 1280 + k * 8:1280 + (k + 1) * 8],
                        self.xb[k][:, s * TS:(s + 1) * TS], k == 0, False)
            self.mm(self.ps[s][0:8, :], self.resw[0:1, 1344:1352],
                    self.onesrow, False, True)
        p.barrier()
        for s in range(NSUB):
            self.act(g0[0:8, s * TS:(s + 1) * TS], self.ps[s][0:8, :], AF.Copy)
        p.barrier()
        if DBG:
            self.dbg_dump(g0[0:8, 0:512], 512, name="logits")
        for j in range(16):
            self.mm(self.ps[j % 2][0:128, 0:8],
                    g0[0:8, j * 128:(j + 1) * 128], self.consf[0:8, 0:8],
                    True, True)
            if j % 2 == 1 or j == 15:
                p.barrier()
                for jj in (j - 1, j) if j % 2 == 1 else (j,):
                    self.cp("vector", g1[:, jj * 8:(jj + 1) * 8],
                            self.ps[jj % 2][0:128, 0:8])
                p.barrier()
        for j in range(16):
            self.reduce("vector", g1[:, 128 + j:129 + j],
                        g1[:, j * 8:(j + 1) * 8], ALU.max)
        self.memset("gpsimd", self.colf[0:128, 452:460], -1e30)
        p.barrier()
        for j in range(16):
            jc = slice(j * 8, (j + 1) * 8)
            self.tss("vector", g2[0:128, 8:16], g1[:, jc],
                     g1[:, 128 + j:129 + j], ALU.is_ge)
            p.barrier()
            self.cp("vector", self.iscr[:, 0:8], g2[0:128, 8:16])
            p.barrier()
            self.p.op("vector", lambda q, jc=jc, j=j: q.select(
                g1[:, 144 + j * 8:152 + j * 8],
                self.iscr[:, 0:8], self.colf[0:128, 452:460], g1[:, jc]))
            p.barrier()
        for j in range(16):
            self.reduce("vector", g1[:, 272 + j:273 + j],
                        g1[:, 144 + j * 8:152 + j * 8], ALU.max)
        p.barrier()
        for j in range(16):
            jc = slice(j * 8, (j + 1) * 8)
            self.tss("vector", g1[:, 288 + j * 8:296 + j * 8], g1[:, jc],
                     g1[:, 272 + j:273 + j], ALU.is_ge)
        p.barrier()
        if DBG:
            self.dbg_dump(g1[:, 0:8], 8, name="lt0")
            self.dbg_dump(g1[:, 128:129], 1, name="m1c0")
            self.dbg_dump(g1[:, 144:152], 8, name="l2c0")
            self.dbg_dump(g1[:, 272:273], 1, name="m2c0")
            self.dbg_dump(g1[:, 288:296], 8, name="maskc0")
        for j in range(16):
            self.mm(self.ps[j % 2][0:8, 0:128],
                    g1[:, 288 + j * 8:296 + j * 8], self.consf[:, 0:128],
                    True, True)
            p.barrier()
            self.cp("vector", g0[0:8, j * 128:(j + 1) * 128],
                    self.ps[j % 2][0:8, 0:128])
            p.barrier()
        self.reduce("vector", self.colf[0:8, 2:3], g0[0:8, :], ALU.add)
        self.ts2("gpsimd", g1[0:8, :], g0[0:8, :], -1.0, 1.0, ALU.mult, ALU.add)
        p.barrier()
        self.memset("vector", g2[0:8, 0:1], 0.0)
        self.cp("gpsimd", g2[0:8, 1:T], g1[0:8, 0:T - 1])
        p.barrier()
        self.scan("vector", g1[0:8, :], g2[0:8, :], g2[0:8, :], 0.0,
                  ALU.add, ALU.bypass)
        p.barrier()
        p.dma("sync", self.ag2_in[0:1, 0:8], self.colf[0:8, 2:3])
        p.barrier()
        self.p.op("gpsimd", lambda q: q.collective_compute(
            "AllGather", ALU.bypass, replica_groups=[[0, 1, 2, 3, 4, 5, 6, 7]],
            ins=[self.ag2_in], outs=[self.ag2_out]))
        p.barrier()
        p.dma("sync", self.colf[0:8, 8:24], self.ag2_out)
        self.load_col(self.inp["comb2_all"], self.colf[0:8, 4:5])
        self.load_col(self.inp["comb2_prec"], self.colf[0:8, 5:6])
        p.barrier()
        self.mm(self.ps[0][0:1, 0:8], self.colf[0:8, 4:5], self.colf[0:8, 8:16],
                True, True)
        self.mm(self.ps[1][0:1, 0:8], self.colf[0:8, 5:6], self.colf[0:8, 8:16],
                True, True)
        p.barrier()
        self.load_col(self.inp["comb2_thrbase"], self.colf[0:1, 7:8])
        p.barrier()
        self.cp("vector", g2[0:1, 8:16], self.ps[0][0:1, 0:8])
        p.barrier()
        self.tt("vector", g2[0:1, 0:8], self.ps[1][0:1, 0:8],
                g2[0:1, 8:16], ALU.subtract)
        p.barrier()
        self.tss("vector", g2[0:1, 0:8], g2[0:1, 0:8], self.colf[0:1, 7:8],
                 ALU.add)
        p.barrier()
        p.dma("sync", self.i_small[0:1, 0:8], g2[0:1, 0:8])
        p.barrier()
        p.dma("sync", self.colf[0:8, 6:7], self.i_small[0:1, 0:8])
        p.barrier()
        self.tss("vector", g2[0:8, :], g1[0:8, :], self.colf[0:8, 6:7],
                 ALU.is_lt)
        p.barrier()
        self.tt("vector", g1[0:8, :], g2[0:8, :], g0[0:8, :], ALU.mult)
        p.barrier()
        self.tt("vector", g2[0:8, :], g2[0:8, :], g1[0:8, :], ALU.subtract)
        p.barrier()
        self.tt("vector", g0[0:8, :], g0[0:8, :], g2[0:8, :], ALU.max)
        p.barrier()
        if DBG:
            self.dbg_dump(g1[:, 0:128], 128, name="lt")
            self.dbg_dump(g0[0:8, 0:2048], 2048, name="sel")
            self.dbg_dump(g1[0:8, 0:2048], 2048, name="zexcl_after")
            self.dbg_dump(self.colf[0:8, 6:7], 1, name="thr")
            self.dbg_dump(self.colf[0:8, 2:3], 1, name="npos_loc")
        self.memset("vector", self.resw[0:64, 512:2560], 0.0)
        p.barrier()
        self.cp("vector", self.resw[0:8, 512:2560], g0[0:8, :])
        p.barrier()
        self.load_rows_bf(self.inp["l1_b2"], 8, D,
                          lambda a, b: self.resw2[0:8, a:b])
        self.load_col(self.inp["l1_b1cols"], self.bcolf[:, 0:16])
        p.barrier()
        for half in range(2):
            self.load_weight(self.inp["l1_W1"][:, half * 1024:(half + 1) * 1024],
                             D, 1024, self.wbf, 0)
            self.load_weight(self.inp["l1_W2"][half * 1024:(half + 1) * 1024, :],
                             1024, 1024, self.wbf, 8192)
            for ho in range(8):
                hg = half * 8 + ho
                e = hg // 2
                for s in range(NSUB):
                    sl = slice(s * TS, (s + 1) * TS)
                    for k in range(NK):
                        self.mm(self.ps[s % 4][:, :],
                                self.wbf[:, k * 1024 + ho * 128:
                                          k * 1024 + (ho + 1) * 128],
                                self.xb[k][:, sl], k == 0, k == NK - 1)
                    self.mm(self.ps[4][:, :],
                            self.sel64[:, e * 128:(e + 1) * 128],
                            self.resw[0:64, 512 + s * TS:512 + (s + 1) * TS],
                            True, True)
                    p.barrier()
                    self.act(g1[:, 0:512], self.ps[s % 4][:, :], AF.Gelu,
                             bias=self.bcolf[:, hg:hg + 1])
                    p.barrier()
                    self.tt("vector", self.hslot(ho)[:, sl], g1[:, 0:512],
                            self.ps[4][:, :], ALU.mult)
                    p.barrier()
            if DBG and half == 0:
                self.dbg_dump(self.hslot(0)[:, 0:512], 512, cast=True, name="h0")
                self.dbg_dump(self.hslot(3)[:, 0:512], 512, cast=True, name="h3")
            for o in range(NK):
                for s in range(NSUB):
                    sl = slice(s * TS, (s + 1) * TS)
                    for hc in range(8):
                        self.mm(self.ps[s % 4][:, :],
                                self.wbf[:, 8192 + hc * 1024 + o * 128:
                                          8192 + hc * 1024 + (o + 1) * 128],
                                self.hslot(hc)[:, sl], hc == 0,
                                half == 0 and hc == 7)
                    if half == 1:
                        self.mm(self.ps[s % 4][:, :],
                                self.resw2[0:8, o * 128:(o + 1) * 128],
                                self.resw[0:8, 512 + s * TS:512 + (s + 1) * TS],
                                False, True)
                p.barrier()
                for s in range(NSUB):
                    sl = slice(s * TS, (s + 1) * TS)
                    self.stt("vector", self.xs[o][:, sl], self.ps[s % 4][:, :],
                             0.5, self.xs[o][:, sl], ALU.mult, ALU.add)
                p.barrier()

    def emit_ffn2(self):
        p = self.p
        self.emit_ln(self.xs, self.xb)
        g1, g2 = self.gf[1], self.gf[2]
        self.load_col(self.inp["l2_theta"], self.colf[:, 128:160])
        self.load_col(self.inp["l2_wr"], self.colf[:, 160:192])
        self.load_col(self.inp["l2_wi"], self.colf[:, 192:224])
        p.barrier()
        TWO_PI = 6.2831853071795865
        self.tss("vector", self.colf[:, 224:256], self.colf[:, 128:160],
                 1.0 / TWO_PI, ALU.mult)
        p.barrier()
        self.cp("vector", self.iscr[:, 0:32], self.colf[:, 224:256])
        p.barrier()
        self.cp("vector", self.colf[:, 256:288], self.iscr[:, 0:32])
        p.barrier()
        self.tt("vector", self.colf[:, 288:320], self.colf[:, 224:256],
                self.colf[:, 256:288], ALU.subtract)
        p.barrier()
        self.tss("vector", self.colf[:, 320:352], self.colf[:, 288:320], 0.5,
                 ALU.is_gt)
        p.barrier()
        self.tt("vector", self.colf[:, 256:288], self.colf[:, 256:288],
                self.colf[:, 320:352], ALU.add)
        p.barrier()
        self.tss("vector", self.colf[:, 320:352], self.colf[:, 288:320], -0.5,
                 ALU.is_lt)
        p.barrier()
        self.tt("vector", self.colf[:, 256:288], self.colf[:, 256:288],
                self.colf[:, 320:352], ALU.subtract)
        p.barrier()
        self.stt("vector", self.colf[:, 224:256], self.colf[:, 256:288],
                 -TWO_PI, self.colf[:, 128:160], ALU.mult, ALU.add)
        p.barrier()
        self.ts2("vector", self.colf[:, 224:256], self.colf[:, 224:256],
                 -3.14159165, 3.14159165, ALU.max, ALU.min)
        p.barrier()
        self.act(self.colf[:, 256:288], self.colf[:, 224:256], AF.Sin)
        p.barrier()
        self.tss("vector", self.colf[:, 288:320], self.colf[:, 224:256],
                 1.5707963267948966, ALU.add)
        p.barrier()
        self.tss("vector", self.colf[:, 320:352], self.colf[:, 288:320],
                 3.14159265, ALU.is_gt)
        p.barrier()
        self.stt("vector", self.colf[:, 288:320], self.colf[:, 320:352],
                 -TWO_PI, self.colf[:, 288:320], ALU.mult, ALU.add)
        p.barrier()
        self.act(self.colf[:, 288:320], self.colf[:, 288:320], AF.Sin)
        p.barrier()
        self.tt("vector", self.colf[:, 320:352], self.colf[:, 288:320],
                self.colf[:, 160:192], ALU.mult)
        p.barrier()
        self.stt("vector", self.colf[:, 320:352], self.colf[:, 256:288],
                 1.0, self.colf[:, 320:352], ALU.bypass, ALU.bypass)
        p.barrier()
        self.tt("vector", self.colf[:, 352:384], self.colf[:, 256:288],
                self.colf[:, 192:224], ALU.mult)
        self.tt("gpsimd", self.colf[:, 320:352], self.colf[:, 288:320],
                self.colf[:, 160:192], ALU.mult)
        p.barrier()
        self.tt("vector", self.colf[:, 320:352], self.colf[:, 320:352],
                self.colf[:, 352:384], ALU.add)
        p.barrier()
        self.tt("vector", self.colf[:, 352:384], self.colf[:, 320:352],
                self.colf[:, 192:224], ALU.mult)
        p.barrier()
        self.tss("vector", self.colf[:, 352:384], self.colf[:, 352:384], 0.1,
                 ALU.mult)
        p.barrier()
        self.tt("vector", self.bcolf[:, 32:64], self.colf[:, 352:384],
                self.colf[:, 256:288], ALU.mult)
        self.tt("gpsimd", self.bcolf[:, 64:96], self.colf[:, 352:384],
                self.colf[:, 288:320], ALU.mult)
        p.barrier()
        for quarter in range(4):
            self.load_rows_bf(self.inp["l2_b1row"][:, quarter * 1024:(quarter + 1) * 1024],
                              1, 1024, lambda a, b: self.resw2[0:1, a:b])
            self.load_weight(self.inp["l2_W1"][:, quarter * 1024:(quarter + 1) * 1024],
                             D, 1024, self.wbf, 0)
            self.load_weight(self.inp["l2_W2"][quarter * 1024:(quarter + 1) * 1024, :],
                             1024, 1024, self.wbf, 8192)
            for ho in range(8):
                hg = quarter * 8 + ho
                for s in range(NSUB):
                    sl = slice(s * TS, (s + 1) * TS)
                    for k in range(NK):
                        self.mm(self.ps[s % 4][:, :],
                                self.wbf[:, k * 1024 + ho * 128:
                                          k * 1024 + (ho + 1) * 128],
                                self.xb[k][:, sl], k == 0, False)
                    self.mm(self.ps[s % 4][:, :],
                            self.resw2[0:1, ho * 128:(ho + 1) * 128],
                            self.onesrow, False, True)
                    p.barrier()
                    self.act(g1[:, 0:512], self.ps[s % 4][:, :], AF.Copy)
                    p.barrier()
                    self.tss("vector", g2[:, 0:512], g1[:, 0:512], 0.1, ALU.mult)
                    p.barrier()
                    self.ts2("vector", g2[:, 0:512], g2[:, 0:512],
                             -1.5707963, 1.5707963, ALU.max, ALU.min)
                    p.barrier()
                    self.act(g2[:, 512:1024], g2[:, 0:512], AF.Sin)
                    p.barrier()
                    self.tss("vector", g2[:, 1024:1536], g2[:, 0:512],
                             1.5707963267948966, ALU.add)
                    p.barrier()
                    self.act(g2[:, 1024:1536], g2[:, 1024:1536], AF.Sin)
                    p.barrier()
                    self.stt("vector", g1[:, 0:512], g2[:, 1024:1536],
                             self.bcolf[:, 32 + hg:33 + hg], g1[:, 0:512],
                             ALU.mult, ALU.add)
                    p.barrier()
                    self.stt("vector", g1[:, 0:512], g2[:, 512:1024],
                             self.bcolf[:, 64 + hg:65 + hg], g1[:, 0:512],
                             ALU.mult, ALU.add)
                    p.barrier()
                    self.tss("vector", self.hslot(ho)[:, sl], g1[:, 0:512],
                             0.0, ALU.max)
                    p.barrier()
            if quarter == 3:
                self.load_rows_bf(self.inp["l2_b2row"], 1, D,
                                  lambda a, b: self.resw2[0:1, a:b])
            for o in range(NK):
                for s in range(NSUB):
                    sl = slice(s * TS, (s + 1) * TS)
                    for hc in range(8):
                        self.mm(self.ps[s % 4][:, :],
                                self.wbf[:, 8192 + hc * 1024 + o * 128:
                                          8192 + hc * 1024 + (o + 1) * 128],
                                self.hslot(hc)[:, sl], hc == 0,
                                quarter < 3 and hc == 7)
                    if quarter == 3:
                        self.mm(self.ps[s % 4][:, :],
                                self.resw2[0:1, o * 128:(o + 1) * 128],
                                self.onesrow, False, True)
                p.barrier()
                for s in range(NSUB):
                    sl = slice(s * TS, (s + 1) * TS)
                    self.stt("vector", self.xs[o][:, sl], self.ps[s % 4][:, :],
                             0.5, self.xs[o][:, sl], ALU.mult, ALU.add)
                p.barrier()

    def emit_ffn3(self):
        p = self.p
        self.emit_ln(self.xs, self.xb)
        g0, g1 = self.gf[0], self.gf[1]
        slab = self.resw[:, 512:2560]
        for k in range(NK):
            self.cp("vector", g0[:, k:k + 1], self.xb[k][:, T - 1:T])
        p.barrier()
        p.dma("sync", self.ag3_in, g0[:, 0:8])
        p.barrier()
        self.p.op("gpsimd", lambda q: q.collective_compute(
            "AllGather", ALU.bypass, replica_groups=[[0, 1, 2, 3, 4, 5, 6, 7]],
            ins=[self.ag3_in], outs=[self.ag3_out]))
        p.barrier()
        for r in range(8):
            p.dma("sync", g0[:, 8 + r * 8:16 + r * 8],
                  self.ag3_out[r * 128:(r + 1) * 128, :])
        p.barrier()
        self.dma_bcast_small(
            None, self.colf[:, 384:392], 8, 1,
            src_dram=self.inp["comb3_prev"])
        self.memset("vector", g0[:, 72:80], 0.0)
        p.barrier()
        for r in range(8):
            self.stt("vector", g0[:, 72:80], g0[:, 8 + r * 8:16 + r * 8],
                     self.colf[:, 384 + r:385 + r], g0[:, 72:80],
                     ALU.mult, ALU.add)
            p.barrier()
        self.load_col(self.inp["l3f_mur"], self.bcolf[:, 0:8])
        self.load_col(self.inp["l3f_muk"], self.bcolf[:, 8:16])
        p.barrier()
        self.ts2("vector", self.bcolf[:, 16:24], self.bcolf[:, 0:8],
                 -1.0, 1.0, ALU.mult, ALU.add)
        self.ts2("gpsimd", self.bcolf[:, 24:32], self.bcolf[:, 8:16],
                 -1.0, 1.0, ALU.mult, ALU.add)
        p.barrier()
        for k in range(NK):
            self.cp("vector", slab[:, 1:T], self.xb[k][:, 0:T - 1])
            self.cp("gpsimd", slab[:, 0:1], g0[:, 72 + k:73 + k])
            p.barrier()
            self.tss("vector", self.hslot(k)[:], self.xb[k][:],
                     self.bcolf[:, k:k + 1] if False else self.bcolf[:, 0:1]
                     if False else self.bcolf[:, 0:8][:, k:k + 1], ALU.mult)
            p.barrier()
            self.stt("vector", self.hslot(k)[:], slab[:],
                     self.bcolf[:, 16 + k:17 + k] if False else
                     self.bcolf[:, 16:24][:, k:k + 1],
                     self.hslot(k)[:], ALU.mult, ALU.add)
            p.barrier()
            self.tss("vector", slab[:], slab[:],
                     self.bcolf[:, 24:32][:, k:k + 1], ALU.mult)
            p.barrier()
            self.stt("vector", slab[:], self.xb[k][:],
                     self.bcolf[:, 8:16][:, k:k + 1], slab[:],
                     ALU.mult, ALU.add)
            p.barrier()
            p.dma("sync", self.i_k[:, k * T:(k + 1) * T], slab[:])
            p.barrier()
        self.load_weight("l3f_wr", D, D, self.wbf, 0)
        for o in range(NK):
            for s in range(NSUB):
                sl = slice(s * TS, (s + 1) * TS)
                for k in range(NK):
                    self.mm(self.ps[s % 4][:, :],
                            self.wbf[:, k * D + o * 128:k * D + (o + 1) * 128],
                            self.hslot(k)[:, sl], k == 0, k == NK - 1)
            p.barrier()
            for s in range(NSUB):
                self.act(self.xb[o][:, s * TS:(s + 1) * TS],
                         self.ps[s % 4][:, :], AF.Sigmoid)
            p.barrier()
        for k in range(NK):
            p.dma("sync", self.hslot(k)[:], self.i_k[:, k * T:(k + 1) * T])
        p.barrier()
        self.load_weight("l3f_wk", D, D, self.wbf, 0)
        for o in range(NK):
            for s in range(NSUB):
                sl = slice(s * TS, (s + 1) * TS)
                for k in range(NK):
                    self.mm(self.ps[s % 4][:, :],
                            self.wbf[:, k * D + o * 128:k * D + (o + 1) * 128],
                            self.hslot(k)[:, sl], k == 0, k == NK - 1)
            p.barrier()
            for s in range(NSUB):
                sl = slice(s * TS, (s + 1) * TS)
                self.tss("vector", g1[:, 0:512], self.ps[s % 4][:, :], 0.0,
                         ALU.max)
                p.barrier()
                self.tt("vector", slab[:, sl], g1[:, 0:512], g1[:, 0:512],
                        ALU.mult)
                p.barrier()
            p.dma("sync", self.i_v[:, o * T:(o + 1) * T], slab[:])
            p.barrier()
        for k in range(NK):
            p.dma("sync", self.hslot(k)[:], self.i_v[:, k * T:(k + 1) * T])
        p.barrier()
        self.load_weight("l3f_wv", D, D, self.wbf, 0)
        for o in range(NK):
            for s in range(NSUB):
                sl = slice(s * TS, (s + 1) * TS)
                for k in range(NK):
                    self.mm(self.ps[s % 4][:, :],
                            self.wbf[:, k * D + o * 128:k * D + (o + 1) * 128],
                            self.hslot(k)[:, sl], k == 0, k == NK - 1)
            p.barrier()
            for s in range(NSUB):
                sl = slice(s * TS, (s + 1) * TS)
                self.tt("vector", g1[:, 0:512], self.ps[s % 4][:, :],
                        self.xb[o][:, sl], ALU.mult)
                p.barrier()
                self.stt("vector", self.xs[o][:, sl], g1[:, 0:512], 0.5,
                         self.xs[o][:, sl], ALU.mult, ALU.add)
                p.barrier()

    def xyslot(self, i):
        return self.hslot(i) if i < 8 else self.xb[i - 8]

    def emit_ln2_affine(self, xb, gcol, bcol):
        p = self.p
        mu_bf, rstd_bf = self.gb[4], self.gb[5]
        sq = [self.gb[2], self.gb[3]]
        for k in range(NK):
            for s in range(NSUB):
                self.mm(self.ps[s][:, :], self.ones_bf,
                        xb[k][:, s * TS:(s + 1) * TS], k == 0, k == NK - 1)
        self.tt("vector", sq[0][:], xb[0][:], xb[0][:], ALU.mult)
        p.barrier()
        for s in range(NSUB):
            sl = slice(s * TS, (s + 1) * TS)
            self.act(self.gf[0][:, sl], self.ps[s][:, :], AF.Copy, scale=1.0 / D)
        p.barrier()
        for s in range(NSUB):
            sl = slice(s * TS, (s + 1) * TS)
            self.cp("scalar", mu_bf[:, sl], self.gf[0][:, sl])
        vbank = [self.ps[4], self.ps[5], self.ps[0], self.ps[1]]
        for k in range(NK):
            for s in range(NSUB):
                self.mm(vbank[s][:, :], self.ones_bf,
                        sq[k % 2][:, s * TS:(s + 1) * TS], k == 0, k == NK - 1)
            if k + 1 < NK:
                self.tt("vector", sq[(k + 1) % 2][:], xb[k + 1][:],
                        xb[k + 1][:], ALU.mult)
            p.barrier()
        for s in range(NSUB):
            sl = slice(s * TS, (s + 1) * TS)
            self.ts2("vector", self.gf[1][:, sl], vbank[s][:, :],
                     1.0 / D, LN_EPS, ALU.mult, ALU.add)
        self.stt("vector", self.gf[2][:], self.gf[0][:], -1.0, self.gf[0][:],
                 ALU.mult, ALU.mult)
        p.barrier()
        self.tt("vector", self.gf[1][:], self.gf[1][:], self.gf[2][:], ALU.add)
        p.barrier()
        self.recip(self.gf[1][:], self.gf[1][:])
        p.barrier()
        self.act(rstd_bf[:], self.gf[1][:], AF.Sqrt)
        p.barrier()
        for k in range(NK):
            eng = ("vector", "gpsimd")[k % 2]
            self.tt(eng, xb[k][:], xb[k][:], mu_bf[:], ALU.subtract)
        p.barrier()
        for k in range(NK):
            eng = ("vector", "gpsimd")[k % 2]
            self.tt(eng, xb[k][:], xb[k][:], rstd_bf[:], ALU.mult)
        p.barrier()
        for k in range(NK):
            eng = ("vector", "gpsimd")[k % 2]
            self.ts2(eng, xb[k][:], xb[k][:], gcol[:, k:k + 1], bcol[:, k:k + 1],
                     ALU.mult, ALU.add)
        p.barrier()

    def emit_conv4(self, slot, prev3, dc):
        p = self.p
        slab = self.resw[:, 512:2560]
        wc = lambda j: self.colf[:, 384 + dc * 4 + j:385 + dc * 4 + j]
        self.tss("vector", slab[:], slot[:], wc(3), ALU.mult)
        p.barrier()
        for j, off in ((2, 1), (1, 2), (0, 3)):
            self.stt("vector", slab[:, off:T], slot[:, 0:T - off], wc(j),
                     slab[:, off:T], ALU.mult, ALU.add)
            p.barrier()
        for t in range(3):
            for j in range(3 - t):
                self.stt("vector", slab[:, t:t + 1], prev3[:, t + j:t + j + 1],
                         wc(j), slab[:, t:t + 1], ALU.mult, ALU.add)
            p.barrier()
        self.p.op("scalar", lambda q: q.activation(
            slot[:], slab[:], AF.Silu, bias=self.bcolf[:, 64 + dc:65 + dc]))
        p.barrier()

    def _mamba_gather_prev(self, tag):
        p = self.p
        p.dma("sync", self.ag4_in[:, 0:24], self.gf[0][:, 0:24])
        p.barrier()
        self.p.op("gpsimd", lambda q: q.collective_compute(
            "AllGather", ALU.bypass, replica_groups=[[0, 1, 2, 3, 4, 5, 6, 7]],
            ins=[self.ag4_in], outs=[self.ag4_out]))
        p.barrier()
        for r in range(8):
            p.dma("sync", self.gf[0][:, 24 + r * 24:48 + r * 24],
                  self.ag4_out[r * 128:(r + 1) * 128, 0:24])
        p.barrier()
        self.memset("vector", self.gf[0][:, 0:24], 0.0)
        p.barrier()
        for r in range(8):
            self.stt("vector", self.gf[0][:, 0:24],
                     self.gf[0][:, 24 + r * 24:48 + r * 24],
                     self.colf[:, 448 + r:449 + r], self.gf[0][:, 0:24],
                     ALU.mult, ALU.add)
            p.barrier()
        p.barrier()

    def emit_mamba(self):
        p = self.p
        TH = 1024
        g0, g1, g2 = self.gf[0], self.gf[1], self.gf[2]
        slab = self.resw[:, 512:2560]
        self.emit_ln(self.xs, self.xb)
        for k in range(NK):
            p.dma("sync", self.i_k[:, k * T:(k + 1) * T], self.xb[k][:])
        p.barrier()
        self.load_col(self.inp["l2a_lng"], self.bcolf[:, 0:8])
        self.load_col(self.inp["l2a_lnb"], self.bcolf[:, 8:16])
        p.barrier()
        self.emit_ln2_affine(self.xb, self.bcolf[:, 0:8], self.bcolf[:, 8:16])
        self.load_col(self.inp["l2a_convw"], self.colf[:, 384:448])
        self.load_col(self.inp["l2a_wdt"], self.bcolf[:, 16:32])
        self.load_col(self.inp["l2a_bdt"], self.bcolf[:, 32:48])
        self.load_col(self.inp["l2a_alog"], self.colf[:, 128:384])
        self.load_col(self.inp["l2a_dpar"], self.bcolf[:, 48:64])
        self.load_col(self.inp["l2a_convb"], self.bcolf[:, 64:80])
        self.dma_bcast_small(None, self.colf[:, 448:456], 8, 1,
                             src_dram=self.inp["comb3_prev"])
        p.barrier()
        self.act(self.colf[:, 128:384], self.colf[:, 128:384], AF.Exp)
        p.barrier()
        self.tss("vector", self.colf[:, 128:384], self.colf[:, 128:384], -1.0,
                 ALU.mult)
        p.barrier()
        for half in range(2):
            self.load_weight(self.inp["l2a_Win"][:, half * 1024:(half + 1) * 1024],
                             D, 1024, self.wbf, 0)
            for i in range(8):
                dc = half * 8 + i
                for s in range(NSUB):
                    sl = slice(s * TS, (s + 1) * TS)
                    for k in range(NK):
                        self.mm(self.ps[s % 4][:, :],
                                self.wbf[:, k * 1024 + i * 128:
                                          k * 1024 + (i + 1) * 128],
                                self.xb[k][:, sl], k == 0, k == NK - 1)
                p.barrier()
                for s in range(NSUB):
                    self.cp("scalar", self.hslot(i)[:, s * TS:(s + 1) * TS],
                            self.ps[s % 4][:, :])
                p.barrier()
            for i in range(8):
                self.cp("vector", g0[:, i * 3:i * 3 + 3],
                        self.hslot(i)[:, T - 3:T])
            p.barrier()
            self._mamba_gather_prev("cv%d" % half)
            for i in range(8):
                dc = half * 8 + i
                self.emit_conv4(self.hslot(i), g0[:, i * 3:i * 3 + 3], dc)
                p.dma("sync", self.i_xc[:, dc * T:(dc + 1) * T], self.hslot(i)[:])
            p.barrier()
        for zh in range(2):
            self.load_weight(
                self.inp["l2a_Win"][:, 2048 + zh * 1024:2048 + (zh + 1) * 1024],
                D, 1024, self.wbf, 0)
            for i in range(8):
                dc = zh * 8 + i
                for s in range(NSUB):
                    sl = slice(s * TS, (s + 1) * TS)
                    for k in range(NK):
                        self.mm(self.ps[s % 4][:, :],
                                self.wbf[:, k * 1024 + i * 128:
                                          k * 1024 + (i + 1) * 128],
                                self.xb[k][:, sl], k == 0, k == NK - 1)
                p.barrier()
                for s in range(NSUB):
                    self.cp("scalar", slab[:, s * TS:(s + 1) * TS],
                            self.ps[s % 4][:, :])
                p.barrier()
                p.dma("sync", self.i_z[:, dc * T:(dc + 1) * T], slab[:])
                p.barrier()
        self.load_weight("l2a_Wx", 2048, 33, self.resw2, 0)
        for gdc in range(2):
            for i in range(8):
                dc = gdc * 8 + i
                p.dma("sync", self.hslot(i)[:], self.i_xc[:, dc * T:(dc + 1) * T])
            p.barrier()
            for s in range(NSUB):
                sl = slice(s * TS, (s + 1) * TS)
                for i in range(8):
                    c = gdc * 8 + i
                    self.mm(self.ps[s][0:1, :], self.resw2[:, c * 33:c * 33 + 1],
                            self.hslot(i)[:, sl], gdc == 0 and i == 0,
                            gdc == 1 and i == 7)
                    self.mm(self.ps[s][32:48, :],
                            self.resw2[:, c * 33 + 1:c * 33 + 17],
                            self.hslot(i)[:, sl], gdc == 0 and i == 0,
                            gdc == 1 and i == 7)
                    self.mm(self.ps[s][64:80, :],
                            self.resw2[:, c * 33 + 17:c * 33 + 33],
                            self.hslot(i)[:, sl], gdc == 0 and i == 0,
                            gdc == 1 and i == 7)
            p.barrier()
        for s in range(NSUB):
            sl = slice(s * TS, (s + 1) * TS)
            self.cp("scalar", g0[0:1, sl], self.ps[s][0:1, :])
            self.cp("vector", slab[0:16, sl], self.ps[s][32:48, :])
            self.cp("scalar", self.wbf[0:16, sl], self.ps[s][64:80, :])
        p.barrier()
        p.dma("sync", self.i_bcB, slab[0:16, 0:T])
        p.dma("sync", self.i_bcC, self.wbf[0:16, 0:T])
        p.dma("sync", self.i_bcD, g0[0:1, 0:T])
        p.barrier()
        self.memset("vector", g2[:, 0:256], 0.0)
        p.barrier()
        for half in range(2):
            hsl = slice(half * TH, (half + 1) * TH)
            p.dma("sync", g1[:, 0:TH],
                  self.free_bcast(self.i_bcD[0:1, hsl], 1) if False else
                  self.i_bcD[0:1, hsl].partition_broadcast(128))
            p.barrier()
            for dc in range(16):
                xcs = self.hslot(dc // 2)[:, (dc % 2) * TH:(dc % 2 + 1) * TH]
                p.dma("sync", xcs, self.i_xc[:, dc * T + half * TH:
                                             dc * T + (half + 1) * TH])
            p.barrier()
            for dc in range(16):
                dts = self.xb[dc // 2][:, (dc % 2) * TH:(dc % 2 + 1) * TH]
                xcs = self.hslot(dc // 2)[:, (dc % 2) * TH:(dc % 2 + 1) * TH]
                self.p.op("scalar", lambda q, dts=dts, dc=dc:
                          q.activation(g2[:, 1024:1024 + TH], g1[:, 0:TH], AF.Exp,
                                       scale=self.bcolf[:, 16 + dc:17 + dc],
                                       bias=self.bcolf[:, 32 + dc:33 + dc]))
                p.barrier()
                self.tss("vector", g2[:, 1024:1024 + TH], g2[:, 1024:1024 + TH],
                         1.0, ALU.add)
                p.barrier()
                self.act(g2[:, 1024:1024 + TH], g2[:, 1024:1024 + TH], AF.Ln)
                p.barrier()
                self.cp("vector", dts, g2[:, 1024:1024 + TH])
                p.barrier()
                self.tt("vector", xcs, xcs, dts, ALU.mult)
                p.barrier()
            for gdc in range(2):
                for s in range(16):
                    p.dma("sync", slab[:, 0:TH],
                          self.i_bcB[s:s + 1, hsl].partition_broadcast(128))
                    p.dma("gpsimd", slab[:, 1024:1024 + TH],
                          self.i_bcC[s:s + 1, hsl].partition_broadcast(128))
                    p.barrier()
                    for i in range(8):
                        dc = gdc * 8 + i
                        dts = self.xb[dc // 2][:, (dc % 2) * TH:(dc % 2 + 1) * TH]
                        xcs = self.hslot(dc // 2)[:, (dc % 2) * TH:(dc % 2 + 1) * TH]
                        dA = self.wbf[:, 0:TH]
                        dBx = self.wbf[:, TH:2 * TH]
                        hh = self.wbf[:, 2 * TH:3 * TH]
                        ysl = self.wbf[:, 8192 + i * TH:8192 + (i + 1) * TH]
                        acol = self.colf[:, 128 + dc * 16 + s:129 + dc * 16 + s]
                        hcol = g2[:, dc * 16 + s:dc * 16 + s + 1]
                        self.p.op("scalar", lambda q, dts=dts, dA=dA, acol=acol:
                                  q.activation(dA, dts, AF.Exp, scale=acol))
                        p.barrier()
                        self.tt("vector", dBx, xcs, slab[:, 0:TH], ALU.mult)
                        p.barrier()
                        self.scan("vector", hh, dA, dBx, hcol, ALU.mult, ALU.add)
                        p.barrier()
                        self.cp("vector", hcol, hh[:, TH - 1:TH])
                        self.tt("gpsimd", dBx, hh, slab[:, TH:2 * TH], ALU.mult)
                        p.barrier()
                        if s == 0:
                            self.cp("vector", ysl, dBx)
                        else:
                            self.tt("vector", ysl, ysl, dBx, ALU.add)
                        p.barrier()
                    p.barrier()
                for i in range(8):
                    dc = gdc * 8 + i
                    p.dma("sync",
                          self.i_o[:, (dc * 2 + half) * TH:(dc * 2 + half + 1) * TH],
                          self.wbf[:, 8192 + i * TH:8192 + (i + 1) * TH])
                p.barrier()
        p.dma("sync", self.ag4_in[:, 0:256], g2[:, 0:256])
        p.barrier()
        self.p.op("gpsimd", lambda q: q.collective_compute(
            "AllGather", ALU.bypass, replica_groups=[[0, 1, 2, 3, 4, 5, 6, 7]],
            ins=[self.ag4_in], outs=[self.ag4_out]))
        p.barrier()
        self.memset("vector", g2[:, 256:512], 0.0)
        p.barrier()
        for r in range(8):
            p.dma("sync", g2[:, 512:768], self.ag4_out[r * 128:(r + 1) * 128, :])
            p.barrier()
            self.stt("vector", g2[:, 256:512], g2[:, 512:768],
                     self.colf[:, 448 + r:449 + r], g2[:, 256:512],
                     ALU.mult, ALU.add)
            p.barrier()
        ycorr = self.wbf[:, 3072:4096]
        CW = 64
        self.memset("vector", ycorr, 0.0)
        p.barrier()
        p.dma("sync", g1[:, 0:CW],
              self.i_bcD[0:1, 0:CW].partition_broadcast(128))
        p.barrier()
        for dc in range(16):
            self.p.op("scalar", lambda q, dc=dc: q.activation(
                g2[:, 1024:1024 + CW], g1[:, 0:CW], AF.Exp,
                scale=self.bcolf[:, 16 + dc:17 + dc],
                bias=self.bcolf[:, 32 + dc:33 + dc]))
            p.barrier()
            self.tss("vector", g2[:, 1024:1024 + CW], g2[:, 1024:1024 + CW],
                     1.0, ALU.add)
            p.barrier()
            self.act(g2[:, 1024:1024 + CW], g2[:, 1024:1024 + CW], AF.Ln)
            p.barrier()
            for s in range(16):
                acol = self.colf[:, 128 + dc * 16 + s:129 + dc * 16 + s]
                hin = g2[:, 256 + dc * 16 + s:257 + dc * 16 + s]
                self.p.op("scalar", lambda q, acol=acol: q.activation(
                    self.wbf[:, 0:CW], g2[:, 1024:1024 + CW], AF.Exp, scale=acol))
                p.dma("sync", slab[:, 0:CW],
                      self.i_bcC[s:s + 1, 0:CW].partition_broadcast(128))
                p.barrier()
                self.scan("vector", self.wbf[:, TH:TH + CW], self.wbf[:, 0:CW],
                          self.wbf[:, 0:CW], 1.0, ALU.mult, ALU.bypass)
                p.barrier()
                self.tss("vector", self.wbf[:, TH:TH + CW], self.wbf[:, TH:TH + CW],
                         hin, ALU.mult)
                p.barrier()
                self.tt("vector", self.wbf[:, TH:TH + CW], self.wbf[:, TH:TH + CW],
                        slab[:, 0:CW], ALU.mult)
                p.barrier()
                self.tt("vector", ycorr[:, dc * CW:(dc + 1) * CW],
                        ycorr[:, dc * CW:(dc + 1) * CW],
                        self.wbf[:, TH:TH + CW], ALU.add)
                p.barrier()
        for dc in range(16):
            sl2 = self.xyslot(dc)
            for half in range(2):
                p.dma("sync", sl2[:, half * TH:(half + 1) * TH],
                      self.i_o[:, (dc * 2 + half) * TH:(dc * 2 + half + 1) * TH])
            p.barrier()
            self.tt("vector", sl2[:, 0:CW], sl2[:, 0:CW],
                    ycorr[:, dc * CW:(dc + 1) * CW], ALU.add)
            p.barrier()
            p.dma("sync", slab[:], self.i_xc[:, dc * T:(dc + 1) * T])
            p.barrier()
            self.stt("vector", sl2[:], slab[:], self.bcolf[:, 48 + dc:49 + dc],
                     sl2[:], ALU.mult, ALU.add)
            p.barrier()
            p.dma("sync", slab[:], self.i_z[:, dc * T:(dc + 1) * T])
            p.barrier()
            self.p.op("scalar", lambda q: q.activation(slab[:], slab[:], AF.Silu))
            p.barrier()
            self.tt("vector", sl2[:], sl2[:], slab[:], ALU.mult)
            p.barrier()
        self.load_weight("l2a_Wout", 2048, D, self.wbf, 0)
        for o in range(NK):
            p.dma("sync", slab[:], self.i_k[:, o * T:(o + 1) * T])
            p.barrier()
            for s in range(NSUB):
                sl = slice(s * TS, (s + 1) * TS)
                for k in range(16):
                    self.mm(self.ps[s % 4][:, :],
                            self.wbf[:, k * D + o * 128:k * D + (o + 1) * 128],
                            self.xyslot(k)[:, sl], k == 0, k == 15)
            p.barrier()
            for s in range(NSUB):
                sl = slice(s * TS, (s + 1) * TS)
                self.tt("vector", g1[:, 0:512], self.ps[s % 4][:, :],
                        slab[:, sl], ALU.add)
                p.barrier()
                self.tt("vector", self.xs[o][:, sl], self.xs[o][:, sl],
                        g1[:, 0:512], ALU.add)
                p.barrier()
    def _shift_mix(self, dest, srcpool, k, mixcol, prevcol):
        p = self.p
        slab = self.resw[:, 512:2560]
        self.cp("vector", slab[:, 1:T], srcpool[k][:, 0:T - 1])
        self.cp("gpsimd", slab[:, 0:1], prevcol)
        p.barrier()
        self.tss("vector", dest[:], srcpool[k][:], mixcol, ALU.mult)
        p.barrier()
        self.stt("vector", dest[:], slab[:], mixcol, dest[:], ALU.mult,
                 ALU.subtract_rev if False else ALU.add)
        p.barrier()
        self.tt("vector", dest[:], dest[:], slab[:], ALU.add)
        p.barrier()

    def shift_mix(self, dest, srcpool, k, mixcol, prevcol):
        p = self.p
        slab = self.resw[:, 512:2560]
        self.cp("vector", slab[:, 1:T], srcpool[k][:, 0:T - 1])
        self.cp("gpsimd", slab[:, 0:1], prevcol)
        p.barrier()
        self.tt("vector", slab[:], slab[:], srcpool[k][:], ALU.subtract)
        p.barrier()
        self.ts2("vector", slab[:], slab[:], -1.0, 0.0, ALU.mult, ALU.add)
        p.barrier()
        self.tss("vector", slab[:], slab[:], mixcol, ALU.mult)
        p.barrier()
        self.cp("vector", dest[:], slab[:])
        p.barrier()
        self.cp("gpsimd", slab[:, 1:T], srcpool[k][:, 0:T - 1])
        self.cp("vector", slab[:, 0:1], prevcol)
        p.barrier()
        self.tt("vector", dest[:], dest[:], slab[:], ALU.add)
        p.barrier()

    def gather_lastcols(self, srcpool, width=8):
        p = self.p
        g0 = self.gf[0]
        for k in range(NK):
            self.cp("vector", g0[:, k:k + 1], srcpool[k][:, T - 1:T])
        p.barrier()
        p.dma("sync", self.ag3_in, g0[:, 0:8])
        p.barrier()
        self.p.op("gpsimd", lambda q: q.collective_compute(
            "AllGather", ALU.bypass, replica_groups=[[0, 1, 2, 3, 4, 5, 6, 7]],
            ins=[self.ag3_in], outs=[self.ag3_out]))
        p.barrier()
        for r in range(8):
            p.dma("sync", g0[:, 8 + r * 8:16 + r * 8],
                  self.ag3_out[r * 128:(r + 1) * 128, :])
        p.barrier()
        self.memset("vector", g0[:, 72:80], 0.0)
        p.barrier()
        for r in range(8):
            self.stt("vector", g0[:, 72:80], g0[:, 8 + r * 8:16 + r * 8],
                     self.colf[:, 448 + r:449 + r], g0[:, 72:80],
                     ALU.mult, ALU.add)
        p.barrier()

    def emit_l3_attn(self):
        p = self.p
        g0, g1, g2 = self.gf[0], self.gf[1], self.gf[2]
        slab = self.resw[:, 512:2560]
        EPS2 = 1e-8
        self.emit_ln(self.xs, self.xb)
        for k in range(NK):
            p.dma("sync", self.i_k[:, k * T:(k + 1) * T], self.xb[k][:])
        p.barrier()
        self.load_col(self.inp["l3_ln1g"], self.bcolf[:, 0:8])
        self.load_col(self.inp["l3_ln1b"], self.bcolf[:, 8:16])
        self.load_col(self.inp["l3_mixk"], self.bcolf[:, 16:24])
        self.load_col(self.inp["l3_mixv"], self.bcolf[:, 24:32])
        self.load_col(self.inp["l3_mixr"], self.bcolf[:, 32:40])
        self.load_col(self.inp["l3_tf"], self.bcolf[:, 40:48])
        self.load_col(self.inp["l3_ln2g"], self.bcolf[:, 48:56])
        self.load_col(self.inp["l3_ln2b"], self.bcolf[:, 56:64])
        self.load_col(self.inp["l3_cmixk"], self.bcolf[:, 64:72])
        self.load_col(self.inp["l3_cmixr"], self.bcolf[:, 72:80])
        self.dma_bcast_small(None, self.colf[:, 448:456], 8, 1,
                             src_dram=self.inp["comb3_prev"])
        p.barrier()
        self.emit_ln2_affine(self.xb, self.bcolf[:, 0:8], self.bcolf[:, 8:16])
        self.gather_lastcols(self.xb)
        for name, mix_off, spill, sig in (("l3_wkt", 16, self.i_o, False),
                                          ("l3_wvt", 24, None, False),
                                          ("l3_wrt", 32, self.i_v, True)):
            for k in range(NK):
                self.shift_mix(self.hslot(k), self.xb, k,
                               self.bcolf[:, mix_off + k:mix_off + k + 1],
                               g0[:, 72 + k:73 + k])
            self.load_weight(name, D, D, self.wbf, 0)
            for o in range(NK):
                for s in range(NSUB):
                    sl = slice(s * TS, (s + 1) * TS)
                    for k in range(NK):
                        self.mm(self.ps[s % 4][:, :],
                                self.wbf[:, k * D + o * 128:k * D + (o + 1) * 128],
                                self.hslot(k)[:, sl], k == 0, k == NK - 1)
                p.barrier()
                for s in range(NSUB):
                    sl = slice(s * TS, (s + 1) * TS)
                    if sig:
                        self.act(slab[:, sl], self.ps[s % 4][:, :], AF.Sigmoid)
                    else:
                        self.cp("scalar", slab[:, sl], self.ps[s % 4][:, :])
                p.barrier()
                if DBG and o == 0:
                    self.dbg_dump(slab[:, 0:256], 256, cast=True,
                                  name=name + "_c0")
                dst = spill if spill is not None else self.i_o
                off = (o * T) if name != "l3_wvt" else (8192 * 2 + o * T)
                if name == "l3_wkt":
                    p.dma("sync", self.i_o[:, o * T:(o + 1) * T], slab[:])
                elif name == "l3_wvt":
                    p.dma("sync", self.i_o[:, 16384 + o * T:16384 + (o + 1) * T],
                          slab[:])
                else:
                    p.dma("sync", self.i_v[:, o * T:(o + 1) * T], slab[:])
                p.barrier()
        EPSW = 1e-08
        LOG1PEPS = 9.99999993922529e-09
        for o in range(NK):
            p.dma("sync", self.gb[0][:], self.i_o[:, o * T:(o + 1) * T])
            p.barrier()
            self.act(self.gb[0][:], self.gb[0][:], AF.Exp)
            p.barrier()
            self.reduce("vector", g2[:, 512 + o:513 + o], self.gb[0][:], ALU.add)
            p.barrier()
        p.dma("sync", self.ag3_in, g2[:, 512:520])
        p.barrier()
        self.p.op("gpsimd", lambda q: q.collective_compute(
            "AllGather", ALU.bypass, replica_groups=[[0, 1, 2, 3, 4, 5, 6, 7]],
            ins=[self.ag3_in], outs=[self.ag3_out]))
        p.barrier()
        for r in range(8):
            p.dma("sync", g0[:, 8 + r * 8:16 + r * 8],
                  self.ag3_out[r * 128:(r + 1) * 128, :])
        self.load_col(self.inp["comb2_prec"], self.colf[0:8, 30:31])
        p.barrier()
        self.dma_bcast_small(self.colf[0:8, 30:31], self.colf[:, 440:448], 8, 1)
        self.memset("vector", g2[:, 520:528], 0.0)
        p.barrier()
        for r in range(8):
            self.stt("vector", g2[:, 520:528], g0[:, 8 + r * 8:16 + r * 8],
                     self.colf[:, 440 + r:441 + r], g2[:, 520:528],
                     ALU.mult, ALU.add)
        p.barrier()
        for o in range(NK):
            p.dma("sync", self.gb[0][:], self.i_o[:, o * T:(o + 1) * T])
            p.barrier()
            self.act(self.gb[0][:], self.gb[0][:], AF.Exp)
            p.barrier()
            self.scan("vector", self.gb[1][:], self.gb[0][:], self.gb[0][:],
                      g2[:, 520 + o:521 + o], ALU.add, ALU.bypass)
            p.barrier()
            if DBG and o == 0:
                self.dbg_dump(self.gb[0][:, 0:256], 256, cast=True, name="ek0")
                self.dbg_dump(self.gb[1][:, 0:256], 256, cast=True, name="Zap0")
            self.cp("vector", self.gb[2][:, 1:T], self.gb[1][:, 0:T - 1])
            self.cp("gpsimd", self.gb[2][:, 0:1], g2[:, 520 + o:521 + o])
            p.barrier()
            self.tt("vector", self.gb[3][:], self.gb[2][:], self.gb[0][:],
                    ALU.is_ge)
            p.barrier()
            self.reduce("vector", g2[:, 528 + o:529 + o], self.gb[3][:], ALU.add)
            self.ts2("gpsimd", self.gb[4][:], self.gb[3][:], EPSW, 1.0,
                     ALU.mult, ALU.add)
            p.barrier()
            self.ts2("vector", self.gb[3][:], self.gb[3][:], -EPSW, 1.0 + EPSW,
                     ALU.mult, ALU.add)
            p.barrier()
            self.tt("vector", self.gb[3][:], self.gb[0][:], self.gb[3][:],
                    ALU.mult)
            p.barrier()
            self.scan("vector", self.gb[1][:], self.gb[4][:], self.gb[3][:],
                      0.0, ALU.mult, ALU.add)
            p.barrier()
            if DBG and o == 0:
                self.dbg_dump(self.gb[1][:, 0:256], 256, cast=True,
                              name="Zexloc0")
            self.scan("vector", self.gb[5][:], self.gb[4][:], self.gb[4][:],
                      1.0, ALU.mult, ALU.bypass)
            p.barrier()
            self.cp("vector", g2[:, 536 + o:537 + o], self.gb[1][:, T - 1:T])
            p.dma("sync", self.i_xc[:, o * T:(o + 1) * T], self.gb[1][:])
            p.dma("gpsimd", self.i_z[:, o * T:(o + 1) * T], self.gb[5][:])
            p.barrier()
        self.p.op("scalar", lambda q: q.activation(
            g2[:, 528:536], g2[:, 528:536], AF.Exp, scale=LOG1PEPS))
        p.barrier()
        self.cp("vector", g0[:, 0:8], g2[:, 536:544])
        self.cp("gpsimd", g0[:, 80:88], g2[:, 528:536])
        p.barrier()
        p.dma("sync", self.ag3_in, g0[:, 0:8])
        p.barrier()
        self.p.op("gpsimd", lambda q: q.collective_compute(
            "AllGather", ALU.bypass, replica_groups=[[0, 1, 2, 3, 4, 5, 6, 7]],
            ins=[self.ag3_in], outs=[self.ag3_out]))
        p.barrier()
        for r in range(8):
            p.dma("sync", g0[:, 8 + r * 8:16 + r * 8],
                  self.ag3_out[r * 128:(r + 1) * 128, :])
        p.barrier()
        p.dma("sync", self.ag3_in, g0[:, 80:88])
        p.barrier()
        self.p.op("gpsimd", lambda q: q.collective_compute(
            "AllGather", ALU.bypass, replica_groups=[[0, 1, 2, 3, 4, 5, 6, 7]],
            ins=[self.ag3_in], outs=[self.ag3_out]))
        p.barrier()
        for r in range(8):
            p.dma("sync", g0[:, 88 + r * 8:96 + r * 8],
                  self.ag3_out[r * 128:(r + 1) * 128, :])
        self.load_col(self.inp["comb_alpha"], self.colf[0:8, 24:25])
        self.load_col(self.inp["comb_gamma"], self.colf[0:8, 25:26])
        self.load_col(self.inp["comb_a"], self.colf[0:8, 26:27])
        p.barrier()
        self.dma_bcast_small(self.colf[0:8, 24:27], self.colf[:, 416:440], 8, 3)
        self.memset("vector", g2[:, 544:552], 0.0)
        p.barrier()
        for r in range(8):
            self.tss("vector", g0[:, 160:168], g0[:, 88 + r * 8:96 + r * 8],
                     self.colf[:, 416 + r * 3:417 + r * 3], ALU.mult)
            p.barrier()
            self.tss("vector", g0[:, 160:168], g0[:, 160:168],
                     self.colf[:, 417 + r * 3:418 + r * 3], ALU.add)
            p.barrier()
            self.tt("vector", g2[:, 544:552], g2[:, 544:552], g0[:, 160:168],
                    ALU.mult)
            p.barrier()
            self.stt("vector", g2[:, 544:552], g0[:, 8 + r * 8:16 + r * 8],
                     self.colf[:, 418 + r * 3:419 + r * 3], g2[:, 544:552],
                     ALU.mult, ALU.add)
            p.barrier()
        if DBG:
            self.dbg_dump(g2[:, 520:528], 8, name="Zinap")
            self.dbg_dump(g2[:, 544:552], 8, name="Zinex")
        for o in range(NK):
            p.dma("sync", self.gb[0][:], self.i_o[:, o * T:(o + 1) * T])
            p.dma("gpsimd", self.gb[1][:], self.i_xc[:, o * T:(o + 1) * T])
            p.barrier()
            self.act(self.gb[0][:], self.gb[0][:], AF.Exp)
            p.barrier()
            p.dma("sync", self.gb[5][:], self.i_z[:, o * T:(o + 1) * T])
            p.barrier()
            self.stt("vector", self.gb[1][:], self.gb[5][:],
                     g2[:, 544 + o:545 + o], self.gb[1][:], ALU.mult, ALU.add)
            p.barrier()
            self.cp("vector", self.gb[2][:, 1:T], self.gb[1][:, 0:T - 1])
            self.cp("gpsimd", self.gb[2][:, 0:1], g2[:, 544 + o:545 + o])
            p.barrier()
            self.tt("vector", self.gb[3][:], self.gb[2][:], self.gb[0][:],
                    ALU.max)
            p.barrier()
            self.recip(g1[:, 0:T], self.gb[3][:])
            p.barrier()
            self.tt("vector", self.gb[4][:], self.gb[2][:], g1[:, 0:T],
                    ALU.mult)
            p.barrier()
            self.tt("vector", self.gb[3][:], self.gb[0][:], g1[:, 0:T],
                    ALU.mult)
            p.barrier()
            p.dma("sync", self.gb[0][:], self.i_o[:, 16384 + o * T:
                                                  16384 + (o + 1) * T])
            p.barrier()
            self.tt("vector", self.gb[3][:], self.gb[3][:], self.gb[0][:],
                    ALU.mult)
            p.barrier()
            self.scan("vector", self.gb[0][:], self.gb[4][:], self.gb[3][:],
                      0.0, ALU.mult, ALU.add)
            p.barrier()
            if DBG and o == 0:
                self.dbg_dump(self.gb[4][:, 0:256], 256, cast=True, name="e1b0")
                self.dbg_dump(self.gb[0][:, 0:256], 256, cast=True,
                              name="aaloc0")
            self.scan("vector", self.gb[5][:], self.gb[4][:], self.gb[4][:],
                      1.0, ALU.mult, ALU.bypass)
            p.barrier()
            self.cp("vector", g2[:, 552 + o:553 + o], self.gb[0][:, T - 1:T])
            self.cp("gpsimd", g2[:, 560 + o:561 + o], self.gb[5][:, T - 1:T])
            p.dma("sync", self.i_xc[:, 16384 + o * T:16384 + (o + 1) * T],
                  self.gb[0][:])
            p.dma("gpsimd", self.i_z[:, 16384 + o * T:16384 + (o + 1) * T],
                  self.gb[5][:])
            p.barrier()
        p.dma("sync", self.ag3_in, g2[:, 552:560])
        p.barrier()
        self.p.op("gpsimd", lambda q: q.collective_compute(
            "AllGather", ALU.bypass, replica_groups=[[0, 1, 2, 3, 4, 5, 6, 7]],
            ins=[self.ag3_in], outs=[self.ag3_out]))
        p.barrier()
        for r in range(8):
            p.dma("sync", g0[:, 8 + r * 8:16 + r * 8],
                  self.ag3_out[r * 128:(r + 1) * 128, :])
        p.barrier()
        p.dma("sync", self.ag3_in, g2[:, 560:568])
        p.barrier()
        self.p.op("gpsimd", lambda q: q.collective_compute(
            "AllGather", ALU.bypass, replica_groups=[[0, 1, 2, 3, 4, 5, 6, 7]],
            ins=[self.ag3_in], outs=[self.ag3_out]))
        p.barrier()
        for r in range(8):
            p.dma("sync", g0[:, 88 + r * 8:96 + r * 8],
                  self.ag3_out[r * 128:(r + 1) * 128, :])
        p.barrier()
        self.memset("vector", g2[:, 568:576], 0.0)
        p.barrier()
        for r in range(8):
            self.tss("vector", g0[:, 160:168], g0[:, 88 + r * 8:96 + r * 8],
                     self.colf[:, 416 + r * 3:417 + r * 3], ALU.mult)
            p.barrier()
            self.tss("vector", g0[:, 160:168], g0[:, 160:168],
                     self.colf[:, 417 + r * 3:418 + r * 3], ALU.add)
            p.barrier()
            self.tt("vector", g2[:, 568:576], g2[:, 568:576], g0[:, 160:168],
                    ALU.mult)
            p.barrier()
            self.stt("vector", g2[:, 568:576], g0[:, 8 + r * 8:16 + r * 8],
                     self.colf[:, 418 + r * 3:419 + r * 3], g2[:, 568:576],
                     ALU.mult, ALU.add)
            p.barrier()
        if DBG:
            self.dbg_dump(g2[:, 568:576], 8, name="aain")
        for o in range(NK):
            p.dma("sync", self.gb[0][:], self.i_o[:, o * T:(o + 1) * T])
            p.dma("gpsimd", self.gb[1][:], self.i_xc[:, o * T:(o + 1) * T])
            p.barrier()
            self.act(self.gb[0][:], self.gb[0][:], AF.Exp)
            p.barrier()
            p.dma("sync", self.gb[5][:], self.i_z[:, o * T:(o + 1) * T])
            p.barrier()
            self.stt("vector", self.gb[1][:], self.gb[5][:],
                     g2[:, 544 + o:545 + o], self.gb[1][:], ALU.mult, ALU.add)
            p.barrier()
            self.cp("vector", self.gb[2][:, 1:T], self.gb[1][:, 0:T - 1])
            self.cp("gpsimd", self.gb[2][:, 0:1], g2[:, 544 + o:545 + o])
            p.barrier()
            p.dma("sync", self.gb[1][:], self.i_xc[:, 16384 + o * T:
                                                   16384 + (o + 1) * T])
            p.dma("gpsimd", self.gb[5][:], self.i_z[:, 16384 + o * T:
                                                     16384 + (o + 1) * T])
            p.barrier()
            self.stt("vector", self.gb[1][:], self.gb[5][:],
                     g2[:, 568 + o:569 + o], self.gb[1][:], ALU.mult, ALU.add)
            p.barrier()
            self.cp("vector", self.gb[4][:, 1:T], self.gb[1][:, 0:T - 1])
            self.cp("gpsimd", self.gb[4][:, 0:1], g2[:, 568 + o:569 + o])
            p.barrier()
            p.dma("sync", self.gb[1][:], self.i_o[:, o * T:(o + 1) * T])
            p.barrier()
            self.p.op("scalar", lambda q, o=o: q.activation(
                self.gb[1][:], self.gb[1][:], AF.Exp,
                bias=self.bcolf[:, 40 + o:41 + o]))
            p.barrier()
            self.tt("vector", self.gb[3][:], self.gb[2][:], self.gb[1][:],
                    ALU.max)
            p.barrier()
            self.recip(g1[:, 0:T], self.gb[3][:])
            p.barrier()
            self.tt("vector", self.gb[2][:], self.gb[2][:], g1[:, 0:T],
                    ALU.mult)
            p.barrier()
            self.tt("vector", self.gb[1][:], self.gb[1][:], g1[:, 0:T],
                    ALU.mult)
            p.barrier()
            self.tt("vector", self.gb[4][:], self.gb[4][:], self.gb[2][:],
                    ALU.mult)
            p.barrier()
            p.dma("sync", self.gb[5][:], self.i_o[:, 16384 + o * T:
                                                  16384 + (o + 1) * T])
            p.barrier()
            self.tt("vector", self.gb[5][:], self.gb[5][:], self.gb[1][:],
                    ALU.mult)
            p.barrier()
            self.tt("vector", self.gb[4][:], self.gb[4][:], self.gb[5][:],
                    ALU.add)
            p.barrier()
            self.tt("vector", self.gb[1][:], self.gb[1][:], self.gb[2][:],
                    ALU.add)
            p.barrier()
            self.tss("vector", self.gb[1][:], self.gb[1][:], EPSW, ALU.add)
            p.barrier()
            self.recip(g1[:, 0:T], self.gb[1][:])
            p.barrier()
            self.tt("vector", self.gb[4][:], self.gb[4][:], g1[:, 0:T],
                    ALU.mult)
            p.barrier()
            if DBG and o == 0:
                self.dbg_dump(self.gb[2][:, 0:256], 256, cast=True, name="zsh0")
                self.dbg_dump(self.gb[4][:, 0:256], 256, cast=True, name="wkv0")
            p.dma("sync", self.gb[1][:], self.i_v[:, o * T:(o + 1) * T])
            p.barrier()
            self.tt("vector", self.gb[4][:], self.gb[4][:], self.gb[1][:],
                    ALU.mult)
            p.barrier()
            p.dma("sync", self.gb[1][:], self.i_k[:, o * T:(o + 1) * T])
            p.barrier()
            self.tt("vector", self.xb[o][:], self.gb[4][:], self.gb[1][:],
                    ALU.add)
            p.barrier()
            if DBG and o == 0:
                self.dbg_dump(self.xb[o][:, 0:256], 256, cast=True, name="x1_0")
        for k in range(NK):
            p.dma("sync", self.i_o[:, k * T:(k + 1) * T], self.xb[k][:])
        p.barrier()
        self.emit_ln2_affine(self.xb, self.bcolf[:, 48:56], self.bcolf[:, 56:64])
        if DBG:
            self.dbg_dump(self.xb[0][:, 0:256], 256, cast=True, name="h2_0")
        self.gather_lastcols(self.xb)
        for k in range(NK):
            self.shift_mix(self.hslot(k), self.xb, k,
                           self.bcolf[:, 72 + k:73 + k], g0[:, 72 + k:73 + k])
        self.load_weight("l3_wrc", D, D, self.wbf, 0)
        for o in range(NK):
            for s in range(NSUB):
                sl = slice(s * TS, (s + 1) * TS)
                for k in range(NK):
                    self.mm(self.ps[s % 4][:, :],
                            self.wbf[:, k * D + o * 128:k * D + (o + 1) * 128],
                            self.hslot(k)[:, sl], k == 0, k == NK - 1)
            p.barrier()
            for s in range(NSUB):
                self.act(slab[:, s * TS:(s + 1) * TS], self.ps[s % 4][:, :],
                         AF.Sigmoid)
            p.barrier()
            if DBG and o == 0:
                self.dbg_dump(slab[:, 0:256], 256, cast=True, name="rc0")
            p.dma("sync", self.i_v[:, o * T:(o + 1) * T], slab[:])
            p.barrier()
        for k in range(NK):
            self.shift_mix(self.hslot(k), self.xb, k,
                           self.bcolf[:, 64 + k:65 + k], g0[:, 72 + k:73 + k])
        if DBG:
            self.dbg_dump(self.hslot(0)[:, 0:256], 256, cast=True, name="xk2_0")
        for quarter in range(4):
            self.load_weight(self.inp["l3_wkc"][:, quarter * 1024:(quarter + 1) * 1024],
                             D, 1024, self.wbf, 0)
            self.load_weight(self.inp["l3_wvc"][quarter * 1024:(quarter + 1) * 1024, :],
                             1024, 1024, self.wbf, 8192)
            for ho in range(8):
                for s in range(NSUB):
                    sl = slice(s * TS, (s + 1) * TS)
                    for k in range(NK):
                        self.mm(self.ps[s % 4][:, :],
                                self.wbf[:, k * 1024 + ho * 128:
                                          k * 1024 + (ho + 1) * 128],
                                self.hslot(k)[:, sl], k == 0, k == NK - 1)
                p.barrier()
                for s in range(NSUB):
                    sl = slice(s * TS, (s + 1) * TS)
                    self.tss("vector", g1[:, 0:512], self.ps[s % 4][:, :],
                             0.0, ALU.max)
                    p.barrier()
                    self.tt("vector", self.xb[ho][:, sl], g1[:, 0:512],
                            g1[:, 0:512], ALU.mult)
                    p.barrier()
                if DBG and quarter == 0 and ho == 0:
                    self.dbg_dump(self.xb[0][:, 0:256], 256, cast=True,
                                  name="kc2_0")
            for o in range(NK):
                p.dma("sync", slab[:], self.i_v[:, o * T:(o + 1) * T])
                if quarter == 0:
                    p.dma("gpsimd", g2[:], self.i_o[:, o * T:(o + 1) * T])
                p.barrier()
                for s in range(NSUB):
                    sl = slice(s * TS, (s + 1) * TS)
                    for hc in range(8):
                        self.mm(self.ps[s % 4][:, :],
                                self.wbf[:, 8192 + hc * 1024 + o * 128:
                                          8192 + hc * 1024 + (o + 1) * 128],
                                self.xb[hc][:, sl], hc == 0, hc == 7)
                p.barrier()
                for s in range(NSUB):
                    sl = slice(s * TS, (s + 1) * TS)
                    self.tt("vector", g1[:, 0:512], self.ps[s % 4][:, :],
                            slab[:, sl], ALU.mult)
                    p.barrier()
                    if quarter == 0:
                        self.tt("vector", g1[:, 0:512], g1[:, 0:512],
                                g2[:, sl], ALU.add)
                        p.barrier()
                    self.tt("vector", self.xs[o][:, sl], self.xs[o][:, sl],
                            g1[:, 0:512], ALU.add)
                    p.barrier()
                if DBG and o == 0:
                    self.dbg_dump(self.xs[0][:, 0:256], 256,
                                  name="xsq%d" % quarter)


    def emit_conv4_l1(self, slot, prev3, f, tapoff, bias_col):
        p = self.p
        slab = self.resw[:, 512:2560]
        wc = lambda j: self.bcolf[:, tapoff + f * 4 + j:tapoff + f * 4 + j + 1]
        self.tss("vector", slab[:], slot[:], wc(3), ALU.mult)
        p.barrier()
        for j, off in ((2, 1), (1, 2), (0, 3)):
            self.stt("vector", slab[:, off:T], slot[:, 0:T - off], wc(j),
                     slab[:, off:T], ALU.mult, ALU.add)
            p.barrier()
        for t in range(3):
            for j in range(3 - t):
                self.stt("vector", slab[:, t:t + 1], prev3[:, t + j:t + j + 1],
                         wc(j), slab[:, t:t + 1], ALU.mult, ALU.add)
            p.barrier()
        self.tss("vector", slot[:], slab[:], bias_col, ALU.add)
        p.barrier()

    def emit_l1_attn(self):
        p = self.p
        g0, g1, g2 = self.gf[0], self.gf[1], self.gf[2]
        slab = self.resw[:, 512:2560]
        NC, C = 16, 128
        trilx = self.l1c[:, 0:128]
        triux = self.l1c[:, 128:256]
        bd16 = self.l1c[:, 386:514]
        id64 = self.cons[0:64, 128:192]
        id16 = self.cons[0:16, 128:144]
        p.dma("sync", self.l1c[:], self.inp["l1a_cons"])
        self.load_col(self.inp["l1a_cqw"], self.bcolf[:, 0:32])
        self.load_col(self.inp["l1a_ckw"], self.bcolf[:, 32:64])
        self.load_col(self.inp["l1a_cqb"], self.bcolf[:, 64:72])
        self.load_col(self.inp["l1a_ckb"], self.bcolf[:, 72:80])
        self.load_col(self.inp["l1a_lnw"], self.bcolf[:, 80:88])
        self.load_col(self.inp["l1a_lnb"], self.bcolf[:, 88:96])
        self.load_col(self.inp["l1a_bbcol"], self.bcolf[0:16, 96:97])
        self.dma_bcast_small(None, self.colf[:, 448:456], 8, 1,
                             src_dram=self.inp["comb3_prev"])
        self.load_col(self.inp["comb_alpha"], self.colf[0:8, 24:25])
        self.load_col(self.inp["comb_gamma"], self.colf[0:8, 25:26])
        self.load_col(self.inp["comb_a"], self.colf[0:8, 26:27])
        self.load_col(self.inp["comb2_prec"], self.colf[0:8, 30:31])
        p.barrier()
        self.dma_bcast_small(self.colf[0:8, 24:27], self.colf[:, 416:440], 8, 3)
        self.dma_bcast_small(self.colf[0:8, 30:31], self.colf[:, 440:448], 8, 1)
        bd2f = self.l1cf[:, 0:2]
        bd2tf = self.l1cf[0:2, 2:130]
        p.dma("sync", self.l1cf[:], self.inp["l1a_consf"])
        p.barrier()
        self.memset("vector", self.colf[:, 16:17], 1.0)
        p.barrier()
        for s in range(NSUB):
            sl = slice(s * TS, (s + 1) * TS)
            for k in range(NK):
                self.tt("vector", g0[:, 0:TS], self.xs[k][:, sl],
                        self.xs[k][:, sl], ALU.mult)
                p.barrier()
                self.mm(self.ps[0][0:1, 0:TS], self.colf[:, 16:17],
                        self.xs[k][:, sl], k == 0, k == NK - 1)
                self.mm(self.ps[1][0:1, 0:TS], self.colf[:, 16:17],
                        g0[:, 0:TS], k == 0, k == NK - 1)
                p.barrier()
            self.cp("vector", g0[0:1, 0:TS], self.ps[0][0:1, 0:TS])
            self.cp("scalar", g0[0:1, TS:2 * TS], self.ps[1][0:1, 0:TS])
            p.barrier()
            self.mm(self.ps[2][:, 0:TS], self.onesf_row[0:1, :], g0[0:1, 0:TS],
                    True, True)
            self.mm(self.ps[3][:, 0:TS], self.onesf_row[0:1, :], g0[0:1, TS:2 * TS],
                    True, True)
            p.barrier()
            self.act(g1[:, sl], self.ps[2][:, 0:TS], AF.Copy, scale=1.0 / D)
            p.barrier()
            self.ts2("vector", g2[:, sl], self.ps[3][:, 0:TS], 1.0 / D,
                     LN_EPS, ALU.mult, ALU.add)
            p.barrier()
        self.stt("vector", g0[:], g1[:], -1.0, g1[:], ALU.mult, ALU.mult)
        p.barrier()
        self.tt("vector", g2[:], g2[:], g0[:], ALU.add)
        p.barrier()
        self.recip(g2[:], g2[:])
        p.barrier()
        self.act(g2[:], g2[:], AF.Sqrt)
        p.barrier()
        for k in range(NK):
            self.tt("vector", g0[:], self.xs[k][:], g1[:], ALU.subtract)
            p.barrier()
            self.tt("vector", g0[:], g0[:], g2[:], ALU.mult)
            p.barrier()
            self.cp("vector", self.xb[k][:], g0[:])
            p.barrier()
            p.dma("sync", self.i_xf32[:, k * T:(k + 1) * T], g0[:])
            p.barrier()
        for s8 in range(8):
            s256 = s8 * 256
            for k in range(NK):
                p.dma("sync", g0[:, 0:256],
                      self.i_xf32[:, k * T + s256:k * T + s256 + 256])
                p.dma("gpsimd", g1[:, 0:16],
                      self.inp["l1a_wb"][k * 128:(k + 1) * 128, :])
                p.barrier()
                self.mm(self.ps[0][0:16, 0:256], g1[:, 0:16], g0[:, 0:256],
                        k == 0, k == NK - 1)
                p.barrier()
            self.p.op("scalar", lambda q, s256=s256: q.activation(
                g2[0:16, s256:s256 + 256], self.ps[0][0:16, 0:256],
                AF.Sigmoid, bias=self.bcolf[0:16, 96:97]))
            p.barrier()
        p.dma("sync", self.i_bf32[:, :], g2[0:16, 0:T])
        self.cp("vector", self.wbf[0:16, 14336:16384], g2[0:16, 0:T])
        p.barrier()
        for wname, dstf, dstb, tapoff, cb, isk in (
                ("l1a_wq", self.i_qf32, self.i_z, 0, 64, False),
                ("l1a_wk", self.i_kf32, self.i_k, 32, 72, True)):
            for s8 in range(8):
                s256 = s8 * 256
                for half in range(2):
                    ob = half * 4
                    for k in range(NK):
                        p.dma("sync", g0[:, 0:256],
                              self.i_xf32[:, k * T + s256:k * T + s256 + 256])
                        p.dma("gpsimd", g1[:, 0:512],
                              self.inp[wname][k * 128:(k + 1) * 128,
                                              ob * 128:ob * 128 + 512])
                        p.barrier()
                        for oo in range(4):
                            self.mm(self.ps[oo][:, 0:256],
                                    g1[:, oo * 128:(oo + 1) * 128],
                                    g0[:, 0:256], k == 0, k == NK - 1)
                        p.barrier()
                    for oo in range(4):
                        self.cp("vector", g2[:, oo * 256:(oo + 1) * 256],
                                self.ps[oo][:, 0:256])
                    p.barrier()
                    for oo in range(4):
                        o = ob + oo
                        p.dma("sync",
                              dstf[:, o * T + s256:o * T + s256 + 256],
                              g2[:, oo * 256:(oo + 1) * 256])
                    p.barrier()
            for f in range(NK):
                p.dma("sync", g0[:, f * 3:(f + 1) * 3],
                      dstf[:, f * T + T - 3:f * T + T])
            p.barrier()
            self._mamba_gather_prev("l1f" + wname)
            wc = lambda j, f: self.bcolf[:, tapoff + f * 4 + j:
                                         tapoff + f * 4 + j + 1]
            for f in range(NK):
                p.dma("sync", g1[:, :], dstf[:, f * T:(f + 1) * T])
                p.barrier()
                self.tss("vector", g2[:], g1[:], wc(3, f), ALU.mult)
                p.barrier()
                for j, off in ((2, 1), (1, 2), (0, 3)):
                    self.stt("vector", g2[:, off:T], g1[:, 0:T - off],
                             wc(j, f), g2[:, off:T], ALU.mult, ALU.add)
                    p.barrier()
                for t in range(3):
                    for j in range(3 - t):
                        self.stt("vector", g2[:, t:t + 1],
                                 g0[:, f * 3 + t + j:f * 3 + t + j + 1],
                                 wc(j, f), g2[:, t:t + 1], ALU.mult, ALU.add)
                    p.barrier()
                self.tss("vector", g2[:], g2[:],
                         self.bcolf[:, cb + f:cb + f + 1], ALU.add)
                p.barrier()
                if isk:
                    self.tt("vector", g1[:], g2[:], g2[:], ALU.mult)
                    p.barrier()
                    for s in range(NSUB):
                        sl = slice(s * TS, (s + 1) * TS)
                        self.mm(self.ps[4][0:2, 0:TS], bd2f, g1[:, sl],
                                True, True)
                        p.barrier()
                        self.act(g0[0:2, 32:32 + TS], self.ps[4][0:2, 0:TS],
                                 AF.Sqrt)
                        p.barrier()
                        self.tss("vector", g0[0:2, 32:32 + TS],
                                 g0[0:2, 32:32 + TS], 1e-12, ALU.max)
                        p.barrier()
                        self.recip(g0[0:2, 32:32 + TS], g0[0:2, 32:32 + TS])
                        p.barrier()
                        self.mm(self.ps[5][:, 0:TS], bd2tf,
                                g0[0:2, 32:32 + TS], True, True)
                        p.barrier()
                        self.tt("vector", g2[:, sl], g2[:, sl],
                                self.ps[5][:, 0:TS], ALU.mult)
                        p.barrier()
                if DBG and f == 0:
                    self.dbg_dump(g2[:, 0:256], 256,
                                  name="knorm0" if isk else "qconv0")
                p.dma("sync", dstf[:, f * T:(f + 1) * T], g2[:])
                p.barrier()
                self.cp("vector", slab[:], g2[:])
                p.barrier()
                p.dma("sync", dstb[:, f * T:(f + 1) * T], slab[:])
                p.barrier()
        self.load_weight("l1a_wv", D, D, self.wbf, 0)
        for o in range(NK):
            for s in range(NSUB):
                sl = slice(s * TS, (s + 1) * TS)
                for k in range(NK):
                    self.mm(self.ps[s % 4][:, :],
                            self.wbf[:, k * D + o * 128:k * D + (o + 1) * 128],
                            self.xb[k][:, sl], k == 0, k == NK - 1)
            p.barrier()
            for s in range(NSUB):
                sl = slice(s * TS, (s + 1) * TS)
                self.cp("scalar", slab[:, sl], self.ps[s % 4][:, :])
            p.barrier()
            p.dma("sync", self.i_v[:, o * T:(o + 1) * T], slab[:])
            p.barrier()
        if L1CUT == 1:
            return
        W = self.wbf
        kch = lambda h: W[0:64, h * C:(h + 1) * C]
        qch = lambda h: W[0:64, 2048 + h * C:2048 + (h + 1) * C]
        vch = lambda h: W[0:64, 4096 + h * C:4096 + (h + 1) * C]
        S0bf = lambda h: W[0:64, 6144 + h * 64:6208 + h * 64]
        zbf = lambda h: W[0:64, 7168 + h:7169 + h]
        RT = lambda h: W[0:64, 7184 + h * 64:7248 + h * 64]
        bcb = lambda h: W[:, 8208 + h:8209 + h]
        Pt, PTt = W[:, 8224:8352], W[:, 8352:8480]
        Dsb, GTsb = W[:, 8480:8608], W[:, 8608:8736]
        Ktm, Vtm = W[:, 8736:8800], W[:, 8800:8864]
        gw = W[0:64, 8864:8992]
        Qt = W[0:64, 8992:9120]
        WtK = W[0:64, 9120:9184]
        dscr = W[:, 9184:9248]
        Xo = W[:, 13344:13472]
        Y2sb = W[:, 13472:13600]
        Y4sb = W[:, 13600:13728]
        ych = lambda h: W[0:64, 9248 + h * C:9248 + (h + 1) * C]
        qtch = lambda h: W[0:64, 11296 + h * C:11296 + (h + 1) * C]
        S0f = lambda h: g1[0:64, h * 64:(h + 1) * 64]
        zf = lambda h: g1[0:64, 1024 + h:1025 + h]
        bcf = lambda h: g1[:, 1040 + h:1041 + h]
        bcn = lambda h: g1[:, 1056 + h:1057 + h]
        T1 = self.ps[0][:, 0:128]
        M1 = self.ps[1][:, 0:128]
        M2 = self.ps[2][:, 0:128]
        M3 = self.ps[3]
        M4 = self.ps[4][:, 0:128]
        DEN = self.ps[5][0:1, 0:128]
        if DBG:
            self.dbg_dump(W[0:16, 14336:14592], 256, cast=True, name="beta0")
        self.memset("vector", g1[0:64, 0:1040], 0.0)
        p.barrier()
        self.cp("vector", W[0:64, 6144:7184], g1[0:64, 0:1040])
        for h in range(NC):
            self.cp("gpsimd", RT(h), id64)
        p.barrier()
        for c in range(NC):
            for h in range(NC):
                b, f = 64 * (h % 2), h // 2
                p.dma("sync", kch(h),
                      self.i_k[b:b + 64, f * T + c * C:f * T + (c + 1) * C])
                p.dma("gpsimd", qch(h),
                      self.i_z[b:b + 64, f * T + c * C:f * T + (c + 1) * C])
                p.dma("sync", vch(h),
                      self.i_v[b:b + 64, f * T + c * C:f * T + (c + 1) * C])
            p.barrier()
            self.transpose(self.psb[:, 256:272],
                           W[0:16, 14336 + c * C:14336 + (c + 1) * C], id16)
            p.barrier()
            self.cp("vector", g1[:, 1040:1056], self.psb[:, 256:272])
            p.barrier()
            self.ts2("vector", g1[:, 1056:1072], g1[:, 1040:1056], -1.0, 0.0,
                     ALU.mult, ALU.add)
            p.barrier()
            for h in range(NC):
                self.transpose(self.psb[:, 0:64], kch(h), id64)
                self.transpose(self.psb[:, 64:128], vch(h), id64)
                p.barrier()
                self.cp("vector", Ktm[:], self.psb[:, 0:64])
                self.cp("scalar", Vtm[:], self.psb[:, 64:128])
                p.barrier()
                self.mm(M3[:, 0:64], kch(h), S0bf(h), True, True)
                self.mm(M1, kch(h), kch(h), True, True)
                p.barrier()
                self.tt("vector", dscr[:], Vtm[:], M3[:, 0:64], ALU.subtract)
                self.stt("vector", Pt[:], M1, bcn(h), trilx, ALU.mult,
                         ALU.mult)
                p.barrier()
                self.tss("vector", Dsb[:, 0:64], dscr[:], bcf(h), ALU.mult)
                self.tss("gpsimd", Dsb[:, 64:128], Ktm[:], bcf(h), ALU.mult)
                if DBG and c == 0 and h == 0:
                    self.dbg_dump(Pt[:], 128, cast=True, name="X0")
                    self.dbg_dump(Dsb[:], 128, cast=True, name="D0")
                self.tt("vector", Xo[:], Pt[:], bd16, ALU.mult)
                p.barrier()
                self.tt("vector", Xo[:], Pt[:], Xo[:], ALU.subtract)
                self.tt("gpsimd", Pt[:], Pt[:], bd16, ALU.mult)
                p.barrier()
                self.transpose(self.psb[:, 128:256], Pt, self.ident_bf)
                p.barrier()
                self.cp("vector", PTt[:], self.psb[:, 128:256])
                p.barrier()
                for l in range(4):
                    self.mm(M1, PTt, Dsb, True, True)
                    self.mm(M2, PTt, Xo, True, True)
                    p.barrier()
                    self.tt("vector", Dsb[:], Dsb[:], M1, ALU.add)
                    self.tt("vector", Xo[:], Xo[:], M2, ALU.add)
                    p.barrier()
                    if l < 3:
                        self.mm(M1, PTt, Pt, True, True)
                        p.barrier()
                        self.cp("vector", Pt[:], M1)
                        p.barrier()
                        self.transpose(self.psb[:, 128:256], Pt,
                                       self.ident_bf)
                        p.barrier()
                        self.cp("vector", PTt[:], self.psb[:, 128:256])
                        p.barrier()
                self.transpose(self.psb[:, 128:256], Xo, self.ident_bf)
                p.barrier()
                self.cp("vector", PTt[:], self.psb[:, 128:256])
                p.barrier()
                self.mm(M1, PTt, Dsb, True, True)
                self.mm(M2, PTt, Xo, True, True)
                p.barrier()
                self.tt("vector", Dsb[:], Dsb[:], M1, ALU.add)
                self.cp("scalar", Y2sb[:], M2)
                p.barrier()
                self.transpose(self.psb[:, 128:256], Y2sb, self.ident_bf)
                p.barrier()
                self.cp("vector", PTt[:], self.psb[:, 128:256])
                p.barrier()
                self.mm(M1, PTt, Dsb, True, True)
                self.mm(M2, PTt, Y2sb, True, True)
                p.barrier()
                self.tt("vector", Dsb[:], Dsb[:], M1, ALU.add)
                self.cp("scalar", Y4sb[:], M2)
                p.barrier()
                self.transpose(self.psb[:, 128:256], Y4sb, self.ident_bf)
                p.barrier()
                self.cp("vector", PTt[:], self.psb[:, 128:256])
                p.barrier()
                self.mm(M1, PTt, Dsb, True, True)
                p.barrier()
                self.tt("vector", Dsb[:], Dsb[:], M1, ALU.add)
                p.barrier()
                if DBG and c == 0 and h == 0:
                    self.dbg_dump(Dsb[:], 128, cast=True, name="UW0")
                self.mm(M1, kch(h), qch(h), True, True)
                p.barrier()
                self.tt("vector", GTsb[:], M1, triux, ALU.mult)
                p.barrier()
                self.mm(M2, Dsb, GTsb, True, True)
                self.mm(M4[0:64, :], S0bf(h), qch(h), True, True)
                p.barrier()
                self.cp("vector", ych(h), M2[0:64, :])
                self.cp("scalar", gw[:], M2[64:128, :])
                p.barrier()
                self.tt("vector", ych(h), ych(h), M4[0:64, :], ALU.add)
                p.barrier()
                if DBG and c == 0 and h == 0:
                    self.dbg_dump(GTsb[:], 128, cast=True, name="GT0")
                    self.dbg_dump(ych(0), 128, cast=True, name="yloc0")
                self.tt("vector", Qt[:], qch(h), gw[:], ALU.subtract)
                p.barrier()
                if c == 0:
                    self.cp("vector", qtch(h), Qt[:])
                else:
                    self.mm(M4[0:64, :], RT(h), Qt[:], True, True)
                    p.barrier()
                    self.cp("vector", qtch(h), M4[0:64, :])
                p.barrier()
                self.mm(M3[0:64, 64:128], Ktm, Dsb[:, 0:64], True, True)
                p.barrier()
                self.tt("vector", S0f(h), S0f(h), M3[0:64, 64:128], ALU.add)
                p.barrier()
                self.cp("vector", S0bf(h), S0f(h))
                p.barrier()
                if DBG and c == 0 and h == 0:
                    self.dbg_dump(g2[0:1, 0:128], 128, name="den0")
                if DBG and c == 1 and h == 0:
                    self.dbg_dump(S0bf(0), 64, cast=True, name="S0c1")
                self.mm(M4[0:64, 0:64], Dsb[:, 64:128], Ktm, True, True)
                p.barrier()
                self.cp("vector", WtK[:], M4[0:64, 0:64])
                p.barrier()
                self.mm(M4[0:64, 64:128], WtK, RT(h), True, True)
                p.barrier()
                self.tt("vector", RT(h), RT(h), M4[0:64, 64:128],
                        ALU.subtract)
                p.barrier()
            for h in range(NC):
                p.dma("sync", self.i_o[0:64, h * T + c * C:h * T + (c + 1) * C],
                      ych(h))
                p.dma("gpsimd",
                      self.i_o[64:128, h * T + c * C:h * T + (c + 1) * C],
                      qtch(h))
        if L1CUT == 2:
            return
        for h in range(NC):
            self.transpose(self.psb[0:64, h * 64:(h + 1) * 64], RT(h), id64)
        p.barrier()
        for h in range(NC):
            self.cp("vector", g2[0:64, h * 64:(h + 1) * 64],
                    self.psb[0:64, h * 64:(h + 1) * 64])
        p.barrier()
        p.dma("sync", self.ag_in[0:64, 0:1024], g1[0:64, 0:1024])
        p.dma("sync", self.ag_in[64:128, 0:1024], g2[0:64, 0:1024])
        p.dma("sync", self.ag_in[0:64, 1024:1032], g1[0:64, 1024:1032])
        p.dma("sync", self.ag_in[64:128, 1024:1032], g1[0:64, 1032:1040])
        p.barrier()
        self.p.op("gpsimd", lambda q: q.collective_compute(
            "AllGather", ALU.bypass, replica_groups=[[0, 1, 2, 3, 4, 5, 6, 7]],
            ins=[self.ag_in], outs=[self.ag_out]))
        p.barrier()
        self.memset("vector", g1[0:64, 0:1024], 0.0)
        self.memset("gpsimd", g1[0:64, 1136:1152], 0.0)
        p.barrier()
        self.cp("vector", W[0:64, 6144:7168], g1[0:64, 0:1024])
        p.barrier()
        scrf = g1[0:64, 1072:1136]
        scrb = W[0:64, 0:64]
        for r in range(8):
            p.dma("sync", g2[0:64, 0:1024],
                  self.ag_out[r * 128:r * 128 + 64, 0:1024])
            p.dma("sync", g2[0:64, 1024:2048],
                  self.ag_out[r * 128 + 64:r * 128 + 128, 0:1024])
            p.dma("gpsimd", g1[0:64, 1152:1160],
                  self.ag_out[r * 128:r * 128 + 64, 1024:1032])
            p.dma("gpsimd", g1[0:64, 1160:1168],
                  self.ag_out[r * 128 + 64:r * 128 + 128, 1024:1032])
            p.barrier()
            for h in range(NC):
                self.tss("vector", scrf, g2[0:64, 1024 + h * 64:1088 + h * 64],
                         self.colf[0:64, 416 + r * 3:417 + r * 3], ALU.mult)
                p.barrier()
                self.stt("vector", scrf, id64,
                         self.colf[0:64, 417 + r * 3:418 + r * 3], scrf,
                         ALU.mult, ALU.add)
                p.barrier()
                self.cp("vector", scrb, scrf)
                p.barrier()
                self.mm(M4[0:64, 0:64], scrb, S0bf(h), True, True)
                p.barrier()
                self.stt("vector", g1[0:64, h * 64:(h + 1) * 64],
                         g2[0:64, h * 64:(h + 1) * 64],
                         self.colf[0:64, 418 + r * 3:419 + r * 3],
                         M4[0:64, 0:64], ALU.mult, ALU.add)
                p.barrier()
                self.cp("vector", S0bf(h), g1[0:64, h * 64:(h + 1) * 64])
                p.barrier()
            self.stt("vector", g1[0:64, 1136:1152], g1[0:64, 1152:1168],
                     self.colf[0:64, 440 + r:441 + r], g1[0:64, 1136:1152],
                     ALU.mult, ALU.add)
            p.barrier()
        self.cp("vector", W[0:64, 7168:7184], g1[0:64, 1136:1152])
        p.barrier()
        if L1CUT == 3:
            return
        qch2 = lambda h: W[0:64, h * C:(h + 1) * C]
        ylch = lambda h: W[0:64, 2048 + h * C:2048 + (h + 1) * C]
        qtld = lambda h: W[0:64, 4096 + h * C:4096 + (h + 1) * C]
        numb = W[0:64, 9184:9312]
        denb = W[0:1, 9312:9440]
        yout = W[0:64, 13728:13856]
        for c in range(NC):
            for h in range(NC):
                b, f = 64 * (h % 2), h // 2
                p.dma("sync", qch2(h),
                      self.i_z[b:b + 64, f * T + c * C:f * T + (c + 1) * C])
                p.dma("gpsimd", ylch(h),
                      self.i_o[0:64, h * T + c * C:h * T + (c + 1) * C])
                p.dma("sync", qtld(h),
                      self.i_o[64:128, h * T + c * C:h * T + (c + 1) * C])
            p.barrier()
            for h in range(NC):
                b, f = 64 * (h % 2), h // 2
                self.mm(M4[0:64, :], S0bf(h), qtld(h), True, True)
                p.barrier()
                self.tt("vector", yout[:], ylch(h), M4[0:64, :], ALU.add)
                p.barrier()
                self.cp("scalar", self.xb[f][b:b + 64, c * C:(c + 1) * C],
                        yout[:])
                p.barrier()
        for f in range(NK):
            p.dma("sync", g0[:], self.i_kf32[:, f * T:(f + 1) * T])
            p.dma("gpsimd", g2[0:2, 0:T], self.i_bf32[2 * f:2 * f + 2, :])
            p.barrier()
            for s in range(NSUB):
                sl = slice(s * TS, (s + 1) * TS)
                self.mm(self.ps[4][:, 0:TS], bd2tf, g2[0:2, sl], True, True)
                p.barrier()
                self.tt("vector", g1[:, sl], g0[:, sl], self.ps[4][:, 0:TS],
                        ALU.mult)
                p.barrier()
            self.reduce("vector", self.colf[:, f:f + 1], g1[:], ALU.add)
            p.barrier()
            p.dma("sync", self.i_bk[:, f * T:(f + 1) * T], g1[:])
            p.barrier()
        p.dma("sync", self.ag4_in[:, 0:8], self.colf[:, 0:8])
        p.barrier()
        self.p.op("gpsimd", lambda q: q.collective_compute(
            "AllGather", ALU.bypass, replica_groups=[[0, 1, 2, 3, 4, 5, 6, 7]],
            ins=[self.ag4_in], outs=[self.ag4_out]))
        p.barrier()
        for r in range(8):
            p.dma("sync", g0[:, 32 + r * 8:40 + r * 8],
                  self.ag4_out[r * 128:(r + 1) * 128, 0:8])
        p.barrier()
        self.memset("vector", self.colf[:, 8:16], 0.0)
        p.barrier()
        for r in range(8):
            self.stt("vector", self.colf[:, 8:16], g0[:, 32 + r * 8:40 + r * 8],
                     self.colf[:, 440 + r:441 + r], self.colf[:, 8:16],
                     ALU.mult, ALU.add)
            p.barrier()
        for f in range(NK):
            p.dma("sync", g1[:], self.i_bk[:, f * T:(f + 1) * T])
            p.dma("gpsimd", g2[:], self.i_qf32[:, f * T:(f + 1) * T])
            p.barrier()
            self.scan("vector", g0[:], g1[:], g1[:],
                      self.colf[:, 8 + f:9 + f], ALU.add, ALU.bypass)
            p.barrier()
            self.tt("vector", g0[:], g0[:], g1[:], ALU.subtract)
            p.barrier()
            self.tt("vector", g0[:], g0[:], g2[:], ALU.mult)
            p.barrier()
            for s in range(NSUB):
                sl = slice(s * TS, (s + 1) * TS)
                self.mm(self.ps[4][0:2, 0:TS], bd2f, g0[:, sl], True, True)
                p.barrier()
                self.cp("vector", g1[0:2, 0:TS], self.ps[4][0:2, 0:TS])
                p.barrier()
                self.tss("vector", g1[0:2, 0:TS], g1[0:2, 0:TS], 1e-06,
                         ALU.add)
                p.barrier()
                self.recip(g1[0:2, 0:TS], g1[0:2, 0:TS])
                p.barrier()
                self.mm(self.ps[5][:, 0:TS], bd2tf, g1[0:2, 0:TS], True, True)
                p.barrier()
                self.tt("vector", self.xb[f][:, sl], self.xb[f][:, sl],
                        self.ps[5][:, 0:TS], ALU.mult)
                p.barrier()
        if DBG:
            self.dbg_dump(self.xb[0][:, 0:256], 256, cast=True, name="yfin0")
            self.dbg_dump(self.xb[0][:, 1024:1280], 256, cast=True,
                          name="yfin8")
        if L1CUT == 4:
            return
        self.load_weight("l1a_wo", D, D, self.wbf, 0)
        if L1CUT == 51:
            return
        for o in range(NK):
            for s in range(NSUB):
                sl = slice(s * TS, (s + 1) * TS)
                for k in range(NK):
                    self.mm(self.ps[s % 4][:, :],
                            self.wbf[:, k * D + o * 128:k * D + (o + 1) * 128],
                            self.xb[k][:, sl], k == 0, k == NK - 1)
            p.barrier()
            for s in range(NSUB):
                sl = slice(s * TS, (s + 1) * TS)
                self.cp("vector", slab[:, sl], self.ps[s % 4][:, :])
            p.barrier()
            p.dma("sync", self.i_o[:, o * T:(o + 1) * T], slab[:])
            p.barrier()
        for k in range(NK):
            p.dma("sync", self.xb[k][:], self.i_o[:, k * T:(k + 1) * T])
        p.barrier()
        if L1CUT == 5:
            return
        self.emit_ln2_affine(self.xb, self.bcolf[:, 80:88],
                             self.bcolf[:, 88:96])
        if L1CUT == 6:
            return
        for k in range(NK):
            self.tt(("vector", "gpsimd")[k % 2], self.xs[k][:], self.xs[k][:],
                    self.xb[k][:], ALU.add)
        p.barrier()

    def store_out(self):
        p = self.p
        for k in range(NK):
            p.dma("sync", self.out[k * 128:(k + 1) * 128, :], self.xs[k][:])
        p.barrier()

    def _emit(self):
        self.load_consts()
        ffn0_done = [False]

        def ffn0():
            self.emit_ffn0(first=not ffn0_done[0])
            ffn0_done[0] = True
        emitters = {0: ffn0, 2: ffn0, 1: self.emit_l0_attn,
                    3: self.emit_moe, 5: self.emit_moe,
                    6: self.emit_ffn2, 8: self.emit_ffn2,
                    9: self.emit_ffn3, 11: self.emit_ffn3,
                    7: self.emit_mamba, 10: self.emit_l3_attn,
                    4: self.emit_l1_attn}
        for i in self.blocks:
            emitters[i]()
        self.dbg_xs()
        self.store_out()


_BUILDER_CACHE = {}


def _get_builder(blocks):
    key = tuple(blocks)
    if key not in _BUILDER_CACHE:
        _BUILDER_CACHE[key] = Builder(blocks)
    return _BUILDER_CACHE[key]


def _consts():
    import ml_dtypes
    bf = ml_dtypes.bfloat16
    c = {}
    c["ones_bf"] = np.ones((128, 128), dtype=bf)
    c["ones_row_bf"] = np.ones((1, 512), dtype=bf)
    c["ident_bf"] = np.eye(128, dtype=np.float32).astype(bf)
    c["ident_f32"] = np.eye(128, dtype=np.float32)
    s, t = np.meshgrid(np.arange(128), np.arange(128), indexing="ij")
    c["triu_incl_bf"] = (s <= t).astype(np.float32).astype(bf)
    sel = np.zeros((64, 2048), np.float32)
    for v in range(16):
        row = v if v < 8 else 32 + (v - 8)
        sel[row, v * 128:(v + 1) * 128] = 1.0
    c["sel64_bf"] = sel.astype(bf)
    lc = np.zeros((128, 514), dtype=np.float32)
    for _i in range(8):
        lc[_i * 16:(_i + 1) * 16, 386 + _i * 16:386 + (_i + 1) * 16] = 1.0
    lc[:, 0:128] = np.tril(np.ones((128, 128)), -1)
    lc[:, 128:256] = np.triu(np.ones((128, 128)), 1)
    lc[0:64, 256] = 1.0
    lc[64:128, 257] = 1.0
    lc[0, 258:322] = 1.0
    lc[1, 322:386] = 1.0
    c["l1a_cons"] = lc.astype(bf)
    lf = np.zeros((128, 130), dtype=np.float32)
    lf[0:64, 0] = 1.0
    lf[64:128, 1] = 1.0
    lf[0, 2:66] = 1.0
    lf[1, 66:130] = 1.0
    c["l1a_consf"] = lf
    return c


def _rope_tables():
    half = 64
    inv_freq = 1.0 / (10000.0 ** (np.arange(half, dtype=np.float64) / half))
    tpos = np.arange(8192, dtype=np.float64)
    fr = tpos[:, None] * inv_freq[None, :]
    return (np.cos(fr).astype(np.float32).T.copy(),
            np.sin(fr).astype(np.float32).T.copy())


def _rope_perm():
    perm = []
    for h in range(8):
        base = h * 128
        perm += [base + 2 * i for i in range(64)]
        perm += [base + 2 * i + 1 for i in range(64)]
    return np.array(perm)


def prepare_in_maps(inputs):
    x = np.ascontiguousarray(inputs["L0_attn_x"])
    consts = _consts()
    cosT, sinT = _rope_tables()
    perm = _rope_perm()
    f32 = lambda a: np.ascontiguousarray(a, dtype=np.float32)
    shared = dict(consts)
    shared["l0_kernel"] = f32(inputs["L0_ffn_kernel"])
    shared["l0_bias_row"] = f32(inputs["L0_ffn_bias"])[None, :]
    shared["l0_a"] = f32(inputs["L0_ffn_a"])
    shared["l0_b"] = f32(inputs["L0_ffn_b"])
    shared["l0_wq"] = f32(inputs["L0_attn_Wq"].T[:, perm])
    shared["l0_wk"] = f32(inputs["L0_attn_Wk"].T[:, perm])
    shared["l0_wv"] = f32(inputs["L0_attn_Wv"].T)
    shared["l0_wo"] = f32(inputs["L0_attn_Wo"].T)
    shared["l0_bo_row"] = f32(inputs["L0_attn_bo"])[None, :]
    shared["l1_gate"] = f32(inputs["L1_ffn_w_gate_w"])
    shared["l1_gateb_row"] = f32(inputs["L1_ffn_w_gate_b"])[None, :]
    shared["l1_W1"] = f32(inputs["L1_ffn_W1"].transpose(1, 0, 2).reshape(1024, 2048))
    shared["l1_b1cols"] = f32(np.ascontiguousarray(
        inputs["L1_ffn_b1"].reshape(2048)[None, :].reshape(16, 128).T))
    shared["l1_W2"] = f32(inputs["L1_ffn_W2"].reshape(2048, 1024))
    shared["l1_b2"] = f32(inputs["L1_ffn_b2"])
    shared["l2_W1"] = f32(inputs["L2_ffn_W1"])
    shared["l2_b1row"] = f32(inputs["L2_ffn_b1"])[None, :]
    shared["l2_theta"] = f32(np.ascontiguousarray(inputs["L2_ffn_quantum_phases"].reshape(32, 128).T))
    shared["l2_wr"] = f32(np.ascontiguousarray(inputs["L2_ffn_quantum_weights_real"].reshape(32, 128).T))
    shared["l2_wi"] = f32(np.ascontiguousarray(inputs["L2_ffn_quantum_weights_imag"].reshape(32, 128).T))
    shared["l2_W2"] = f32(inputs["L2_ffn_W2"])
    shared["l2_b2row"] = f32(inputs["L2_ffn_b2"])[None, :]
    shared["l3f_wr"] = f32(inputs["L3_ffn_W_r"].T)
    shared["l3f_wk"] = f32(inputs["L3_ffn_W_k"].T)
    shared["l3f_wv"] = f32(inputs["L3_ffn_W_v"].T)
    shared["l3f_mur"] = f32(np.ascontiguousarray(inputs["L3_ffn_mu_r"].reshape(8, 128).T))
    shared["l3f_muk"] = f32(np.ascontiguousarray(inputs["L3_ffn_mu_k"].reshape(8, 128).T))
    shared["l2a_lng"] = f32(np.ascontiguousarray(inputs["L2_attn_ln_g"].reshape(8, 128).T))
    shared["l2a_lnb"] = f32(np.ascontiguousarray(inputs["L2_attn_ln_b"].reshape(8, 128).T))
    shared["l2a_Win"] = f32(inputs["L2_attn_W_in"])
    shared["l2a_convw"] = f32(np.ascontiguousarray(
        inputs["L2_attn_conv_w"].reshape(16, 128, 4).transpose(1, 0, 2).reshape(128, 64)))
    shared["l2a_convb"] = f32(np.ascontiguousarray(
        inputs["L2_attn_conv_b"].reshape(16, 128).T))
    shared["l2a_Wx"] = f32(inputs["L2_attn_W_x"])
    shared["l2a_wdt"] = f32(np.ascontiguousarray(inputs["L2_attn_w_dt"].reshape(16, 128).T))
    shared["l2a_bdt"] = f32(np.ascontiguousarray(inputs["L2_attn_b_dt"].reshape(16, 128).T))
    shared["l2a_alog"] = f32(np.ascontiguousarray(
        inputs["L2_attn_A_log"].reshape(16, 128, 16).transpose(1, 0, 2).reshape(128, 256)))
    shared["l2a_dpar"] = f32(np.ascontiguousarray(inputs["L2_attn_D_param"].reshape(16, 128).T))
    shared["l2a_Wout"] = f32(inputs["L2_attn_W_out"])
    shared["ones_f32"] = np.ones((1, 128), np.float32)
    shared["l1a_wq"] = f32(inputs["L1_attn_Wq"].T)
    shared["l1a_wk"] = f32(inputs["L1_attn_Wk"].T)
    shared["l1a_wv"] = f32(inputs["L1_attn_Wv"].T)
    shared["l1a_wo"] = f32(inputs["L1_attn_Wo"].T)
    shared["l1a_wb"] = f32(inputs["L1_attn_Wb"].T)
    shared["l1a_bbcol"] = f32(np.asarray(inputs["L1_attn_bb"]).reshape(16, 1))
    tapf = lambda w: f32(np.ascontiguousarray(
        np.asarray(w).reshape(8, 128, 4).transpose(1, 0, 2).reshape(128, 32)))
    shared["l1a_cqw"] = tapf(inputs["L1_attn_conv_qw"])
    shared["l1a_ckw"] = tapf(inputs["L1_attn_conv_kw"])
    ccl = lambda v: f32(np.ascontiguousarray(np.asarray(v).reshape(8, 128).T))
    shared["l1a_cqb"] = ccl(inputs["L1_attn_conv_qb"])
    shared["l1a_ckb"] = ccl(inputs["L1_attn_conv_kb"])
    shared["l1a_lnw"] = ccl(inputs["L1_attn_ln_w"])
    shared["l1a_lnb"] = ccl(inputs["L1_attn_ln_b"])
    shared["l3_wkt"] = f32(inputs["L3_attn_Wk_t"].T)
    shared["l3_wvt"] = f32(inputs["L3_attn_Wv_t"].T)
    shared["l3_wrt"] = f32(inputs["L3_attn_Wr_t"].T)
    shared["l3_wkc"] = f32(inputs["L3_attn_Wk_c"].T)
    shared["l3_wvc"] = f32(inputs["L3_attn_Wv_c"].T)
    shared["l3_wrc"] = f32(inputs["L3_attn_Wr_c"].T)
    cc = lambda v: f32(np.ascontiguousarray(np.asarray(v).reshape(8, 128).T))
    shared["l3_ln1g"] = cc(inputs["L3_attn_ln1_g"])
    shared["l3_ln1b"] = cc(inputs["L3_attn_ln1_b"])
    shared["l3_ln2g"] = cc(inputs["L3_attn_ln2_g"])
    shared["l3_ln2b"] = cc(inputs["L3_attn_ln2_b"])
    shared["l3_mixk"] = cc(inputs["L3_attn_mix_k"])
    shared["l3_mixv"] = cc(inputs["L3_attn_mix_v"])
    shared["l3_mixr"] = cc(inputs["L3_attn_mix_r"])
    shared["l3_cmixk"] = cc(inputs["L3_attn_cmix_k"])
    shared["l3_cmixr"] = cc(inputs["L3_attn_cmix_r"])
    shared["l3_tf"] = cc(inputs["L3_attn_time_first"])
    shared["l0_wba"] = f32(np.concatenate(
        [inputs["L0_attn_Wbeta"].T, inputs["L0_attn_Walpha"].T], axis=1))
    shared["l0_bba"] = f32(np.concatenate(
        [inputs["L0_attn_bbeta"], inputs["L0_attn_balpha"]]))[:, None]

    maps = []
    for c in range(8):
        b, q = c // 4, c % 4
        m = dict(shared)
        bstart = 4 * b
        alpha = np.zeros((8, 1), np.float32)
        gamma = np.zeros((8, 1), np.float32)
        acol = np.zeros((8, 1), np.float32)
        for r in range(8):
            pred = (r >= bstart) and (r < c)
            acol[r] = 1.0 if pred else 0.0
            alpha[r] = 1.0 if (pred and r != bstart) else 0.0
            gamma[r] = 0.0 if pred else 1.0
        m["comb_alpha"] = alpha; m["comb_gamma"] = gamma; m["comb_a"] = acol
        m_all = np.zeros((8, 1), np.float32)
        m_prec = np.zeros((8, 1), np.float32)
        for r in range(8):
            if r // 4 == b:
                m_all[r] = 1.0
                if r < c:
                    m_prec[r] = 1.0
        m["comb2_all"] = m_all; m["comb2_prec"] = m_prec
        prev1 = np.zeros((8, 1), np.float32)
        if q > 0:
            prev1[c - 1] = 1.0
        m["comb3_prev"] = prev1
        m["comb2_thrbase"] = np.full((1, 1), 4096.0 - 2048.0 * float(m_prec.sum()), np.float32)
        xs = x[b, q * T:(q + 1) * T, :]
        m["x_fm"] = np.ascontiguousarray(xs.T)
        m["rope_cos"] = np.ascontiguousarray(cosT[:, q * T:(q + 1) * T])
        m["rope_sin"] = np.ascontiguousarray(sinT[:, q * T:(q + 1) * T])
        maps.append(m)
    return maps


def kernel(blocks=None, **inputs):
    blocks = list(range(N_SUB_BLOCKS)) if blocks is None else list(blocks)
    b = _get_builder(blocks)
    maps = prepare_in_maps(inputs)
    maps = [{k: m[k] for k in b.input_names} for m in maps]
    res = run_bass_kernel_spmd(b.nc, maps, core_ids=list(range(8)))
    out = np.empty((2, 8192, 1024), dtype=np.float32)
    for c in range(8):
        bb, q = c // 4, c % 4
        out[bb, q * T:(q + 1) * T, :] = res.results[c]["out_fm"].T
    return out
